# version 05
# baseline (speedup 1.0000x reference)
import contextlib
import math
import numpy as np
import ml_dtypes
import concourse.bass as bass
import concourse.mybir as mybir
from concourse.bass_utils import run_bass_kernel_spmd

F32 = mybir.dt.float32
BF16 = mybir.dt.bfloat16
AF = mybir.ActivationFunctionType
ALU = mybir.AluOpType
BF = ml_dtypes.bfloat16

D = 1024
S = 8192
SH = 4096
NDT = 8
NP = 8
EPS = 1e-5

ENGINES = ("sync", "gpsimd", "tensor", "scalar", "vector")
SELF_SEM = {"vector": "dve", "scalar": "act"}
import os as _os
TRUNC = int(_os.environ.get("KTRUNC", "6"))
TAILN = int(_os.environ.get("KTAILN", "8"))
KC2 = _os.environ.get("KC2", "1") == "1"
KGM = _os.environ.get("KGM", "1") == "1"
KW2 = _os.environ.get("KW2", "1") == "1"
KSTEP = int(_os.environ.get("KSTEP", "99"))
KYALT = int(_os.environ.get("KYALT", "0"))
KYB = int(_os.environ.get("KYB", "2"))
KYF = int(_os.environ.get("KYF", "8"))
KYSRC = _os.environ.get("KYSRC", "w2")
KYRHS = _os.environ.get("KYRHS", "zln")
KYW = _os.environ.get("KYW", "1") == "1"
KYD = int(_os.environ.get("KYD", "8"))
KYN = int(_os.environ.get("KYN", "512"))


class Em:

    def __init__(self):
        self.ops = {e: [] for e in ENGINES}
        self.cnt = {}
        self.last_wait = {e: {} for e in ENGINES}

    def sem(self, name):
        self.cnt.setdefault(name, 0)

    def op(self, eng, fn, waits=(), inc=None, amt=1):
        waits = list(waits)
        if eng in SELF_SEM:
            assert inc == SELF_SEM[eng], f"{eng} op must inc {SELF_SEM[eng]}"
            waits.append((SELF_SEM[eng], self.cnt[SELF_SEM[eng]]))
        w = []
        for sname, val in waits:
            assert val <= self.cnt[sname], (
                f"deadlock: {eng} waits {sname}>={val} > {self.cnt[sname]}")
            if val <= 0 or self.last_wait[eng].get(sname, -1) >= val:
                continue
            self.last_wait[eng][sname] = val
            w.append((sname, val))
        v = None
        if inc is not None:
            self.cnt[inc] += amt
            v = self.cnt[inc]
        self.ops[eng].append((w, fn, inc, amt))
        return v


def build_nc():
    nc = bass.Bass("TRN2", target_bir_lowering=False)

    d_xT = nc.dram_tensor("xT", [D, S], F32, kind="ExternalInput")
    d_w1p = nc.dram_tensor("w1p", [NP, D, 512], F32, kind="ExternalInput")
    d_b1c = nc.dram_tensor("b1c", [128, 32], F32, kind="ExternalInput")
    d_gb = nc.dram_tensor("gb", [128, 24], F32, kind="ExternalInput")
    d_consts = nc.dram_tensor("consts", [128, 256], BF16, kind="ExternalInput")
    d_cb = nc.dram_tensor("cb", [128, 8], F32, kind="ExternalInput")
    d_w2 = nc.dram_tensor("w2", [D, D], F32, kind="ExternalInput")
    d_th = nc.dram_tensor("th", [D, D], F32, kind="ExternalInput")
    d_thn = nc.dram_tensor("thn", [D, D], F32, kind="ExternalInput")
    d_ph = nc.dram_tensor("ph", [D, D], F32, kind="ExternalInput")
    d_phn = nc.dram_tensor("phn", [D, D], F32, kind="ExternalInput")
    d_out = nc.dram_tensor("out", [D, SH], F32, kind="ExternalOutput")
    d_ln1 = nc.dram_tensor("ln1scr", [NDT, 128, S], BF16)
    d_zscr = nc.dram_tensor("zscr", [NP, 128, SH], BF16)

    em = Em()
    for s in ("pe", "act", "dve", "ldc", "ldta", "ldtb", "ldx", "stln0",
              "stln1", "ldw", "ldr0", "ldr1", "stz0", "stz1", "ldz", "ldxc",
              "stout"):
        em.sem(s)

    with contextlib.ExitStack() as es:
        s_id = es.enter_context(nc.sbuf_tensor("c_id", [128, 128], BF16))
        s_ones = es.enter_context(nc.sbuf_tensor("c_ones", [128, 128], BF16))
        s_b1c = es.enter_context(nc.sbuf_tensor("c_b1c", [128, 32], F32))
        s_gb = es.enter_context(nc.sbuf_tensor("c_gb", [128, 24], F32))
        s_cb = es.enter_context(nc.sbuf_tensor("c_cb", [128, 8], F32))
        s_betabf = es.enter_context(nc.sbuf_tensor("c_bb", [128, 8], BF16))
        s_cos = es.enter_context(nc.sbuf_tensor("cosm", [128, 8192], BF16))
        s_sin = es.enter_context(nc.sbuf_tensor("sinm", [128, 8192], BF16))
        p_all = es.enter_context(nc.psum_tensor("p_all", [128, 4096], F32))
        sems = {}

        def bank(i, w=512):
            return p_all[:, i * 512:i * 512 + w]

        for dst_, src_ in ((s_id, lambda: d_consts[:, 0:128]),
                           (s_ones, lambda: d_consts[:, 128:256]),
                           (s_b1c, lambda: d_b1c[:, :]),
                           (s_gb, lambda: d_gb[:, :]),
                           (s_cb, lambda: d_cb[:, :])):
            em.op("sync", lambda e, d=dst_, s=src_: e.dma_start(
                out=d[:, :], in_=s()), waits=[("ldc", em.cnt["ldc"])],
                inc="ldc", amt=16)
        em.op("gpsimd", lambda e: e.dma_start(out=s_betabf[:, :],
              in_=d_gb[:, 8:16]), waits=[("ldw", em.cnt["ldw"])],
              inc="ldw", amt=16)
        LDC = em.cnt["ldc"]
        PI2 = lambda: s_cb[:, 0:1]
        EPSB = lambda: s_cb[:, 1:2]
        TINY = lambda: s_cb[:, 2:3]
        K1024 = lambda: s_cb[:, 3:4]
        ONE = lambda: s_cb[:, 4:5]


        with contextlib.ExitStack() as ps:
            s_ta = [ps.enter_context(nc.sbuf_tensor(f"ta{i}", [128, 1024], F32))
                    for i in range(2)]
            s_tb = [ps.enter_context(nc.sbuf_tensor(f"tb{i}", [128, 1024], F32))
                    for i in range(2)]
            s_tsum = [ps.enter_context(
                nc.sbuf_tensor(f"ts{i}", [128, 1024], F32)) for i in range(2)]
            buf_free = [0, 0]
            sum_free = [0, 0]
            it = 0
            for da, db_, cosb, dst in ((d_th, d_thn, True, s_cos),
                                       (d_ph, d_phn, False, s_sin)):
                for dt_ in range(NDT):
                    par = it % 2
                    la = em.op("sync", lambda e, a=da, d=dt_, p=par:
                               e.dma_start(out=s_ta[p][:, :],
                                           in_=a[d * 128:(d + 1) * 128, :]),
                               waits=[("dve", buf_free[par]),
                                      ("ldta", em.cnt["ldta"])],
                               inc="ldta", amt=16)
                    lb = em.op("sync", lambda e, a=db_, d=dt_, p=par:
                               e.dma_start(out=s_tb[p][:, :],
                                           in_=a[d * 128:(d + 1) * 128, :]),
                               waits=[("ldtb", em.cnt["ldtb"])],
                               inc="ldtb", amt=16)
                    sm = em.op("vector", lambda e, p=par: e.tensor_tensor(
                        s_tsum[p][:, :], s_ta[p][:, :], s_tb[p][:, :],
                        op=ALU.add),
                        waits=[("ldta", la), ("ldtb", lb),
                               ("act", sum_free[par])],
                        inc="dve")
                    buf_free[par] = sm
                    if cosb:
                        a = em.op("scalar", lambda e, p=par:
                                  e.activation(
                                      out=s_tsum[p][:, :],
                                      in_=s_tsum[p][:, :], func=AF.Sin,
                                      bias=PI2()),
                                  waits=[("dve", sm), ("ldc", LDC)], inc="act")
                        sm = em.op("vector", lambda e, d=dt_, p=par, ds=dst:
                                   e.tensor_scalar(
                                       ds[:, d * 1024:(d + 1) * 1024],
                                       s_tsum[p][:, :], ONE(), None,
                                       op0=ALU.subtract),
                                   waits=[("act", a)], inc="dve")
                        buf_free[par] = sm
                        sum_free[par] = a
                        it += 1
                        continue
                    if True:
                        a = em.op("scalar", lambda e, d=dt_, p=par, ds=dst:
                                  e.activation(
                                      out=ds[:, d * 1024:(d + 1) * 1024],
                                      in_=s_tsum[p][:, :], func=AF.Sin),
                                  waits=[("dve", sm)], inc="act")
                    sum_free[par] = a
                    it += 1

        with contextlib.ExitStack() as ps:
            s_pxt = ps.enter_context(nc.sbuf_tensor("pxt", [128, 8192], F32))
            s_px = [s_pxt[:, d * 1024:(d + 1) * 1024] for d in range(NDT)]
            s_pbf = [ps.enter_context(
                nc.sbuf_tensor(f"pb{i}", [128, 1024], BF16)) for i in range(2)]
            s_psq = [ps.enter_context(
                nc.sbuf_tensor(f"pq{i}", [128, 1024], BF16)) for i in range(2)]
            s_mus = ps.enter_context(nc.sbuf_tensor("mus", [128, 1024], F32))
            s_var = ps.enter_context(nc.sbuf_tensor("var", [128, 1024], F32))
            s_rb = ps.enter_context(nc.sbuf_tensor("rb", [128, 1024], F32))
            s_t1 = ps.enter_context(nc.sbuf_tensor("t1", [128, 1024], F32))
            s_l1c = [ps.enter_context(
                nc.sbuf_tensor(f"l1{i}", [128, 1024], BF16)) for i in range(2)]
            px_free = 0
            pbf_free = [0, 0]
            sq_free = [0, 0]
            psq_free = [0, 0]
            l1_free = [0, 0]
            chain_free = 0
            mstat_done = 0
            for ch in range(8 if TRUNC >= 2 else 0):
                c0 = ch * 1024
                bar = ([("dve", em.cnt["dve"]), ("act", em.cnt["act"])]
                       if ch == 0 else [])
                ldp = em.op("sync", lambda e, c=c0: e.dma_start(
                    out=s_pxt[:, :].rearrange("p (d t) -> p d t", d=NDT),
                    in_=d_xT[:, c:c + 1024]
                    .rearrange("(d p) t -> p d t", p=128)),
                    waits=[("dve", px_free), ("ldx", em.cnt["ldx"])] + bar,
                    inc="ldx", amt=16)
                lds = [ldp] * NDT
                for d in range(NDT):
                    cv = em.op("vector", lambda e, d=d: e.tensor_copy(
                        s_pbf[d % 2][:, :], s_px[d][:, :]),
                        waits=[("ldx", lds[d]), ("pe", pbf_free[d % 2]),
                               ("act", sq_free[d % 2])],
                        inc="dve")
                    sq = em.op("scalar", lambda e, d=d: e.activation(
                        out=s_psq[d % 2][:, :], in_=s_pbf[d % 2][:, :],
                        func=AF.Square),
                        waits=[("dve", cv), ("pe", psq_free[d % 2])],
                        inc="act")
                    sq_free[d % 2] = sq
                    mlast = None
                    for h in range(2):
                        mlast = em.op("tensor", lambda e, d=d, h=h: e.matmul(
                            bank(h), s_ones[:, :],
                            s_pbf[d % 2][:, h * 512:(h + 1) * 512],
                            start=(d == 0), stop=(d == NDT - 1)),
                            waits=([("dve", cv), ("ldc", LDC),
                                    ("dve", chain_free)] if h == 0 else []),
                            inc="pe")
                    pbf_free[d % 2] = mlast
                    for h in range(2):
                        mlast = em.op("tensor", lambda e, d=d, h=h: e.matmul(
                            bank(2 + h), s_ones[:, :],
                            s_psq[d % 2][:, h * 512:(h + 1) * 512],
                            start=(d == 0), stop=(d == NDT - 1)),
                            waits=([("act", sq)] if h == 0 else []),
                            inc="pe")
                    psq_free[d % 2] = mlast
                    mstat_done = mlast
                cmu = em.op("vector", lambda e: e.tensor_copy(
                    s_mus[:, :], p_all[:, 0:1024]),
                    waits=[("pe", mstat_done)], inc="dve")
                em.op("vector", lambda e: e.tensor_tensor(
                    s_var[:, :], s_mus[:, :], s_mus[:, :], op=ALU.mult),
                    inc="dve")
                cvar = em.op("vector", lambda e: e.tensor_tensor(
                    s_var[:, :], p_all[:, 1024:2048], s_var[:, :],
                    op=ALU.subtract), inc="dve")
                chain_free = cvar
                em.op("scalar", lambda e: e.activation(
                    out=s_rb[:, :], in_=s_var[:, :], func=AF.Ln, bias=EPSB()),
                    waits=[("dve", cvar)], inc="act")
                arb = em.op("scalar", lambda e: e.activation(
                    out=s_rb[:, :], in_=s_rb[:, :], func=AF.Exp, scale=-0.5),
                    inc="act")
                for d in range(NDT):
                    t = em.op("vector", lambda e, d=d: e.tensor_tensor(
                        s_t1[:, :], s_px[d][:, :], s_mus[:, :],
                        op=ALU.subtract),
                        waits=[("act", arb)], inc="dve")
                    if d == NDT - 1:
                        px_free = t
                    lc = em.op("vector", lambda e, d=d: e.tensor_tensor(
                        s_l1c[d % 2][:, :], s_t1[:, :], s_rb[:, :],
                        op=ALU.mult),
                        waits=[(f"stln{d % 2}", l1_free[d % 2])],
                        inc="dve")
                    st = em.op("gpsimd", lambda e, d=d, c=c0: e.dma_start(
                        out=d_ln1[d, :, c:c + 1024], in_=s_l1c[d % 2][:, :]),
                        waits=[("dve", lc),
                               (f"stln{d % 2}", em.cnt[f"stln{d % 2}"])],
                        inc=f"stln{d % 2}", amt=16)
                    l1_free[d % 2] = st
        LN1_ST0 = em.cnt["stln0"]
        LN1_ST1 = em.cnt["stln1"]

        with contextlib.ExitStack() as ps:
            s_kT = [ps.enter_context(nc.sbuf_tensor(f"kT{i}", [128, S], BF16))
                    for i in range(2)]
            s_vsq = [ps.enter_context(nc.sbuf_tensor(f"vs{i}", [128, S], BF16))
                     for i in range(2)]
            s_qT = [ps.enter_context(nc.sbuf_tensor(f"qT{i}", [128, SH], BF16))
                    for i in range(2)]
            s_uT = [ps.enter_context(nc.sbuf_tensor(f"uT{i}", [128, SH], BF16))
                    for i in range(2)]
            s_vfm = ps.enter_context(nc.sbuf_tensor("vfm", [128, S], BF16))
            s_rhs = [ps.enter_context(
                nc.sbuf_tensor(f"rh{i}", [128, 4096], BF16)) for i in range(2)]
            s_w1 = ps.enter_context(nc.sbuf_tensor("w1b", [128, 4096], BF16))
            s_att = [ps.enter_context(
                nc.sbuf_tensor(f"at{i}", [128, 1024], BF16)) for i in range(2)]
            s_z = [ps.enter_context(
                nc.sbuf_tensor(f"z{i}", [128, 512], BF16)) for i in range(2)]

            w1_free = 0
            rhs_free = [0, 0]
            unit_free = [0] * 4
            vt_free = 0
            attn_read_done = [0, 0]
            z_free = [0, 0]
            zm_val = 0
            proj_done_act = 0
            proj_done_dve = 0

            s_pstg = [ps.enter_context(
                nc.sbuf_tensor(f"pst{i}", [128, 512], F32)) for i in range(2)]
            NPAIR = 0 if TRUNC < 3 else (NP if TRUNC >= 5 else 1)
            proj_done = {}
            bank7 = [0]
            pstg_free = [0, 0]

            def proj_steps(pair):
                nonlocal w1_free, proj_done
                par = pair % 2
                bar = ([("dve", em.cnt["dve"]), ("act", em.cnt["act"]),
                        ("pe", em.cnt["pe"])] if pair == 0 else [])
                lw = em.op("gpsimd", lambda e, p=pair: e.dma_start(
                    out=s_w1[:, :].rearrange("p (d c) -> p d c", d=NDT),
                    in_=d_w1p[p].rearrange("(d p) c -> p d c", p=128)),
                    waits=[("pe", w1_free), ("ldc", LDC),
                           ("ldw", em.cnt["ldw"])] + bar,
                    inc="ldw", amt=16)
                yield
                g = 0
                kinds_dst = {"k": (3, s_kT[par]), "v": (1, s_vfm),
                             "q": (2, s_qT[par]), "u": (0, s_uT[par])}
                pend_vt = None
                for j in range(17):
                    if j < 16:
                        jp = j % 2
                        lr = em.op("sync", lambda e, j=j, p=jp: e.dma_start(
                            out=s_rhs[p][:, :].rearrange(
                                "p (d t) -> p d t", d=NDT),
                            in_=d_ln1[:, :, j * 512:(j + 1) * 512]
                            .rearrange("d p t -> p d t")),
                            waits=[("pe", rhs_free[jp]), ("stln0", LN1_ST0),
                                   ("stln1", LN1_ST1),
                                   (f"ldr{jp}", em.cnt[f"ldr{jp}"])],
                            inc=f"ldr{jp}", amt=16)
                        yield
                        kinds = (["k", "v", "q", "u"] if j < 8
                                 else ["k", "v"])
                        mm_last = None
                        for kind in kinds:
                            ki, dst = kinds_dst[kind]
                            gp = g % 2
                            for d in range(NDT):
                                w = []
                                if d == 0:
                                    w = [(f"ldr{jp}", lr), ("ldw", lw),
                                         ("dve", bank7[0]),
                                         ("pe", attn_read_done[par])]
                                mm_last = em.op("tensor",
                                    lambda e, d=d, k=ki, p=jp: e.matmul(
                                        bank(7),
                                        s_w1[:, d * 512 + k * 128:
                                             d * 512 + k * 128 + 128],
                                        s_rhs[p][:, d * 512:(d + 1) * 512],
                                        start=(d == 0), stop=(d == NDT - 1)),
                                    waits=w, inc="pe")
                            yield
                            cp = em.op("vector", lambda e, gp=gp:
                                       e.tensor_copy(s_pstg[gp][:, :],
                                                     bank(7)),
                                       waits=[("pe", mm_last),
                                              ("act", pstg_free[gp])],
                                       inc="dve")
                            bank7[0] = cp
                            yield
                            bcol = ki * 8 + pair
                            ev = em.op("scalar", lambda e, gp=gp, j=j,
                                       b=bcol, ds=dst: e.activation(
                                           out=ds[:, j * 512:(j + 1) * 512],
                                           in_=s_pstg[gp][:, :],
                                           func=AF.Silu,
                                           bias=s_b1c[:, b:b + 1]),
                                       waits=[("dve", cp),
                                              ("pe", attn_read_done[par])],
                                       inc="act")
                            pstg_free[gp] = ev
                            if kind == "v":
                                vdef = (j, ev)
                            g += 1
                            yield
                        rhs_free[jp] = mm_last
                    if pend_vt is not None:
                        vj, vev = pend_vt
                        tlast = None
                        for c in range(4):
                            tlast = em.op("tensor",
                                lambda e, vj=vj, c=c: e.matmul(
                                    p_all[:, 3584 + c * 128:
                                          3584 + (c + 1) * 128],
                                    s_vfm[:, vj * 512 + c * 128:
                                          vj * 512 + (c + 1) * 128],
                                    s_id[:, :], start=True, stop=True),
                                waits=([("act", vev), ("dve", bank7[0]),
                                        ("ldc", LDC)] if c == 0 else []),
                                inc="pe")
                        yield
                        bank7[0] = em.op("vector", lambda e, vj=vj, p=par:
                                         e.tensor_copy(
                                             s_vsq[p][:, vj * 512:
                                                      (vj + 1) * 512],
                                             bank(7)),
                                         waits=[("pe", tlast),
                                                ("pe", attn_read_done[par])],
                                         inc="dve")
                        yield
                    if j < 16:
                        pend_vt = vdef
                proj_done[pair] = (em.cnt["act"], em.cnt["dve"])
                w1_free = em.cnt["pe"]

            if NPAIR > 0:
                for _ in proj_steps(0):
                    pass
            for pair in range(NPAIR):
                par = pair % 2
                bg = (proj_steps(pair + 1) if pair + 1 < NPAIR else iter(()))
                bg_alive = [True]

                def pop_bg(n):
                    for _ in range(n):
                        if not bg_alive[0]:
                            return
                        try:
                            next(bg)
                        except StopIteration:
                            bg_alive[0] = False
                            return

                if TRUNC < 4:
                    pop_bg(1 << 30)
                    continue
                NU = 512
                qk_val = [0] * NU
                av_val = [0] * NU
                silu_val = [0] * NU
                pd = proj_done[pair]
                for i in range(NU + 2):
                    if i < NU:
                        sl = i % 3
                        qp, kc = i // 64, i % 64
                        w = [("act", silu_val[i - 3] if i >= 3 else 0)]
                        if i == 0:
                            w += [("act", pd[0]), ("dve", pd[1])]
                        em.op("tensor", lambda e, sl=sl, q=qp, k=kc, p=par:
                              e.matmul(
                                  p_all[:, 512 + sl * 1024:
                                        512 + sl * 1024 + 512],
                                  s_kT[p][0:64, k * 128:(k + 1) * 128],
                                  s_qT[p][0:64, q * 512:(q + 1) * 512],
                                  start=True, stop=True,
                                  tile_position=(0, 0)),
                              waits=w)
                        qk_val[i] = em.op("tensor",
                              lambda e, sl=sl, q=qp, k=kc, p=par: e.matmul(
                                  p_all[:, 512 + sl * 1024 + 512:
                                        512 + sl * 1024 + 1024],
                                  s_kT[p][64:128, k * 128:(k + 1) * 128],
                                  s_qT[p][64:128, q * 512:(q + 1) * 512],
                                  start=True, stop=True,
                                  tile_position=(64, 0)),
                              inc="pe")
                    u = i - 2
                    if u >= 0:
                        w = [("pe", qk_val[u]),
                             ("pe", av_val[u - 2] if u >= 2 else 0)]
                        sl = u % 3
                        silu_val[u] = em.op("scalar",
                              lambda e, sl=sl, ab=u % 2: e.activation(
                                  out=s_att[ab][:, :],
                                  in_=p_all[:, 512 + sl * 1024:
                                            512 + sl * 1024 + 1024],
                                  func=AF.Silu),
                              waits=w, inc="act")
                        qp, kc = u // 64, u % 64
                        w = [("act", silu_val[u])]
                        if kc == 0:
                            w.append(("dve", zm_val))
                        em.op("tensor", lambda e, k=kc, ab=u % 2, p=par:
                              e.matmul(
                                  p_all[0:64, 0:512],
                                  s_vsq[p][:, k * 128:k * 128 + 64],
                                  s_att[ab][:, 0:512],
                                  start=(k == 0), stop=(k == 63),
                                  tile_position=(0, 0)),
                              waits=w)
                        av_val[u] = em.op("tensor",
                              lambda e, k=kc, ab=u % 2, p=par: e.matmul(
                                  p_all[64:128, 0:512],
                                  s_vsq[p][:, k * 128 + 64:k * 128 + 128],
                                  s_att[ab][:, 512:1024],
                                  start=(k == 0), stop=(k == 63),
                                  tile_position=(0, 64)),
                              inc="pe")
                        if kc == 63:
                            zp = (qp % 2)
                            zm_val = em.op("vector",
                                  lambda e, q=qp, p=par, zp=zp:
                                  e.tensor_tensor(
                                      s_z[zp][:, :], p_all[:, 0:512],
                                      s_uT[p][:, q * 512:(q + 1) * 512],
                                      op=ALU.mult),
                                  waits=[("pe", av_val[u]),
                                         (f"stz{zp}", z_free[zp])],
                                  inc="dve")
                            z_free[zp] = em.op("gpsimd",
                                  lambda e, q=qp, pr=pair, zp=zp:
                                  e.dma_start(
                                      out=d_zscr[pr, :, q * 512:(q + 1) * 512],
                                      in_=s_z[zp][:, :]),
                                  waits=[("dve", zm_val),
                                         (f"stz{zp}", em.cnt[f"stz{zp}"])],
                                  inc=f"stz{zp}", amt=16)
                    if i % 3 == 2:
                        pop_bg(1)
                pop_bg(1 << 30)
                attn_read_done[par] = em.cnt["pe"]
        Z_ALL0 = em.cnt["stz0"]
        Z_ALL1 = em.cnt["stz1"]


        if TRUNC >= 6:
            with contextlib.ExitStack() as ts:
                s_w2 = ts.enter_context(nc.sbuf_tensor("w2b", [128, 8192], BF16))
                s_c2 = ts.enter_context(nc.sbuf_tensor("c2", [128, 8], F32))
                s_zc = [ts.enter_context(
                    nc.sbuf_tensor("zc0", [128, 4096], BF16))] * 2
                s_xc = [ts.enter_context(
                    nc.sbuf_tensor("xc0", [128, 4096], F32))] * 2
                s_zsq = ts.enter_context(nc.sbuf_tensor("zsq", [128, 4096], BF16))
                s_zln = ts.enter_context(nc.sbuf_tensor("zln", [128, 4096], BF16))
                s_y = ts.enter_context(nc.sbuf_tensor("yb", [128, 4096], BF16))
                s_ysq = ts.enter_context(nc.sbuf_tensor("ysq", [128, 4096], BF16))
                s_x1 = ts.enter_context(nc.sbuf_tensor("x1", [128, 4096], F32))
                s_x1b = ts.enter_context(nc.sbuf_tensor("x1b", [128, 4096], BF16))
                s_h = ts.enter_context(nc.sbuf_tensor("hb", [128, 4096], BF16))
                s_mag = ts.enter_context(nc.sbuf_tensor("mag", [128, 4096], BF16))
                s_m2 = ts.enter_context(nc.sbuf_tensor("m2", [128, 4096], BF16))
                s_sqa = ts.enter_context(nc.sbuf_tensor("sqa", [128, 512], F32))
                s_sqb = ts.enter_context(nc.sbuf_tensor("sqb", [128, 512], F32))
                s_x2 = ts.enter_context(nc.sbuf_tensor("x2", [128, 4096], F32))
                s_mu = [ts.enter_context(
                    nc.sbuf_tensor(f"mu{i}", [128, 512], F32)) for i in range(2)]
                s_vr = [ts.enter_context(
                    nc.sbuf_tensor(f"vr{i}", [128, 512], F32)) for i in range(2)]
                s_r = [ts.enter_context(
                    nc.sbuf_tensor(f"r{i}", [128, 512], F32)) for i in range(4)]
                s_mr = [ts.enter_context(
                    nc.sbuf_tensor(f"mr{i}", [128, 512], F32)) for i in range(4)]

                lw2 = em.op("gpsimd", lambda e: e.dma_start(
                    out=s_w2[:, :].rearrange("p (d c) -> p d c", d=NDT),
                    in_=d_w2.rearrange("(d p) c -> p d c", p=128)),
                    waits=[("ldc", LDC), ("ldw", em.cnt["ldw"]),
                           ("pe", em.cnt["pe"]), ("act", em.cnt["act"]),
                           ("dve", em.cnt["dve"])],
                    inc="ldw", amt=16)
                c2mm = None
                for f in range(8):
                    for d in range(NDT):
                        c2mm = em.op("tensor", lambda e, f=f, d=d: e.matmul(
                            p_all[:, 3072 + f:3072 + f + 1],
                            s_w2[:, d * 1024 + f * 128:d * 1024 + (f + 1) * 128],
                            s_betabf[:, d:d + 1],
                            start=(d == 0), stop=(d == NDT - 1)),
                            waits=([("ldw", lw2), ("ldc", LDC),
                                    ("act", em.cnt["act"]),
                                    ("dve", em.cnt["dve"])]
                                   if f == 0 and d == 0 else []),
                            inc="pe")
                c2dve = em.op("vector", lambda e: e.tensor_tensor(
                    s_c2[:, :], p_all[:, 3072:3080], s_gb[:, 16:24], op=ALU.add),
                    waits=[("pe", c2mm)], inc="dve")
                gma = None
                for d in range(NDT):
                    gma = em.op("vector", lambda e, d=d: e.tensor_scalar(
                        s_w2[:, d * 1024:(d + 1) * 1024],
                        s_w2[:, d * 1024:(d + 1) * 1024],
                        s_gb[:, d:d + 1], None, op0=ALU.mult),
                        waits=[("pe", c2mm)], inc="dve")
                W2RDY = gma if KGM else em.cnt["dve"]

                zc_free = [em.cnt["pe"], em.cnt["dve"]]
                xc_free = [em.cnt["dve"], 0]
                zsq_free = 0
                stv = 0

                def rstd_chain(mwait, cpar, rix):
                    em.op("vector", lambda e: e.tensor_copy(
                        s_mu[cpar][:, :], bank(0)),
                        waits=[("pe", mwait)], inc="dve")
                    em.op("vector", lambda e: e.tensor_tensor(
                        s_vr[cpar][:, :], s_mu[cpar][:, :], s_mu[cpar][:, :],
                        op=ALU.mult), inc="dve")
                    vfree = em.op("vector", lambda e: e.tensor_tensor(
                        s_vr[cpar][:, :], bank(1), s_vr[cpar][:, :],
                        op=ALU.subtract), inc="dve")
                    em.op("scalar", lambda e: e.activation(
                        out=s_r[rix][:, :], in_=s_vr[cpar][:, :], func=AF.Ln,
                        bias=EPSB()), waits=[("dve", vfree)], inc="act")
                    ar = em.op("scalar", lambda e: e.activation(
                        out=s_r[rix][:, :], in_=s_r[rix][:, :], func=AF.Exp,
                        scale=-0.5), inc="act")
                    mrv = em.op("vector", lambda e: e.tensor_tensor(
                        s_mr[rix][:, :], s_mu[cpar][:, :], s_r[rix][:, :],
                        op=ALU.mult), waits=[("act", ar)], inc="dve")
                    return vfree, mrv

                stats_free = c2dve
                for n in range(TAILN):
                    c0 = n * 512
                    np_ = n % 2
                    lz = em.op("sync", lambda e, c=c0, p=np_: e.dma_start(
                        out=s_zc[p][:, :].rearrange("p (d t) -> p d t", d=NP),
                        in_=d_zscr[:, :, c:c + 512].rearrange("d p t -> p d t")),
                        waits=[("stz0", Z_ALL0), ("stz1", Z_ALL1),
                               ("pe", zc_free[0]), ("dve", zc_free[1]),
                               ("ldz", em.cnt["ldz"])],
                        inc="ldz", amt=16)
                    lx = em.op("sync", lambda e, c=c0, p=np_: e.dma_start(
                        out=s_xc[p][:, :].rearrange("p (d t) -> p d t", d=NDT),
                        in_=d_xT[:, c:c + 512]
                        .rearrange("(d p) t -> p d t", p=128)),
                        waits=[("dve", xc_free[0]), ("ldxc", em.cnt["ldxc"])],
                        inc="ldxc", amt=16)
                    if KSTEP <= 1: continue
                    zq = em.op("scalar", lambda e, p=np_: e.activation(
                        out=s_zsq[:, :], in_=s_zc[p][:, :], func=AF.Square),
                        waits=[("ldz", lz), ("dve", zsq_free)], inc="act")
                    for d in range(NDT):
                        em.op("tensor", lambda e, d=d, p=np_: e.matmul(
                            bank(0), s_ones[:, :],
                            s_zc[p][:, d * 512:(d + 1) * 512],
                            start=(d == 0), stop=(d == NDT - 1)),
                            waits=([("ldz", lz), ("dve", stats_free)]
                                   if d == 0 else []))
                    st2 = None
                    for d in range(NDT):
                        st2 = em.op("tensor", lambda e, d=d: e.matmul(
                            bank(1), s_ones[:, :],
                            s_zsq[:, d * 512:(d + 1) * 512],
                            start=(d == 0), stop=(d == NDT - 1)),
                            waits=([("act", zq)] if d == 0 else []), inc="pe")
                    stats_free, mr2 = rstd_chain(st2, 0, 0)
                    if KSTEP <= 2: continue
                    zl = None
                    for d in range(NDT):
                        em.op("vector", lambda e, d=d, p=np_: e.tensor_tensor(
                            s_zsq[:, d * 512:(d + 1) * 512],
                            s_zc[p][:, d * 512:(d + 1) * 512], s_r[0][:, :],
                            op=ALU.mult), waits=[("dve", mr2)], inc="dve")
                        zl = em.op("vector", lambda e, d=d: e.tensor_tensor(
                            s_zln[:, d * 512:(d + 1) * 512],
                            s_zsq[:, d * 512:(d + 1) * 512], s_mr[0][:, :],
                            op=ALU.subtract), inc="dve")
                    if KSTEP <= 3: continue
                    ysq_l = None
                    ypar_dve = [0, 0]
                    for f in range(8):
                        mm = None
                        for d in range(NDT):
                            w = []
                            if d == 0:
                                w = [("dve", zl), ("dve", W2RDY),
                                     ("dve", ypar_dve[f % 2])]
                            mm = em.op("tensor", lambda e, f=f, d=d: e.matmul(
                                bank(2 + f % 2),
                                s_w2[:, d * 1024 + f * 128:
                                     d * 1024 + (f + 1) * 128],
                                s_zln[:, d * 512:(d + 1) * 512],
                                start=(d == 0), stop=(d == NDT - 1)),
                                waits=w, inc="pe")
                            if f == 0 and d == NDT - 1:
                                zc_free[0] = mm
                                zc_free[1] = zl
                        yev = em.op("vector", lambda e, f=f: e.tensor_tensor(
                            s_y[:, f * 512:(f + 1) * 512],
                            bank(2 + f % 2),
                            s_c2[:, f:f + 1].broadcast_to([128, 512]),
                            op=ALU.add),
                            waits=[("pe", mm), ("dve", c2dve)], inc="dve")
                        ysq_l = em.op("scalar", lambda e, f=f: e.activation(
                            out=s_ysq[:, f * 512:(f + 1) * 512],
                            in_=s_y[:, f * 512:(f + 1) * 512],
                            func=AF.Square),
                            waits=[("dve", yev)], inc="act")
                        ypar_dve[f % 2] = yev
                    if KSTEP <= 4: continue
                    for f in range(8):
                        em.op("tensor", lambda e, f=f: e.matmul(
                            bank(0), s_ones[:, :], s_y[:, f * 512:(f + 1) * 512],
                            start=(f == 0), stop=(f == 7)),
                            waits=([("dve", yev), ("dve", stats_free)]
                                   if f == 0 else []))
                    sty = None
                    for f in range(8):
                        sty = em.op("tensor", lambda e, f=f: e.matmul(
                            bank(1), s_ones[:, :], s_ysq[:, f * 512:(f + 1) * 512],
                            start=(f == 0), stop=(f == 7)),
                            waits=([("act", ysq_l)] if f == 0 else []), inc="pe")
                    stats_free, mry = rstd_chain(sty, 1, 1)
                    if KSTEP <= 5: continue
                    x1b_l = None
                    x1sq_l = None
                    for f in range(8):
                        em.op("vector", lambda e, f=f: e.tensor_tensor(
                            s_zsq[:, f * 512:(f + 1) * 512],
                            s_y[:, f * 512:(f + 1) * 512], s_r[1][:, :],
                            op=ALU.mult),
                            waits=[("dve", mry), ("ldxc", lx)], inc="dve")
                        em.op("vector", lambda e, f=f: e.tensor_tensor(
                            s_zsq[:, f * 512:(f + 1) * 512],
                            s_zsq[:, f * 512:(f + 1) * 512], s_mr[1][:, :],
                            op=ALU.subtract), inc="dve")
                        x1v = em.op("vector", lambda e, f=f, p=np_:
                                    e.tensor_tensor(
                                        s_x1[:, f * 512:(f + 1) * 512],
                                        s_zsq[:, f * 512:(f + 1) * 512],
                                        s_xc[p][:, f * 512:(f + 1) * 512],
                                        op=ALU.add), inc="dve")
                        x1b_l = em.op("scalar", lambda e, f=f: e.activation(
                            out=s_x1b[:, f * 512:(f + 1) * 512],
                            in_=s_x1[:, f * 512:(f + 1) * 512], func=AF.Copy),
                            waits=[("dve", x1v)], inc="act")
                        x1sq_l = em.op("scalar", lambda e, f=f: e.activation(
                            out=s_ysq[:, f * 512:(f + 1) * 512],
                            in_=s_x1[:, f * 512:(f + 1) * 512], func=AF.Square),
                            inc="act")
                    xc_free[0] = em.cnt["dve"]
                    if KSTEP <= 6: continue
                    for f in range(8):
                        em.op("tensor", lambda e, f=f: e.matmul(
                            bank(0), s_ones[:, :], s_x1b[:, f * 512:(f + 1) * 512],
                            start=(f == 0), stop=(f == 7)),
                            waits=([("act", x1b_l), ("dve", stats_free)]
                                   if f == 0 else []))
                    st3 = None
                    for f in range(8):
                        st3 = em.op("tensor", lambda e, f=f: e.matmul(
                            bank(1), s_ones[:, :], s_ysq[:, f * 512:(f + 1) * 512],
                            start=(f == 0), stop=(f == 7)),
                            waits=([("act", x1sq_l)] if f == 0 else []), inc="pe")
                    stats_free, mr3 = rstd_chain(st3, 0, 2)
                    if KSTEP <= 7: continue
                    hl = None
                    for f in range(8):
                        em.op("vector", lambda e, f=f: e.tensor_tensor(
                            s_zsq[:, f * 512:(f + 1) * 512],
                            s_x1b[:, f * 512:(f + 1) * 512], s_r[2][:, :],
                            op=ALU.mult), waits=[("dve", mr3)], inc="dve")
                        hl = em.op("vector", lambda e, f=f: e.tensor_tensor(
                            s_h[:, f * 512:(f + 1) * 512],
                            s_zsq[:, f * 512:(f + 1) * 512], s_mr[2][:, :],
                            op=ALU.subtract), inc="dve")
                    if KSTEP <= 8: continue
                    ri_free = [0, 0, 0, 0]
                    m2_l = 0
                    mag_l = None
                    for fo in range(8):
                        rb_ = 4 + (fo % 2) * 2
                        ib_ = 5 + (fo % 2) * 2
                        mmr = None
                        for d in range(NDT):
                            w = []
                            if d == 0:
                                w = [("dve", hl), ("act", ri_free[2 * (fo % 2)])]
                            mmr = em.op("tensor", lambda e, d=d, f=fo, b=rb_:
                                        e.matmul(
                                            bank(b),
                                            s_cos[:, d * 1024 + f * 128:
                                                  d * 1024 + (f + 1) * 128],
                                            s_h[:, d * 512:(d + 1) * 512],
                                            start=(d == 0), stop=(d == NDT - 1)),
                                        waits=w, inc="pe")
                        mmi = None
                        for d in range(NDT):
                            w = []
                            if d == 0:
                                w = [("act", ri_free[2 * (fo % 2) + 1])]
                            mmi = em.op("tensor", lambda e, d=d, f=fo, b=ib_:
                                        e.matmul(
                                            bank(b),
                                            s_sin[:, d * 1024 + f * 128:
                                                  d * 1024 + (f + 1) * 128],
                                            s_h[:, d * 512:(d + 1) * 512],
                                            start=(d == 0), stop=(d == NDT - 1)),
                                        waits=w, inc="pe")
                        sa = em.op("scalar", lambda e, b=rb_: e.activation(
                            out=s_sqa[:, :], in_=bank(b), func=AF.Square),
                            waits=[("pe", mmr), ("dve", m2_l)], inc="act")
                        ri_free[2 * (fo % 2)] = sa
                        sb_ = em.op("scalar", lambda e, b=ib_: e.activation(
                            out=s_sqb[:, :], in_=bank(b), func=AF.Square),
                            waits=[("pe", mmi)], inc="act")
                        ri_free[2 * (fo % 2) + 1] = sb_
                        m2_l = em.op("vector", lambda e, f=fo: e.tensor_tensor(
                            s_m2[:, f * 512:(f + 1) * 512], s_sqa[:, :],
                            s_sqb[:, :], op=ALU.add),
                            waits=[("act", sb_)], inc="dve")
                        em.op("scalar", lambda e, f=fo: e.activation(
                            out=s_mag[:, f * 512:(f + 1) * 512],
                            in_=s_m2[:, f * 512:(f + 1) * 512], func=AF.Ln,
                            bias=TINY()), waits=[("dve", m2_l)], inc="act")
                        mag_l = em.op("scalar", lambda e, f=fo: e.activation(
                            out=s_mag[:, f * 512:(f + 1) * 512],
                            in_=s_mag[:, f * 512:(f + 1) * 512], func=AF.Exp,
                            scale=0.5), inc="act")
                    if KSTEP <= 9: continue
                    for f in range(8):
                        em.op("tensor", lambda e, f=f: e.matmul(
                            bank(0), s_ones[:, :], s_mag[:, f * 512:(f + 1) * 512],
                            start=(f == 0), stop=(f == 7)),
                            waits=([("act", mag_l), ("dve", stats_free)]
                                   if f == 0 else []))
                    st4 = None
                    for f in range(8):
                        st4 = em.op("tensor", lambda e, f=f: e.matmul(
                            bank(1), s_ones[:, :], s_m2[:, f * 512:(f + 1) * 512],
                            start=(f == 0), stop=(f == 7)),
                            waits=([("dve", m2_l)] if f == 0 else []), inc="pe")
                    stats_free, mr4 = rstd_chain(st4, 1, 3)
                    if KSTEP <= 10: continue
                    prev_st = stv
                    xl = None
                    for f in range(8):
                        em.op("vector", lambda e, f=f: e.tensor_tensor(
                            s_zsq[:, f * 512:(f + 1) * 512],
                            s_mag[:, f * 512:(f + 1) * 512], s_r[3][:, :],
                            op=ALU.mult),
                            waits=[("dve", mr4), ("stout", prev_st)], inc="dve")
                        em.op("vector", lambda e, f=f: e.tensor_tensor(
                            s_zsq[:, f * 512:(f + 1) * 512],
                            s_zsq[:, f * 512:(f + 1) * 512], s_mr[3][:, :],
                            op=ALU.subtract), inc="dve")
                        xl = em.op("vector", lambda e, f=f: e.tensor_tensor(
                            s_x2[:, f * 512:(f + 1) * 512],
                            s_zsq[:, f * 512:(f + 1) * 512],
                            s_x1[:, f * 512:(f + 1) * 512], op=ALU.add),
                            inc="dve")
                    zsq_free = xl
                    stv = em.op("gpsimd", lambda e, c=c0: e.dma_start(
                        out=d_out[:, c:c + 512].rearrange("(f p) t -> p f t",
                                                          p=128),
                        in_=s_x2[:, :].rearrange("p (f t) -> p f t", f=NDT)),
                        waits=[("dve", xl), ("stout", em.cnt["stout"])],
                        inc="stout", amt=16)
                em.op("gpsimd", lambda e: e.nop(),
                      waits=[("stout", em.cnt["stout"])])
            if int(_os.environ.get("KEXTRA2", "0")):
                em.op("tensor", lambda e: e.matmul(
                    bank(2), s_cos[:, 0:128], s_cos[:, 0:512],
                    start=True, stop=True), inc="pe")
            if TAILN < 8:
                waits = [(s_, em.cnt[s_]) for s_ in em.cnt]
                _mk = _os.environ.get("KMARK", "cos")
                msrc = {"cos": s_cos[:, 0:4096], "zln": s_zln[:, :],
                        "zc": s_zc[0][:, :], "w2": s_w2[:, 0:4096],
                        "r0": s_r[0][:, :], "mr0": s_mr[0][:, :]}[_mk]
                em.op("gpsimd", lambda e, m=msrc: e.dma_start(
                    out=d_out[0:128, 0:m.shape[-1]], in_=m),
                    waits=waits, inc="stout", amt=16)
                em.op("gpsimd", lambda e: e.nop(),
                      waits=[("stout", em.cnt["stout"])])
        else:
            waits = [(s_, em.cnt[s_]) for s_ in em.cnt]
            em.op("gpsimd", lambda e: e.dma_start(
                out=d_out[0:128, :], in_=s_cos[:, 0:4096]),
                waits=waits, inc="stout", amt=16)
            em.op("gpsimd", lambda e: e.nop(),
                  waits=[("stout", em.cnt["stout"])])

        for name in em.cnt:
            sems[name] = es.enter_context(nc.semaphore(name))
        with nc.Block() as block:
            def mk(eng_name):
                def body(eng):
                    for waits, fn, inc, amt in em.ops[eng_name]:
                        for sname, val in waits:
                            eng.wait_ge(sems[sname], val)
                        r = fn(eng)
                        if inc is not None:
                            assert r is not None, f"{eng_name} inc={inc}"
                            r.then_inc(sems[inc], amt)
                return body
            block.sync(mk("sync"))
            block.gpsimd(mk("gpsimd"))
            block.tensor(mk("tensor"))
            block.scalar(mk("scalar"))
            block.vector(mk("vector"))

    return nc


def kernel(L0_attn_x, L0_attn_W1, L0_attn_b1, L0_attn_W2, L0_attn_b2,
           L0_attn_gamma, L0_attn_beta, L0_ffn_theta, L0_ffn_phi,
           L0_ffn_theta_noise, L0_ffn_phi_noise):
    x = np.asarray(L0_attn_x, np.float32)
    W1 = np.asarray(L0_attn_W1, np.float32)
    b1 = np.asarray(L0_attn_b1, np.float32)
    W2 = np.ascontiguousarray(np.asarray(L0_attn_W2, np.float32))
    b2 = np.asarray(L0_attn_b2, np.float32)
    gamma = np.asarray(L0_attn_gamma, np.float32)
    beta = np.asarray(L0_attn_beta, np.float32)
    th = np.ascontiguousarray(np.asarray(L0_ffn_theta, np.float32))
    thn = np.ascontiguousarray(np.asarray(L0_ffn_theta_noise, np.float32))
    ph = np.ascontiguousarray(np.asarray(L0_ffn_phi, np.float32))
    phn = np.ascontiguousarray(np.asarray(L0_ffn_phi_noise, np.float32))

    xT = np.ascontiguousarray(x.transpose(0, 2, 1))

    w1p = np.empty((NP, D, 512), np.float32)
    for p in range(NP):
        for k in range(4):
            w1p[p, :, k * 128:(k + 1) * 128] = \
                W1[:, k * D + p * 128:k * D + (p + 1) * 128]
    b1c = np.ascontiguousarray(
        b1.reshape(4, 8, 128).transpose(2, 0, 1).reshape(128, 32))
    gb = np.ascontiguousarray(np.concatenate(
        [gamma.reshape(8, 128).T, beta.reshape(8, 128).T,
         b2.reshape(8, 128).T], axis=1))
    consts = np.concatenate(
        [np.eye(128, dtype=np.float32) / S,
         np.ones((128, 128), np.float32) / D], axis=1).astype(BF)
    cb = np.tile(np.array([[math.pi / 2, EPS, 1e-30, 1024.0,
                            1.0, 0.0, 0.0, 0.0]], np.float32), (128, 1))

    nc = build_nc()
    in_maps = []
    for c in range(8):
        b, half = c // 2, c % 2
        off = half * SH
        xrot = np.ascontiguousarray(
            np.concatenate([xT[b][:, off:], xT[b][:, :off]], axis=1))
        in_maps.append(dict(
            xT=xrot, w1p=w1p, b1c=b1c, gb=gb, consts=consts, cb=cb,
            w2=W2, th=th, thn=thn, ph=ph, phn=phn))
    res = run_bass_kernel_spmd(nc, in_maps, core_ids=list(range(8)))
    out = np.empty_like(x)
    for c in range(8):
        b, half = c // 2, c % 2
        out[b, half * SH:(half + 1) * SH, :] = res.results[c]["out"].T
    return out


if __name__ == "__main__":
    import reference as _r
    inp = _r.setup_inputs(0)
    want = np.asarray(_r.reference(**inp))
    got = kernel(**{k: np.asarray(v) for k, v in inp.items()})
    err = got - want
    rvr = float((err ** 2).mean() / (want ** 2).mean())
    print("rvr", rvr, "max", float(np.abs(err).max()))


# version 06
# speedup vs baseline: 1.0029x; 1.0029x over previous
import contextlib
import math
import numpy as np
import ml_dtypes
import concourse.bass as bass
import concourse.mybir as mybir
from concourse.bass_utils import run_bass_kernel_spmd

F32 = mybir.dt.float32
BF16 = mybir.dt.bfloat16
AF = mybir.ActivationFunctionType
ALU = mybir.AluOpType
BF = ml_dtypes.bfloat16

D = 1024
S = 8192
SH = 4096
NDT = 8
NP = 8
EPS = 1e-5

ENGINES = ("sync", "gpsimd", "tensor", "scalar", "vector")
SELF_SEM = {"vector": "dve", "scalar": "act"}
import os as _os
TRUNC = int(_os.environ.get("KTRUNC", "6"))
TAILN = int(_os.environ.get("KTAILN", "8"))
KC2 = _os.environ.get("KC2", "1") == "1"
KGM = _os.environ.get("KGM", "1") == "1"
KW2 = _os.environ.get("KW2", "1") == "1"
KSTEP = int(_os.environ.get("KSTEP", "99"))
KYALT = int(_os.environ.get("KYALT", "0"))
KYB = int(_os.environ.get("KYB", "2"))
KYF = int(_os.environ.get("KYF", "8"))
KYSRC = _os.environ.get("KYSRC", "w2")
KYRHS = _os.environ.get("KYRHS", "zln")
KYW = _os.environ.get("KYW", "1") == "1"
KYD = int(_os.environ.get("KYD", "8"))
KYN = int(_os.environ.get("KYN", "512"))


class Em:

    def __init__(self):
        self.ops = {e: [] for e in ENGINES}
        self.cnt = {}
        self.last_wait = {e: {} for e in ENGINES}

    def sem(self, name):
        self.cnt.setdefault(name, 0)

    def op(self, eng, fn, waits=(), inc=None, amt=1):
        waits = list(waits)
        if eng in SELF_SEM:
            assert inc == SELF_SEM[eng], f"{eng} op must inc {SELF_SEM[eng]}"
            waits.append((SELF_SEM[eng], self.cnt[SELF_SEM[eng]]))
        w = []
        for sname, val in waits:
            assert val <= self.cnt[sname], (
                f"deadlock: {eng} waits {sname}>={val} > {self.cnt[sname]}")
            if val <= 0 or self.last_wait[eng].get(sname, -1) >= val:
                continue
            self.last_wait[eng][sname] = val
            w.append((sname, val))
        v = None
        if inc is not None:
            self.cnt[inc] += amt
            v = self.cnt[inc]
        self.ops[eng].append((w, fn, inc, amt))
        return v


def build_nc():
    nc = bass.Bass("TRN2", target_bir_lowering=False)

    d_xT = nc.dram_tensor("xT", [D, S], F32, kind="ExternalInput")
    d_w1p = nc.dram_tensor("w1p", [NP, D, 512], F32, kind="ExternalInput")
    d_b1c = nc.dram_tensor("b1c", [128, 32], F32, kind="ExternalInput")
    d_gb = nc.dram_tensor("gb", [128, 24], F32, kind="ExternalInput")
    d_consts = nc.dram_tensor("consts", [128, 256], BF16, kind="ExternalInput")
    d_cb = nc.dram_tensor("cb", [128, 8], F32, kind="ExternalInput")
    d_w2 = nc.dram_tensor("w2", [D, D], F32, kind="ExternalInput")
    d_th = nc.dram_tensor("th", [D, D], F32, kind="ExternalInput")
    d_thn = nc.dram_tensor("thn", [D, D], F32, kind="ExternalInput")
    d_ph = nc.dram_tensor("ph", [D, D], F32, kind="ExternalInput")
    d_phn = nc.dram_tensor("phn", [D, D], F32, kind="ExternalInput")
    d_out = nc.dram_tensor("out", [D, SH], F32, kind="ExternalOutput")
    d_ln1 = nc.dram_tensor("ln1scr", [NDT, 128, S], BF16)
    d_zscr = nc.dram_tensor("zscr", [NP, 128, SH], BF16)

    em = Em()
    for s in ("pe", "act", "dve", "ldc", "ldta", "ldtb", "ldx", "stln0",
              "stln1", "ldw", "ldr0", "ldr1", "stz0", "stz1", "ldz", "ldxc",
              "stout"):
        em.sem(s)

    with contextlib.ExitStack() as es:
        s_id = es.enter_context(nc.sbuf_tensor("c_id", [128, 128], BF16))
        s_ones = es.enter_context(nc.sbuf_tensor("c_ones", [128, 128], BF16))
        s_b1c = es.enter_context(nc.sbuf_tensor("c_b1c", [128, 32], F32))
        s_gb = es.enter_context(nc.sbuf_tensor("c_gb", [128, 24], F32))
        s_cb = es.enter_context(nc.sbuf_tensor("c_cb", [128, 8], F32))
        s_betabf = es.enter_context(nc.sbuf_tensor("c_bb", [128, 8], BF16))
        s_cos = es.enter_context(nc.sbuf_tensor("cosm", [128, 8192], BF16))
        s_sin = es.enter_context(nc.sbuf_tensor("sinm", [128, 8192], BF16))
        p_all = es.enter_context(nc.psum_tensor("p_all", [128, 4096], F32))
        sems = {}

        def bank(i, w=512):
            return p_all[:, i * 512:i * 512 + w]

        for dst_, src_ in ((s_id, lambda: d_consts[:, 0:128]),
                           (s_ones, lambda: d_consts[:, 128:256]),
                           (s_b1c, lambda: d_b1c[:, :]),
                           (s_gb, lambda: d_gb[:, :]),
                           (s_cb, lambda: d_cb[:, :])):
            em.op("sync", lambda e, d=dst_, s=src_: e.dma_start(
                out=d[:, :], in_=s()), waits=[("ldc", em.cnt["ldc"])],
                inc="ldc", amt=16)
        em.op("gpsimd", lambda e: e.dma_start(out=s_betabf[:, :],
              in_=d_gb[:, 8:16]), waits=[("ldw", em.cnt["ldw"])],
              inc="ldw", amt=16)
        LDC = em.cnt["ldc"]
        PI2 = lambda: s_cb[:, 0:1]
        EPSB = lambda: s_cb[:, 1:2]
        TINY = lambda: s_cb[:, 2:3]
        K1024 = lambda: s_cb[:, 3:4]
        ONE = lambda: s_cb[:, 4:5]


        with contextlib.ExitStack() as ps:
            s_ta = [ps.enter_context(nc.sbuf_tensor(f"ta{i}", [128, 1024], F32))
                    for i in range(2)]
            s_tb = [ps.enter_context(nc.sbuf_tensor(f"tb{i}", [128, 1024], F32))
                    for i in range(2)]
            s_tsum = [ps.enter_context(
                nc.sbuf_tensor(f"ts{i}", [128, 1024], F32)) for i in range(2)]
            buf_free = [0, 0]
            sum_free = [0, 0]
            it = 0
            for da, db_, cosb, dst in ((d_th, d_thn, True, s_cos),
                                       (d_ph, d_phn, False, s_sin)):
                for dt_ in range(NDT):
                    par = it % 2
                    la = em.op("sync", lambda e, a=da, d=dt_, p=par:
                               e.dma_start(out=s_ta[p][:, :],
                                           in_=a[d * 128:(d + 1) * 128, :]),
                               waits=[("dve", buf_free[par]),
                                      ("ldta", em.cnt["ldta"])],
                               inc="ldta", amt=16)
                    lb = em.op("sync", lambda e, a=db_, d=dt_, p=par:
                               e.dma_start(out=s_tb[p][:, :],
                                           in_=a[d * 128:(d + 1) * 128, :]),
                               waits=[("ldtb", em.cnt["ldtb"])],
                               inc="ldtb", amt=16)
                    sm = em.op("vector", lambda e, p=par: e.tensor_tensor(
                        s_tsum[p][:, :], s_ta[p][:, :], s_tb[p][:, :],
                        op=ALU.add),
                        waits=[("ldta", la), ("ldtb", lb),
                               ("act", sum_free[par])],
                        inc="dve")
                    buf_free[par] = sm
                    if cosb:
                        a = em.op("scalar", lambda e, p=par:
                                  e.activation(
                                      out=s_tsum[p][:, :],
                                      in_=s_tsum[p][:, :], func=AF.Sin,
                                      bias=PI2()),
                                  waits=[("dve", sm), ("ldc", LDC)], inc="act")
                        sm = em.op("vector", lambda e, d=dt_, p=par, ds=dst:
                                   e.tensor_scalar(
                                       ds[:, d * 1024:(d + 1) * 1024],
                                       s_tsum[p][:, :], ONE(), None,
                                       op0=ALU.subtract),
                                   waits=[("act", a)], inc="dve")
                        buf_free[par] = sm
                        sum_free[par] = a
                        it += 1
                        continue
                    if True:
                        a = em.op("scalar", lambda e, d=dt_, p=par, ds=dst:
                                  e.activation(
                                      out=ds[:, d * 1024:(d + 1) * 1024],
                                      in_=s_tsum[p][:, :], func=AF.Sin),
                                  waits=[("dve", sm)], inc="act")
                    sum_free[par] = a
                    it += 1

        with contextlib.ExitStack() as ps:
            s_pxt = ps.enter_context(nc.sbuf_tensor("pxt", [128, 8192], F32))
            s_px = [s_pxt[:, d * 1024:(d + 1) * 1024] for d in range(NDT)]
            s_pbf = [ps.enter_context(
                nc.sbuf_tensor(f"pb{i}", [128, 1024], BF16)) for i in range(2)]
            s_psq = [ps.enter_context(
                nc.sbuf_tensor(f"pq{i}", [128, 1024], BF16)) for i in range(2)]
            s_mus = ps.enter_context(nc.sbuf_tensor("mus", [128, 1024], F32))
            s_var = ps.enter_context(nc.sbuf_tensor("var", [128, 1024], F32))
            s_rb = ps.enter_context(nc.sbuf_tensor("rb", [128, 1024], F32))
            s_t1 = ps.enter_context(nc.sbuf_tensor("t1", [128, 1024], F32))
            s_l1c = [ps.enter_context(
                nc.sbuf_tensor(f"l1{i}", [128, 1024], BF16)) for i in range(2)]
            px_free = 0
            pbf_free = [0, 0]
            sq_free = [0, 0]
            psq_free = [0, 0]
            l1_free = [0, 0]
            chain_free = 0
            mstat_done = 0
            for ch in range(8 if TRUNC >= 2 else 0):
                c0 = ch * 1024
                bar = ([("dve", em.cnt["dve"]), ("act", em.cnt["act"])]
                       if ch == 0 else [])
                ldp = em.op("sync", lambda e, c=c0: e.dma_start(
                    out=s_pxt[:, :].rearrange("p (d t) -> p d t", d=NDT),
                    in_=d_xT[:, c:c + 1024]
                    .rearrange("(d p) t -> p d t", p=128)),
                    waits=[("dve", px_free), ("ldx", em.cnt["ldx"])] + bar,
                    inc="ldx", amt=16)
                lds = [ldp] * NDT
                for d in range(NDT):
                    cv = em.op("vector", lambda e, d=d: e.tensor_copy(
                        s_pbf[d % 2][:, :], s_px[d][:, :]),
                        waits=[("ldx", lds[d]), ("pe", pbf_free[d % 2]),
                               ("act", sq_free[d % 2])],
                        inc="dve")
                    sq = em.op("scalar", lambda e, d=d: e.activation(
                        out=s_psq[d % 2][:, :], in_=s_pbf[d % 2][:, :],
                        func=AF.Square),
                        waits=[("dve", cv), ("pe", psq_free[d % 2])],
                        inc="act")
                    sq_free[d % 2] = sq
                    mlast = None
                    for h in range(2):
                        mlast = em.op("tensor", lambda e, d=d, h=h: e.matmul(
                            bank(h), s_ones[:, :],
                            s_pbf[d % 2][:, h * 512:(h + 1) * 512],
                            start=(d == 0), stop=(d == NDT - 1)),
                            waits=([("dve", cv), ("ldc", LDC),
                                    ("dve", chain_free)] if h == 0 else []),
                            inc="pe")
                    pbf_free[d % 2] = mlast
                    for h in range(2):
                        mlast = em.op("tensor", lambda e, d=d, h=h: e.matmul(
                            bank(2 + h), s_ones[:, :],
                            s_psq[d % 2][:, h * 512:(h + 1) * 512],
                            start=(d == 0), stop=(d == NDT - 1)),
                            waits=([("act", sq)] if h == 0 else []),
                            inc="pe")
                    psq_free[d % 2] = mlast
                    mstat_done = mlast
                cmu = em.op("vector", lambda e: e.tensor_copy(
                    s_mus[:, :], p_all[:, 0:1024]),
                    waits=[("pe", mstat_done)], inc="dve")
                em.op("vector", lambda e: e.tensor_tensor(
                    s_var[:, :], s_mus[:, :], s_mus[:, :], op=ALU.mult),
                    inc="dve")
                cvar = em.op("vector", lambda e: e.tensor_tensor(
                    s_var[:, :], p_all[:, 1024:2048], s_var[:, :],
                    op=ALU.subtract), inc="dve")
                chain_free = cvar
                em.op("scalar", lambda e: e.activation(
                    out=s_rb[:, :], in_=s_var[:, :], func=AF.Ln, bias=EPSB()),
                    waits=[("dve", cvar)], inc="act")
                arb = em.op("scalar", lambda e: e.activation(
                    out=s_rb[:, :], in_=s_rb[:, :], func=AF.Exp, scale=-0.5),
                    inc="act")
                for d in range(NDT):
                    t = em.op("vector", lambda e, d=d: e.tensor_tensor(
                        s_t1[:, :], s_px[d][:, :], s_mus[:, :],
                        op=ALU.subtract),
                        waits=[("act", arb)], inc="dve")
                    if d == NDT - 1:
                        px_free = t
                    lc = em.op("vector", lambda e, d=d: e.tensor_tensor(
                        s_l1c[d % 2][:, :], s_t1[:, :], s_rb[:, :],
                        op=ALU.mult),
                        waits=[(f"stln{d % 2}", l1_free[d % 2])],
                        inc="dve")
                    st = em.op("gpsimd", lambda e, d=d, c=c0: e.dma_start(
                        out=d_ln1[d, :, c:c + 1024], in_=s_l1c[d % 2][:, :]),
                        waits=[("dve", lc),
                               (f"stln{d % 2}", em.cnt[f"stln{d % 2}"])],
                        inc=f"stln{d % 2}", amt=16)
                    l1_free[d % 2] = st
        LN1_ST0 = em.cnt["stln0"]
        LN1_ST1 = em.cnt["stln1"]

        with contextlib.ExitStack() as ps:
            s_kT = [ps.enter_context(nc.sbuf_tensor(f"kT{i}", [128, S], BF16))
                    for i in range(2)]
            s_vsq = [ps.enter_context(nc.sbuf_tensor(f"vs{i}", [128, S], BF16))
                     for i in range(2)]
            s_qT = [ps.enter_context(nc.sbuf_tensor(f"qT{i}", [128, SH], BF16))
                    for i in range(2)]
            s_uT = [ps.enter_context(nc.sbuf_tensor(f"uT{i}", [128, SH], BF16))
                    for i in range(2)]
            s_vfm = ps.enter_context(nc.sbuf_tensor("vfm", [128, S], BF16))
            s_rhs = [ps.enter_context(
                nc.sbuf_tensor(f"rh{i}", [128, 4096], BF16)) for i in range(2)]
            s_w1 = ps.enter_context(nc.sbuf_tensor("w1b", [128, 4096], BF16))
            s_att = [ps.enter_context(
                nc.sbuf_tensor(f"at{i}", [128, 1024], BF16)) for i in range(2)]
            s_z = [ps.enter_context(
                nc.sbuf_tensor(f"z{i}", [128, 512], BF16)) for i in range(2)]

            w1_free = 0
            rhs_free = [0, 0]
            unit_free = [0] * 4
            vt_free = 0
            attn_read_done = [0, 0]
            z_free = [0, 0]
            zm_val = 0
            proj_done_act = 0
            proj_done_dve = 0

            s_pstg = [ps.enter_context(
                nc.sbuf_tensor(f"pst{i}", [128, 512], F32)) for i in range(4)]
            NPAIR = 0 if TRUNC < 3 else (NP if TRUNC >= 5 else 1)
            proj_done = {}
            bank7 = [0]
            pstg_free = [0, 0, 0, 0]

            def proj_steps(pair):
                nonlocal w1_free, proj_done
                par = pair % 2
                bar = ([("dve", em.cnt["dve"]), ("act", em.cnt["act"]),
                        ("pe", em.cnt["pe"])] if pair == 0 else [])
                lw = em.op("gpsimd", lambda e, p=pair: e.dma_start(
                    out=s_w1[:, :].rearrange("p (d c) -> p d c", d=NDT),
                    in_=d_w1p[p].rearrange("(d p) c -> p d c", p=128)),
                    waits=[("pe", w1_free), ("ldc", LDC),
                           ("ldw", em.cnt["ldw"])] + bar,
                    inc="ldw", amt=16)
                yield
                g = 0
                kinds_dst = {"k": (3, s_kT[par]), "v": (1, s_vfm),
                             "q": (2, s_qT[par]), "u": (0, s_uT[par])}
                pend_vt = None
                for j in range(17):
                    if j < 16:
                        jp = j % 2
                        lr = em.op("sync", lambda e, j=j, p=jp: e.dma_start(
                            out=s_rhs[p][:, :].rearrange(
                                "p (d t) -> p d t", d=NDT),
                            in_=d_ln1[:, :, j * 512:(j + 1) * 512]
                            .rearrange("d p t -> p d t")),
                            waits=[("pe", rhs_free[jp]), ("stln0", LN1_ST0),
                                   ("stln1", LN1_ST1),
                                   (f"ldr{jp}", em.cnt[f"ldr{jp}"])],
                            inc=f"ldr{jp}", amt=16)
                        yield
                        kinds = (["k", "v", "q", "u"] if j < 8
                                 else ["k", "v"])
                        mm_last = None
                        for kind in kinds:
                            ki, dst = kinds_dst[kind]
                            gp = g % 4
                            for d in range(NDT):
                                w = []
                                if d == 0:
                                    w = [(f"ldr{jp}", lr), ("ldw", lw),
                                         ("dve", bank7[0]),
                                         ("pe", attn_read_done[par])]
                                mm_last = em.op("tensor",
                                    lambda e, d=d, k=ki, p=jp: e.matmul(
                                        bank(7),
                                        s_w1[:, d * 512 + k * 128:
                                             d * 512 + k * 128 + 128],
                                        s_rhs[p][:, d * 512:(d + 1) * 512],
                                        start=(d == 0), stop=(d == NDT - 1)),
                                    waits=w, inc="pe")
                            yield
                            cp = em.op("vector", lambda e, gp=gp:
                                       e.tensor_copy(s_pstg[gp][:, :],
                                                     bank(7)),
                                       waits=[("pe", mm_last),
                                              ("act", pstg_free[gp])],
                                       inc="dve")
                            bank7[0] = cp
                            yield
                            bcol = ki * 8 + pair
                            ev = em.op("scalar", lambda e, gp=gp, j=j,
                                       b=bcol, ds=dst: e.activation(
                                           out=ds[:, j * 512:(j + 1) * 512],
                                           in_=s_pstg[gp][:, :],
                                           func=AF.Silu,
                                           bias=s_b1c[:, b:b + 1]),
                                       waits=[("dve", cp),
                                              ("pe", attn_read_done[par])],
                                       inc="act")
                            pstg_free[gp] = ev
                            if kind == "v":
                                vdef = (j, ev)
                            g += 1
                            yield
                        rhs_free[jp] = mm_last
                    if pend_vt is not None:
                        vj, vev = pend_vt
                        tlast = None
                        for c in range(4):
                            tlast = em.op("tensor",
                                lambda e, vj=vj, c=c: e.matmul(
                                    p_all[:, 3584 + c * 128:
                                          3584 + (c + 1) * 128],
                                    s_vfm[:, vj * 512 + c * 128:
                                          vj * 512 + (c + 1) * 128],
                                    s_id[:, :], start=True, stop=True),
                                waits=([("act", vev), ("dve", bank7[0]),
                                        ("ldc", LDC)] if c == 0 else []),
                                inc="pe")
                        yield
                        bank7[0] = em.op("vector", lambda e, vj=vj, p=par:
                                         e.tensor_copy(
                                             s_vsq[p][:, vj * 512:
                                                      (vj + 1) * 512],
                                             bank(7)),
                                         waits=[("pe", tlast),
                                                ("pe", attn_read_done[par])],
                                         inc="dve")
                        yield
                    if j < 16:
                        pend_vt = vdef
                proj_done[pair] = (em.cnt["act"], em.cnt["dve"])
                w1_free = em.cnt["pe"]

            if NPAIR > 0:
                for _ in proj_steps(0):
                    pass
            for pair in range(NPAIR):
                par = pair % 2
                bg = (proj_steps(pair + 1) if pair + 1 < NPAIR else iter(()))
                bg_alive = [True]

                def pop_bg(n):
                    for _ in range(n):
                        if not bg_alive[0]:
                            return
                        try:
                            next(bg)
                        except StopIteration:
                            bg_alive[0] = False
                            return

                if TRUNC < 4:
                    pop_bg(1 << 30)
                    continue
                NU = 512
                qk_val = [0] * NU
                av_val = [0] * NU
                silu_val = [0] * NU
                pd = proj_done[pair]
                for i in range(NU + 2):
                    if i < NU:
                        sl = i % 3
                        qp, kc = i // 64, i % 64
                        w = [("act", silu_val[i - 3] if i >= 3 else 0)]
                        if i == 0:
                            w += [("act", pd[0]), ("dve", pd[1])]
                        em.op("tensor", lambda e, sl=sl, q=qp, k=kc, p=par:
                              e.matmul(
                                  p_all[:, 512 + sl * 1024:
                                        512 + sl * 1024 + 512],
                                  s_kT[p][0:64, k * 128:(k + 1) * 128],
                                  s_qT[p][0:64, q * 512:(q + 1) * 512],
                                  start=True, stop=True,
                                  tile_position=(0, 0)),
                              waits=w)
                        qk_val[i] = em.op("tensor",
                              lambda e, sl=sl, q=qp, k=kc, p=par: e.matmul(
                                  p_all[:, 512 + sl * 1024 + 512:
                                        512 + sl * 1024 + 1024],
                                  s_kT[p][64:128, k * 128:(k + 1) * 128],
                                  s_qT[p][64:128, q * 512:(q + 1) * 512],
                                  start=True, stop=True,
                                  tile_position=(64, 0)),
                              inc="pe")
                    u = i - 2
                    if u >= 0:
                        w = [("pe", qk_val[u]),
                             ("pe", av_val[u - 2] if u >= 2 else 0)]
                        sl = u % 3
                        silu_val[u] = em.op("scalar",
                              lambda e, sl=sl, ab=u % 2: e.activation(
                                  out=s_att[ab][:, :],
                                  in_=p_all[:, 512 + sl * 1024:
                                            512 + sl * 1024 + 1024],
                                  func=AF.Silu),
                              waits=w, inc="act")
                        qp, kc = u // 64, u % 64
                        w = [("act", silu_val[u])]
                        if kc == 0:
                            w.append(("dve", zm_val))
                        em.op("tensor", lambda e, k=kc, ab=u % 2, p=par:
                              e.matmul(
                                  p_all[0:64, 0:512],
                                  s_vsq[p][:, k * 128:k * 128 + 64],
                                  s_att[ab][:, 0:512],
                                  start=(k == 0), stop=(k == 63),
                                  tile_position=(0, 0)),
                              waits=w)
                        av_val[u] = em.op("tensor",
                              lambda e, k=kc, ab=u % 2, p=par: e.matmul(
                                  p_all[64:128, 0:512],
                                  s_vsq[p][:, k * 128 + 64:k * 128 + 128],
                                  s_att[ab][:, 512:1024],
                                  start=(k == 0), stop=(k == 63),
                                  tile_position=(0, 64)),
                              inc="pe")
                        if kc == 63:
                            zp = (qp % 2)
                            zm_val = em.op("vector",
                                  lambda e, q=qp, p=par, zp=zp:
                                  e.tensor_tensor(
                                      s_z[zp][:, :], p_all[:, 0:512],
                                      s_uT[p][:, q * 512:(q + 1) * 512],
                                      op=ALU.mult),
                                  waits=[("pe", av_val[u]),
                                         (f"stz{zp}", z_free[zp])],
                                  inc="dve")
                            z_free[zp] = em.op("gpsimd",
                                  lambda e, q=qp, pr=pair, zp=zp:
                                  e.dma_start(
                                      out=d_zscr[pr, :, q * 512:(q + 1) * 512],
                                      in_=s_z[zp][:, :]),
                                  waits=[("dve", zm_val),
                                         (f"stz{zp}", em.cnt[f"stz{zp}"])],
                                  inc=f"stz{zp}", amt=16)
                    if i % 3 == 2:
                        pop_bg(1)
                pop_bg(1 << 30)
                attn_read_done[par] = em.cnt["pe"]
        Z_ALL0 = em.cnt["stz0"]
        Z_ALL1 = em.cnt["stz1"]


        if TRUNC >= 6:
            with contextlib.ExitStack() as ts:
                s_w2 = ts.enter_context(nc.sbuf_tensor("w2b", [128, 8192], BF16))
                s_c2 = ts.enter_context(nc.sbuf_tensor("c2", [128, 8], F32))
                s_zc = [ts.enter_context(
                    nc.sbuf_tensor("zc0", [128, 4096], BF16))] * 2
                s_xc = [ts.enter_context(
                    nc.sbuf_tensor("xc0", [128, 4096], F32))] * 2
                s_zsq = ts.enter_context(nc.sbuf_tensor("zsq", [128, 4096], BF16))
                s_zln = ts.enter_context(nc.sbuf_tensor("zln", [128, 4096], BF16))
                s_y = ts.enter_context(nc.sbuf_tensor("yb", [128, 4096], BF16))
                s_ysq = ts.enter_context(nc.sbuf_tensor("ysq", [128, 4096], BF16))
                s_x1 = ts.enter_context(nc.sbuf_tensor("x1", [128, 4096], F32))
                s_x1b = ts.enter_context(nc.sbuf_tensor("x1b", [128, 4096], BF16))
                s_h = ts.enter_context(nc.sbuf_tensor("hb", [128, 4096], BF16))
                s_mag = ts.enter_context(nc.sbuf_tensor("mag", [128, 4096], BF16))
                s_m2 = ts.enter_context(nc.sbuf_tensor("m2", [128, 4096], BF16))
                s_sqa = ts.enter_context(nc.sbuf_tensor("sqa", [128, 512], F32))
                s_sqb = ts.enter_context(nc.sbuf_tensor("sqb", [128, 512], F32))
                s_x2 = ts.enter_context(nc.sbuf_tensor("x2", [128, 4096], F32))
                s_mu = [ts.enter_context(
                    nc.sbuf_tensor(f"mu{i}", [128, 512], F32)) for i in range(2)]
                s_vr = [ts.enter_context(
                    nc.sbuf_tensor(f"vr{i}", [128, 512], F32)) for i in range(2)]
                s_r = [ts.enter_context(
                    nc.sbuf_tensor(f"r{i}", [128, 512], F32)) for i in range(4)]
                s_mr = [ts.enter_context(
                    nc.sbuf_tensor(f"mr{i}", [128, 512], F32)) for i in range(4)]

                lw2 = em.op("gpsimd", lambda e: e.dma_start(
                    out=s_w2[:, :].rearrange("p (d c) -> p d c", d=NDT),
                    in_=d_w2.rearrange("(d p) c -> p d c", p=128)),
                    waits=[("ldc", LDC), ("ldw", em.cnt["ldw"]),
                           ("pe", em.cnt["pe"]), ("act", em.cnt["act"]),
                           ("dve", em.cnt["dve"])],
                    inc="ldw", amt=16)
                c2mm = None
                for f in range(8):
                    for d in range(NDT):
                        c2mm = em.op("tensor", lambda e, f=f, d=d: e.matmul(
                            p_all[:, 3072 + f:3072 + f + 1],
                            s_w2[:, d * 1024 + f * 128:d * 1024 + (f + 1) * 128],
                            s_betabf[:, d:d + 1],
                            start=(d == 0), stop=(d == NDT - 1)),
                            waits=([("ldw", lw2), ("ldc", LDC),
                                    ("act", em.cnt["act"]),
                                    ("dve", em.cnt["dve"])]
                                   if f == 0 and d == 0 else []),
                            inc="pe")
                c2dve = em.op("vector", lambda e: e.tensor_tensor(
                    s_c2[:, :], p_all[:, 3072:3080], s_gb[:, 16:24], op=ALU.add),
                    waits=[("pe", c2mm)], inc="dve")
                gma = None
                for d in range(NDT):
                    gma = em.op("vector", lambda e, d=d: e.tensor_scalar(
                        s_w2[:, d * 1024:(d + 1) * 1024],
                        s_w2[:, d * 1024:(d + 1) * 1024],
                        s_gb[:, d:d + 1], None, op0=ALU.mult),
                        waits=[("pe", c2mm)], inc="dve")
                W2RDY = gma if KGM else em.cnt["dve"]

                zc_free = [em.cnt["pe"], em.cnt["dve"]]
                xc_free = [em.cnt["dve"], 0]
                zsq_free = 0
                stv = 0

                def rstd_chain(mwait, cpar, rix):
                    em.op("vector", lambda e: e.tensor_copy(
                        s_mu[cpar][:, :], bank(0)),
                        waits=[("pe", mwait)], inc="dve")
                    em.op("vector", lambda e: e.tensor_tensor(
                        s_vr[cpar][:, :], s_mu[cpar][:, :], s_mu[cpar][:, :],
                        op=ALU.mult), inc="dve")
                    vfree = em.op("vector", lambda e: e.tensor_tensor(
                        s_vr[cpar][:, :], bank(1), s_vr[cpar][:, :],
                        op=ALU.subtract), inc="dve")
                    em.op("scalar", lambda e: e.activation(
                        out=s_r[rix][:, :], in_=s_vr[cpar][:, :], func=AF.Ln,
                        bias=EPSB()), waits=[("dve", vfree)], inc="act")
                    ar = em.op("scalar", lambda e: e.activation(
                        out=s_r[rix][:, :], in_=s_r[rix][:, :], func=AF.Exp,
                        scale=-0.5), inc="act")
                    mrv = em.op("vector", lambda e: e.tensor_tensor(
                        s_mr[rix][:, :], s_mu[cpar][:, :], s_r[rix][:, :],
                        op=ALU.mult), waits=[("act", ar)], inc="dve")
                    return vfree, mrv

                stats_free = c2dve
                for n in range(TAILN):
                    c0 = n * 512
                    np_ = n % 2
                    lz = em.op("sync", lambda e, c=c0, p=np_: e.dma_start(
                        out=s_zc[p][:, :].rearrange("p (d t) -> p d t", d=NP),
                        in_=d_zscr[:, :, c:c + 512].rearrange("d p t -> p d t")),
                        waits=[("stz0", Z_ALL0), ("stz1", Z_ALL1),
                               ("pe", zc_free[0]), ("dve", zc_free[1]),
                               ("ldz", em.cnt["ldz"])],
                        inc="ldz", amt=16)
                    lx = em.op("sync", lambda e, c=c0, p=np_: e.dma_start(
                        out=s_xc[p][:, :].rearrange("p (d t) -> p d t", d=NDT),
                        in_=d_xT[:, c:c + 512]
                        .rearrange("(d p) t -> p d t", p=128)),
                        waits=[("dve", xc_free[0]), ("ldxc", em.cnt["ldxc"])],
                        inc="ldxc", amt=16)
                    if KSTEP <= 1: continue
                    zq = em.op("scalar", lambda e, p=np_: e.activation(
                        out=s_zsq[:, :], in_=s_zc[p][:, :], func=AF.Square),
                        waits=[("ldz", lz), ("dve", zsq_free)], inc="act")
                    for d in range(NDT):
                        em.op("tensor", lambda e, d=d, p=np_: e.matmul(
                            bank(0), s_ones[:, :],
                            s_zc[p][:, d * 512:(d + 1) * 512],
                            start=(d == 0), stop=(d == NDT - 1)),
                            waits=([("ldz", lz), ("dve", stats_free)]
                                   if d == 0 else []))
                    st2 = None
                    for d in range(NDT):
                        st2 = em.op("tensor", lambda e, d=d: e.matmul(
                            bank(1), s_ones[:, :],
                            s_zsq[:, d * 512:(d + 1) * 512],
                            start=(d == 0), stop=(d == NDT - 1)),
                            waits=([("act", zq)] if d == 0 else []), inc="pe")
                    stats_free, mr2 = rstd_chain(st2, 0, 0)
                    if KSTEP <= 2: continue
                    zl = None
                    for d in range(NDT):
                        em.op("vector", lambda e, d=d, p=np_: e.tensor_tensor(
                            s_zsq[:, d * 512:(d + 1) * 512],
                            s_zc[p][:, d * 512:(d + 1) * 512], s_r[0][:, :],
                            op=ALU.mult), waits=[("dve", mr2)], inc="dve")
                        zl = em.op("vector", lambda e, d=d: e.tensor_tensor(
                            s_zln[:, d * 512:(d + 1) * 512],
                            s_zsq[:, d * 512:(d + 1) * 512], s_mr[0][:, :],
                            op=ALU.subtract), inc="dve")
                    if KSTEP <= 3: continue
                    ysq_l = None
                    ypar_dve = [0, 0]
                    for f in range(8):
                        mm = None
                        for d in range(NDT):
                            w = []
                            if d == 0:
                                w = [("dve", zl), ("dve", W2RDY),
                                     ("dve", ypar_dve[f % 2])]
                            mm = em.op("tensor", lambda e, f=f, d=d: e.matmul(
                                bank(2 + f % 2),
                                s_w2[:, d * 1024 + f * 128:
                                     d * 1024 + (f + 1) * 128],
                                s_zln[:, d * 512:(d + 1) * 512],
                                start=(d == 0), stop=(d == NDT - 1)),
                                waits=w, inc="pe")
                            if f == 0 and d == NDT - 1:
                                zc_free[0] = mm
                                zc_free[1] = zl
                        yev = em.op("vector", lambda e, f=f: e.tensor_tensor(
                            s_y[:, f * 512:(f + 1) * 512],
                            bank(2 + f % 2),
                            s_c2[:, f:f + 1].broadcast_to([128, 512]),
                            op=ALU.add),
                            waits=[("pe", mm), ("dve", c2dve)], inc="dve")
                        ysq_l = em.op("scalar", lambda e, f=f: e.activation(
                            out=s_ysq[:, f * 512:(f + 1) * 512],
                            in_=s_y[:, f * 512:(f + 1) * 512],
                            func=AF.Square),
                            waits=[("dve", yev)], inc="act")
                        ypar_dve[f % 2] = yev
                    if KSTEP <= 4: continue
                    for f in range(8):
                        em.op("tensor", lambda e, f=f: e.matmul(
                            bank(0), s_ones[:, :], s_y[:, f * 512:(f + 1) * 512],
                            start=(f == 0), stop=(f == 7)),
                            waits=([("dve", yev), ("dve", stats_free)]
                                   if f == 0 else []))
                    sty = None
                    for f in range(8):
                        sty = em.op("tensor", lambda e, f=f: e.matmul(
                            bank(1), s_ones[:, :], s_ysq[:, f * 512:(f + 1) * 512],
                            start=(f == 0), stop=(f == 7)),
                            waits=([("act", ysq_l)] if f == 0 else []), inc="pe")
                    stats_free, mry = rstd_chain(sty, 1, 1)
                    if KSTEP <= 5: continue
                    x1b_l = None
                    x1sq_l = None
                    for f in range(8):
                        em.op("vector", lambda e, f=f: e.tensor_tensor(
                            s_zsq[:, f * 512:(f + 1) * 512],
                            s_y[:, f * 512:(f + 1) * 512], s_r[1][:, :],
                            op=ALU.mult),
                            waits=[("dve", mry), ("ldxc", lx)], inc="dve")
                        em.op("vector", lambda e, f=f: e.tensor_tensor(
                            s_zsq[:, f * 512:(f + 1) * 512],
                            s_zsq[:, f * 512:(f + 1) * 512], s_mr[1][:, :],
                            op=ALU.subtract), inc="dve")
                        x1v = em.op("vector", lambda e, f=f, p=np_:
                                    e.tensor_tensor(
                                        s_x1[:, f * 512:(f + 1) * 512],
                                        s_zsq[:, f * 512:(f + 1) * 512],
                                        s_xc[p][:, f * 512:(f + 1) * 512],
                                        op=ALU.add), inc="dve")
                        x1b_l = em.op("scalar", lambda e, f=f: e.activation(
                            out=s_x1b[:, f * 512:(f + 1) * 512],
                            in_=s_x1[:, f * 512:(f + 1) * 512], func=AF.Copy),
                            waits=[("dve", x1v)], inc="act")
                        x1sq_l = em.op("scalar", lambda e, f=f: e.activation(
                            out=s_ysq[:, f * 512:(f + 1) * 512],
                            in_=s_x1[:, f * 512:(f + 1) * 512], func=AF.Square),
                            inc="act")
                    xc_free[0] = em.cnt["dve"]
                    if KSTEP <= 6: continue
                    for f in range(8):
                        em.op("tensor", lambda e, f=f: e.matmul(
                            bank(0), s_ones[:, :], s_x1b[:, f * 512:(f + 1) * 512],
                            start=(f == 0), stop=(f == 7)),
                            waits=([("act", x1b_l), ("dve", stats_free)]
                                   if f == 0 else []))
                    st3 = None
                    for f in range(8):
                        st3 = em.op("tensor", lambda e, f=f: e.matmul(
                            bank(1), s_ones[:, :], s_ysq[:, f * 512:(f + 1) * 512],
                            start=(f == 0), stop=(f == 7)),
                            waits=([("act", x1sq_l)] if f == 0 else []), inc="pe")
                    stats_free, mr3 = rstd_chain(st3, 0, 2)
                    if KSTEP <= 7: continue
                    hl = None
                    for f in range(8):
                        em.op("vector", lambda e, f=f: e.tensor_tensor(
                            s_zsq[:, f * 512:(f + 1) * 512],
                            s_x1b[:, f * 512:(f + 1) * 512], s_r[2][:, :],
                            op=ALU.mult), waits=[("dve", mr3)], inc="dve")
                        hl = em.op("vector", lambda e, f=f: e.tensor_tensor(
                            s_h[:, f * 512:(f + 1) * 512],
                            s_zsq[:, f * 512:(f + 1) * 512], s_mr[2][:, :],
                            op=ALU.subtract), inc="dve")
                    if KSTEP <= 8: continue
                    ri_free = [0, 0, 0, 0]
                    m2_l = 0
                    mag_l = None
                    for fo in range(8):
                        rb_ = 4 + (fo % 2) * 2
                        ib_ = 5 + (fo % 2) * 2
                        mmr = None
                        for d in range(NDT):
                            w = []
                            if d == 0:
                                w = [("dve", hl), ("act", ri_free[2 * (fo % 2)])]
                            mmr = em.op("tensor", lambda e, d=d, f=fo, b=rb_:
                                        e.matmul(
                                            bank(b),
                                            s_cos[:, d * 1024 + f * 128:
                                                  d * 1024 + (f + 1) * 128],
                                            s_h[:, d * 512:(d + 1) * 512],
                                            start=(d == 0), stop=(d == NDT - 1)),
                                        waits=w, inc="pe")
                        mmi = None
                        for d in range(NDT):
                            w = []
                            if d == 0:
                                w = [("act", ri_free[2 * (fo % 2) + 1])]
                            mmi = em.op("tensor", lambda e, d=d, f=fo, b=ib_:
                                        e.matmul(
                                            bank(b),
                                            s_sin[:, d * 1024 + f * 128:
                                                  d * 1024 + (f + 1) * 128],
                                            s_h[:, d * 512:(d + 1) * 512],
                                            start=(d == 0), stop=(d == NDT - 1)),
                                        waits=w, inc="pe")
                        sa = em.op("scalar", lambda e, b=rb_: e.activation(
                            out=s_sqa[:, :], in_=bank(b), func=AF.Square),
                            waits=[("pe", mmr), ("dve", m2_l)], inc="act")
                        ri_free[2 * (fo % 2)] = sa
                        sb_ = em.op("scalar", lambda e, b=ib_: e.activation(
                            out=s_sqb[:, :], in_=bank(b), func=AF.Square),
                            waits=[("pe", mmi)], inc="act")
                        ri_free[2 * (fo % 2) + 1] = sb_
                        m2_l = em.op("vector", lambda e, f=fo: e.tensor_tensor(
                            s_m2[:, f * 512:(f + 1) * 512], s_sqa[:, :],
                            s_sqb[:, :], op=ALU.add),
                            waits=[("act", sb_)], inc="dve")
                        em.op("scalar", lambda e, f=fo: e.activation(
                            out=s_mag[:, f * 512:(f + 1) * 512],
                            in_=s_m2[:, f * 512:(f + 1) * 512], func=AF.Ln,
                            bias=TINY()), waits=[("dve", m2_l)], inc="act")
                        mag_l = em.op("scalar", lambda e, f=fo: e.activation(
                            out=s_mag[:, f * 512:(f + 1) * 512],
                            in_=s_mag[:, f * 512:(f + 1) * 512], func=AF.Exp,
                            scale=0.5), inc="act")
                    if KSTEP <= 9: continue
                    for f in range(8):
                        em.op("tensor", lambda e, f=f: e.matmul(
                            bank(0), s_ones[:, :], s_mag[:, f * 512:(f + 1) * 512],
                            start=(f == 0), stop=(f == 7)),
                            waits=([("act", mag_l), ("dve", stats_free)]
                                   if f == 0 else []))
                    st4 = None
                    for f in range(8):
                        st4 = em.op("tensor", lambda e, f=f: e.matmul(
                            bank(1), s_ones[:, :], s_m2[:, f * 512:(f + 1) * 512],
                            start=(f == 0), stop=(f == 7)),
                            waits=([("dve", m2_l)] if f == 0 else []), inc="pe")
                    stats_free, mr4 = rstd_chain(st4, 1, 3)
                    if KSTEP <= 10: continue
                    prev_st = stv
                    xl = None
                    for f in range(8):
                        em.op("vector", lambda e, f=f: e.tensor_tensor(
                            s_zsq[:, f * 512:(f + 1) * 512],
                            s_mag[:, f * 512:(f + 1) * 512], s_r[3][:, :],
                            op=ALU.mult),
                            waits=[("dve", mr4), ("stout", prev_st)], inc="dve")
                        em.op("vector", lambda e, f=f: e.tensor_tensor(
                            s_zsq[:, f * 512:(f + 1) * 512],
                            s_zsq[:, f * 512:(f + 1) * 512], s_mr[3][:, :],
                            op=ALU.subtract), inc="dve")
                        xl = em.op("vector", lambda e, f=f: e.tensor_tensor(
                            s_x2[:, f * 512:(f + 1) * 512],
                            s_zsq[:, f * 512:(f + 1) * 512],
                            s_x1[:, f * 512:(f + 1) * 512], op=ALU.add),
                            inc="dve")
                    zsq_free = xl
                    stv = em.op("gpsimd", lambda e, c=c0: e.dma_start(
                        out=d_out[:, c:c + 512].rearrange("(f p) t -> p f t",
                                                          p=128),
                        in_=s_x2[:, :].rearrange("p (f t) -> p f t", f=NDT)),
                        waits=[("dve", xl), ("stout", em.cnt["stout"])],
                        inc="stout", amt=16)
                em.op("gpsimd", lambda e: e.nop(),
                      waits=[("stout", em.cnt["stout"])])
            if int(_os.environ.get("KEXTRA2", "0")):
                em.op("tensor", lambda e: e.matmul(
                    bank(2), s_cos[:, 0:128], s_cos[:, 0:512],
                    start=True, stop=True), inc="pe")
            if TAILN < 8:
                waits = [(s_, em.cnt[s_]) for s_ in em.cnt]
                _mk = _os.environ.get("KMARK", "cos")
                msrc = {"cos": s_cos[:, 0:4096], "zln": s_zln[:, :],
                        "zc": s_zc[0][:, :], "w2": s_w2[:, 0:4096],
                        "r0": s_r[0][:, :], "mr0": s_mr[0][:, :]}[_mk]
                em.op("gpsimd", lambda e, m=msrc: e.dma_start(
                    out=d_out[0:128, 0:m.shape[-1]], in_=m),
                    waits=waits, inc="stout", amt=16)
                em.op("gpsimd", lambda e: e.nop(),
                      waits=[("stout", em.cnt["stout"])])
        else:
            waits = [(s_, em.cnt[s_]) for s_ in em.cnt]
            em.op("gpsimd", lambda e: e.dma_start(
                out=d_out[0:128, :], in_=s_cos[:, 0:4096]),
                waits=waits, inc="stout", amt=16)
            em.op("gpsimd", lambda e: e.nop(),
                  waits=[("stout", em.cnt["stout"])])

        for name in em.cnt:
            sems[name] = es.enter_context(nc.semaphore(name))
        with nc.Block() as block:
            def mk(eng_name):
                def body(eng):
                    for waits, fn, inc, amt in em.ops[eng_name]:
                        for sname, val in waits:
                            eng.wait_ge(sems[sname], val)
                        r = fn(eng)
                        if inc is not None:
                            assert r is not None, f"{eng_name} inc={inc}"
                            r.then_inc(sems[inc], amt)
                return body
            block.sync(mk("sync"))
            block.gpsimd(mk("gpsimd"))
            block.tensor(mk("tensor"))
            block.scalar(mk("scalar"))
            block.vector(mk("vector"))

    return nc


def kernel(L0_attn_x, L0_attn_W1, L0_attn_b1, L0_attn_W2, L0_attn_b2,
           L0_attn_gamma, L0_attn_beta, L0_ffn_theta, L0_ffn_phi,
           L0_ffn_theta_noise, L0_ffn_phi_noise):
    x = np.asarray(L0_attn_x, np.float32)
    W1 = np.asarray(L0_attn_W1, np.float32)
    b1 = np.asarray(L0_attn_b1, np.float32)
    W2 = np.ascontiguousarray(np.asarray(L0_attn_W2, np.float32))
    b2 = np.asarray(L0_attn_b2, np.float32)
    gamma = np.asarray(L0_attn_gamma, np.float32)
    beta = np.asarray(L0_attn_beta, np.float32)
    th = np.ascontiguousarray(np.asarray(L0_ffn_theta, np.float32))
    thn = np.ascontiguousarray(np.asarray(L0_ffn_theta_noise, np.float32))
    ph = np.ascontiguousarray(np.asarray(L0_ffn_phi, np.float32))
    phn = np.ascontiguousarray(np.asarray(L0_ffn_phi_noise, np.float32))

    xT = np.ascontiguousarray(x.transpose(0, 2, 1))

    w1p = np.empty((NP, D, 512), np.float32)
    for p in range(NP):
        for k in range(4):
            w1p[p, :, k * 128:(k + 1) * 128] = \
                W1[:, k * D + p * 128:k * D + (p + 1) * 128]
    b1c = np.ascontiguousarray(
        b1.reshape(4, 8, 128).transpose(2, 0, 1).reshape(128, 32))
    gb = np.ascontiguousarray(np.concatenate(
        [gamma.reshape(8, 128).T, beta.reshape(8, 128).T,
         b2.reshape(8, 128).T], axis=1))
    consts = np.concatenate(
        [np.eye(128, dtype=np.float32) / S,
         np.ones((128, 128), np.float32) / D], axis=1).astype(BF)
    cb = np.tile(np.array([[math.pi / 2, EPS, 1e-30, 1024.0,
                            1.0, 0.0, 0.0, 0.0]], np.float32), (128, 1))

    nc = build_nc()
    in_maps = []
    for c in range(8):
        b, half = c // 2, c % 2
        off = half * SH
        xrot = np.ascontiguousarray(
            np.concatenate([xT[b][:, off:], xT[b][:, :off]], axis=1))
        in_maps.append(dict(
            xT=xrot, w1p=w1p, b1c=b1c, gb=gb, consts=consts, cb=cb,
            w2=W2, th=th, thn=thn, ph=ph, phn=phn))
    res = run_bass_kernel_spmd(nc, in_maps, core_ids=list(range(8)))
    out = np.empty_like(x)
    for c in range(8):
        b, half = c // 2, c % 2
        out[b, half * SH:(half + 1) * SH, :] = res.results[c]["out"].T
    return out


if __name__ == "__main__":
    import reference as _r
    inp = _r.setup_inputs(0)
    want = np.asarray(_r.reference(**inp))
    got = kernel(**{k: np.asarray(v) for k, v in inp.items()})
    err = got - want
    rvr = float((err ** 2).mean() / (want ** 2).mean())
    print("rvr", rvr, "max", float(np.abs(err).max()))


# version 07
# speedup vs baseline: 1.0334x; 1.0304x over previous
import contextlib
import math
import numpy as np
import ml_dtypes
import concourse.bass as bass
import concourse.mybir as mybir
from concourse.bass_utils import run_bass_kernel_spmd

F32 = mybir.dt.float32
BF16 = mybir.dt.bfloat16
AF = mybir.ActivationFunctionType
ALU = mybir.AluOpType
BF = ml_dtypes.bfloat16

D = 1024
S = 8192
SH = 4096
NDT = 8
NP = 8
EPS = 1e-5

ENGINES = ("sync", "gpsimd", "tensor", "scalar", "vector")
SELF_SEM = {"vector": "dve", "scalar": "act"}
import os as _os
TRUNC = int(_os.environ.get("KTRUNC", "6"))
TAILN = int(_os.environ.get("KTAILN", "8"))
KC2 = _os.environ.get("KC2", "1") == "1"
KGM = _os.environ.get("KGM", "1") == "1"
KW2 = _os.environ.get("KW2", "1") == "1"
KSTEP = int(_os.environ.get("KSTEP", "99"))
KYALT = int(_os.environ.get("KYALT", "0"))
KYB = int(_os.environ.get("KYB", "2"))
KYF = int(_os.environ.get("KYF", "8"))
KYSRC = _os.environ.get("KYSRC", "w2")
KYRHS = _os.environ.get("KYRHS", "zln")
KYW = _os.environ.get("KYW", "1") == "1"
KYD = int(_os.environ.get("KYD", "8"))
KYN = int(_os.environ.get("KYN", "512"))


class Em:

    def __init__(self):
        self.ops = {e: [] for e in ENGINES}
        self.cnt = {}
        self.last_wait = {e: {} for e in ENGINES}

    def sem(self, name):
        self.cnt.setdefault(name, 0)

    def op(self, eng, fn, waits=(), inc=None, amt=1):
        waits = list(waits)
        if eng in SELF_SEM:
            assert inc == SELF_SEM[eng], f"{eng} op must inc {SELF_SEM[eng]}"
            waits.append((SELF_SEM[eng], self.cnt[SELF_SEM[eng]]))
        w = []
        for sname, val in waits:
            assert val <= self.cnt[sname], (
                f"deadlock: {eng} waits {sname}>={val} > {self.cnt[sname]}")
            if val <= 0 or self.last_wait[eng].get(sname, -1) >= val:
                continue
            self.last_wait[eng][sname] = val
            w.append((sname, val))
        v = None
        if inc is not None:
            self.cnt[inc] += amt
            v = self.cnt[inc]
        self.ops[eng].append((w, fn, inc, amt))
        return v


def build_nc():
    nc = bass.Bass("TRN2", target_bir_lowering=False)

    d_xT = nc.dram_tensor("xT", [D, S], F32, kind="ExternalInput")
    d_w1p = nc.dram_tensor("w1p", [NP, D, 512], F32, kind="ExternalInput")
    d_b1c = nc.dram_tensor("b1c", [128, 32], F32, kind="ExternalInput")
    d_gb = nc.dram_tensor("gb", [128, 24], F32, kind="ExternalInput")
    d_consts = nc.dram_tensor("consts", [128, 256], BF16, kind="ExternalInput")
    d_cb = nc.dram_tensor("cb", [128, 8], F32, kind="ExternalInput")
    d_w2 = nc.dram_tensor("w2", [D, D], F32, kind="ExternalInput")
    d_th = nc.dram_tensor("th", [D, D], F32, kind="ExternalInput")
    d_thn = nc.dram_tensor("thn", [D, D], F32, kind="ExternalInput")
    d_ph = nc.dram_tensor("ph", [D, D], F32, kind="ExternalInput")
    d_phn = nc.dram_tensor("phn", [D, D], F32, kind="ExternalInput")
    d_out = nc.dram_tensor("out", [D, SH], F32, kind="ExternalOutput")
    d_ln1 = nc.dram_tensor("ln1scr", [NDT, 128, S], BF16)
    d_zscr = nc.dram_tensor("zscr", [NP, 128, SH], BF16)

    em = Em()
    for s in ("pe", "act", "dve", "ldc", "ldta", "ldtb", "ldx", "stln0",
              "stln1", "ldw", "ldr0", "ldr1", "stz0", "stz1", "ldz", "ldxc",
              "stout"):
        em.sem(s)

    with contextlib.ExitStack() as es:
        s_id = es.enter_context(nc.sbuf_tensor("c_id", [128, 128], BF16))
        s_ones = es.enter_context(nc.sbuf_tensor("c_ones", [128, 128], BF16))
        s_b1c = es.enter_context(nc.sbuf_tensor("c_b1c", [128, 32], F32))
        s_gb = es.enter_context(nc.sbuf_tensor("c_gb", [128, 24], F32))
        s_cb = es.enter_context(nc.sbuf_tensor("c_cb", [128, 8], F32))
        s_betabf = es.enter_context(nc.sbuf_tensor("c_bb", [128, 8], BF16))
        s_cos = es.enter_context(nc.sbuf_tensor("cosm", [128, 8192], BF16))
        s_sin = es.enter_context(nc.sbuf_tensor("sinm", [128, 8192], BF16))
        p_all = es.enter_context(nc.psum_tensor("p_all", [128, 4096], F32))
        sems = {}

        def bank(i, w=512):
            return p_all[:, i * 512:i * 512 + w]

        for dst_, src_ in ((s_id, lambda: d_consts[:, 0:128]),
                           (s_ones, lambda: d_consts[:, 128:256]),
                           (s_b1c, lambda: d_b1c[:, :]),
                           (s_gb, lambda: d_gb[:, :]),
                           (s_cb, lambda: d_cb[:, :])):
            em.op("sync", lambda e, d=dst_, s=src_: e.dma_start(
                out=d[:, :], in_=s()), waits=[("ldc", em.cnt["ldc"])],
                inc="ldc", amt=16)
        em.op("gpsimd", lambda e: e.dma_start(out=s_betabf[:, :],
              in_=d_gb[:, 8:16]), waits=[("ldw", em.cnt["ldw"])],
              inc="ldw", amt=16)
        LDC = em.cnt["ldc"]
        PI2 = lambda: s_cb[:, 0:1]
        EPSB = lambda: s_cb[:, 1:2]
        TINY = lambda: s_cb[:, 2:3]
        K1024 = lambda: s_cb[:, 3:4]
        ONE = lambda: s_cb[:, 4:5]


        with contextlib.ExitStack() as ps:
            s_ta = [ps.enter_context(nc.sbuf_tensor(f"ta{i}", [128, 1024], F32))
                    for i in range(2)]
            s_tb = [ps.enter_context(nc.sbuf_tensor(f"tb{i}", [128, 1024], F32))
                    for i in range(2)]
            s_tsum = [ps.enter_context(
                nc.sbuf_tensor(f"ts{i}", [128, 1024], F32)) for i in range(2)]
            buf_free = [0, 0]
            sum_free = [0, 0]
            it = 0
            for da, db_, cosb, dst in ((d_th, d_thn, True, s_cos),
                                       (d_ph, d_phn, False, s_sin)):
                for dt_ in range(NDT):
                    par = it % 2
                    la = em.op("sync", lambda e, a=da, d=dt_, p=par:
                               e.dma_start(out=s_ta[p][:, :],
                                           in_=a[d * 128:(d + 1) * 128, :]),
                               waits=[("dve", buf_free[par]),
                                      ("ldta", em.cnt["ldta"])],
                               inc="ldta", amt=16)
                    lb = em.op("sync", lambda e, a=db_, d=dt_, p=par:
                               e.dma_start(out=s_tb[p][:, :],
                                           in_=a[d * 128:(d + 1) * 128, :]),
                               waits=[("ldtb", em.cnt["ldtb"])],
                               inc="ldtb", amt=16)
                    sm = em.op("vector", lambda e, p=par: e.tensor_tensor(
                        s_tsum[p][:, :], s_ta[p][:, :], s_tb[p][:, :],
                        op=ALU.add),
                        waits=[("ldta", la), ("ldtb", lb),
                               ("act", sum_free[par])],
                        inc="dve")
                    buf_free[par] = sm
                    if cosb:
                        a = em.op("scalar", lambda e, p=par:
                                  e.activation(
                                      out=s_tsum[p][:, :],
                                      in_=s_tsum[p][:, :], func=AF.Sin,
                                      bias=PI2()),
                                  waits=[("dve", sm), ("ldc", LDC)], inc="act")
                        sm = em.op("vector", lambda e, d=dt_, p=par, ds=dst:
                                   e.tensor_scalar(
                                       ds[:, d * 1024:(d + 1) * 1024],
                                       s_tsum[p][:, :], ONE(), None,
                                       op0=ALU.subtract),
                                   waits=[("act", a)], inc="dve")
                        buf_free[par] = sm
                        sum_free[par] = a
                        it += 1
                        continue
                    if True:
                        a = em.op("scalar", lambda e, d=dt_, p=par, ds=dst:
                                  e.activation(
                                      out=ds[:, d * 1024:(d + 1) * 1024],
                                      in_=s_tsum[p][:, :], func=AF.Sin),
                                  waits=[("dve", sm)], inc="act")
                    sum_free[par] = a
                    it += 1

        with contextlib.ExitStack() as ps:
            s_pxt = ps.enter_context(nc.sbuf_tensor("pxt", [128, 8192], F32))
            s_px = [s_pxt[:, d * 1024:(d + 1) * 1024] for d in range(NDT)]
            s_pbf = [ps.enter_context(
                nc.sbuf_tensor(f"pb{i}", [128, 1024], BF16)) for i in range(2)]
            s_psq = [ps.enter_context(
                nc.sbuf_tensor(f"pq{i}", [128, 1024], BF16)) for i in range(2)]
            s_mus = ps.enter_context(nc.sbuf_tensor("mus", [128, 1024], F32))
            s_var = ps.enter_context(nc.sbuf_tensor("var", [128, 1024], F32))
            s_rb = ps.enter_context(nc.sbuf_tensor("rb", [128, 1024], F32))
            s_t1 = ps.enter_context(nc.sbuf_tensor("t1", [128, 1024], F32))
            s_l1c = [ps.enter_context(
                nc.sbuf_tensor(f"l1{i}", [128, 1024], BF16)) for i in range(2)]
            px_free = 0
            pbf_free = [0, 0]
            sq_free = [0, 0]
            psq_free = [0, 0]
            l1_free = [0, 0]
            chain_free = 0
            mstat_done = 0
            for ch in range(8 if TRUNC >= 2 else 0):
                c0 = ch * 1024
                bar = ([("dve", em.cnt["dve"]), ("act", em.cnt["act"])]
                       if ch == 0 else [])
                ldp = em.op("sync", lambda e, c=c0: e.dma_start(
                    out=s_pxt[:, :].rearrange("p (d t) -> p d t", d=NDT),
                    in_=d_xT[:, c:c + 1024]
                    .rearrange("(d p) t -> p d t", p=128)),
                    waits=[("dve", px_free), ("ldx", em.cnt["ldx"])] + bar,
                    inc="ldx", amt=16)
                lds = [ldp] * NDT
                for d in range(NDT):
                    cv = em.op("vector", lambda e, d=d: e.tensor_copy(
                        s_pbf[d % 2][:, :], s_px[d][:, :]),
                        waits=[("ldx", lds[d]), ("pe", pbf_free[d % 2]),
                               ("act", sq_free[d % 2])],
                        inc="dve")
                    sq = em.op("scalar", lambda e, d=d: e.activation(
                        out=s_psq[d % 2][:, :], in_=s_pbf[d % 2][:, :],
                        func=AF.Square),
                        waits=[("dve", cv), ("pe", psq_free[d % 2])],
                        inc="act")
                    sq_free[d % 2] = sq
                    mlast = None
                    for h in range(2):
                        mlast = em.op("tensor", lambda e, d=d, h=h: e.matmul(
                            bank(h), s_ones[:, :],
                            s_pbf[d % 2][:, h * 512:(h + 1) * 512],
                            start=(d == 0), stop=(d == NDT - 1)),
                            waits=([("dve", cv), ("ldc", LDC),
                                    ("dve", chain_free)] if h == 0 else []),
                            inc="pe")
                    pbf_free[d % 2] = mlast
                    for h in range(2):
                        mlast = em.op("tensor", lambda e, d=d, h=h: e.matmul(
                            bank(2 + h), s_ones[:, :],
                            s_psq[d % 2][:, h * 512:(h + 1) * 512],
                            start=(d == 0), stop=(d == NDT - 1)),
                            waits=([("act", sq)] if h == 0 else []),
                            inc="pe")
                    psq_free[d % 2] = mlast
                    mstat_done = mlast
                cmu = em.op("vector", lambda e: e.tensor_copy(
                    s_mus[:, :], p_all[:, 0:1024]),
                    waits=[("pe", mstat_done)], inc="dve")
                em.op("vector", lambda e: e.tensor_tensor(
                    s_var[:, :], s_mus[:, :], s_mus[:, :], op=ALU.mult),
                    inc="dve")
                cvar = em.op("vector", lambda e: e.tensor_tensor(
                    s_var[:, :], p_all[:, 1024:2048], s_var[:, :],
                    op=ALU.subtract), inc="dve")
                chain_free = cvar
                em.op("scalar", lambda e: e.activation(
                    out=s_rb[:, :], in_=s_var[:, :], func=AF.Ln, bias=EPSB()),
                    waits=[("dve", cvar)], inc="act")
                arb = em.op("scalar", lambda e: e.activation(
                    out=s_rb[:, :], in_=s_rb[:, :], func=AF.Exp, scale=-0.5),
                    inc="act")
                for d in range(NDT):
                    t = em.op("vector", lambda e, d=d: e.tensor_tensor(
                        s_t1[:, :], s_px[d][:, :], s_mus[:, :],
                        op=ALU.subtract),
                        waits=[("act", arb)], inc="dve")
                    if d == NDT - 1:
                        px_free = t
                    lc = em.op("vector", lambda e, d=d: e.tensor_tensor(
                        s_l1c[d % 2][:, :], s_t1[:, :], s_rb[:, :],
                        op=ALU.mult),
                        waits=[(f"stln{d % 2}", l1_free[d % 2])],
                        inc="dve")
                    st = em.op("gpsimd", lambda e, d=d, c=c0: e.dma_start(
                        out=d_ln1[d, :, c:c + 1024], in_=s_l1c[d % 2][:, :]),
                        waits=[("dve", lc),
                               (f"stln{d % 2}", em.cnt[f"stln{d % 2}"])],
                        inc=f"stln{d % 2}", amt=16)
                    l1_free[d % 2] = st
        LN1_ST0 = em.cnt["stln0"]
        LN1_ST1 = em.cnt["stln1"]

        with contextlib.ExitStack() as ps:
            s_kT = [ps.enter_context(nc.sbuf_tensor(f"kT{i}", [128, S], BF16))
                    for i in range(2)]
            s_vsq = [ps.enter_context(nc.sbuf_tensor(f"vs{i}", [128, S], BF16))
                     for i in range(2)]
            s_qT = [ps.enter_context(nc.sbuf_tensor(f"qT{i}", [128, SH], BF16))
                    for i in range(2)]
            s_uT = [ps.enter_context(nc.sbuf_tensor(f"uT{i}", [128, SH], BF16))
                    for i in range(2)]
            s_vfm = ps.enter_context(nc.sbuf_tensor("vfm", [128, S], BF16))
            s_rhs = [ps.enter_context(
                nc.sbuf_tensor(f"rh{i}", [128, 4096], BF16)) for i in range(2)]
            s_w1 = ps.enter_context(nc.sbuf_tensor("w1b", [128, 4096], BF16))
            s_att = [ps.enter_context(
                nc.sbuf_tensor(f"at{i}", [128, 1024], BF16)) for i in range(2)]
            s_z = [ps.enter_context(
                nc.sbuf_tensor(f"z{i}", [128, 512], BF16)) for i in range(2)]

            w1_free = 0
            rhs_free = [0, 0]
            unit_free = [0] * 4
            vt_free = 0
            attn_read_done = [0, 0]
            z_free = [0, 0]
            zm_val = 0
            proj_done_act = 0
            proj_done_dve = 0

            s_pstg = [ps.enter_context(
                nc.sbuf_tensor(f"pst{i}", [128, 512], F32)) for i in range(4)]
            NPAIR = 0 if TRUNC < 3 else (NP if TRUNC >= 5 else 1)
            proj_done = {}
            bank7 = [0]
            pstg_free = [0, 0, 0, 0]

            def proj_steps(pair):
                nonlocal w1_free, proj_done
                par = pair % 2
                bar = ([("dve", em.cnt["dve"]), ("act", em.cnt["act"]),
                        ("pe", em.cnt["pe"])] if pair == 0 else [])
                lw = em.op("gpsimd", lambda e, p=pair: e.dma_start(
                    out=s_w1[:, :].rearrange("p (d c) -> p d c", d=NDT),
                    in_=d_w1p[p].rearrange("(d p) c -> p d c", p=128)),
                    waits=[("pe", w1_free), ("ldc", LDC),
                           ("ldw", em.cnt["ldw"])] + bar,
                    inc="ldw", amt=16)
                yield
                g = 0
                kinds_dst = {"k": (3, s_kT[par]), "v": (1, s_vfm),
                             "q": (2, s_qT[par]), "u": (0, s_uT[par])}
                pend_vt = None
                for j in range(17):
                    if j < 16:
                        jp = j % 2
                        lr = em.op("sync", lambda e, j=j, p=jp: e.dma_start(
                            out=s_rhs[p][:, :].rearrange(
                                "p (d t) -> p d t", d=NDT),
                            in_=d_ln1[:, :, j * 512:(j + 1) * 512]
                            .rearrange("d p t -> p d t")),
                            waits=[("pe", rhs_free[jp]), ("stln0", LN1_ST0),
                                   ("stln1", LN1_ST1),
                                   (f"ldr{jp}", em.cnt[f"ldr{jp}"])],
                            inc=f"ldr{jp}", amt=16)
                        yield
                        kinds = (["k", "v", "q", "u"] if j < 8
                                 else ["k", "v"])
                        mm_last = None
                        for kind in kinds:
                            ki, dst = kinds_dst[kind]
                            gp = g % 4
                            for d in range(NDT):
                                w = []
                                if d == 0:
                                    w = [(f"ldr{jp}", lr), ("ldw", lw),
                                         ("dve", bank7[0]),
                                         ("pe", attn_read_done[par])]
                                mm_last = em.op("tensor",
                                    lambda e, d=d, k=ki, p=jp: e.matmul(
                                        bank(7),
                                        s_w1[:, d * 512 + k * 128:
                                             d * 512 + k * 128 + 128],
                                        s_rhs[p][:, d * 512:(d + 1) * 512],
                                        start=(d == 0), stop=(d == NDT - 1)),
                                    waits=w, inc="pe")
                                if d % 2 == 1:
                                    yield
                            cp = em.op("vector", lambda e, gp=gp:
                                       e.tensor_copy(s_pstg[gp][:, :],
                                                     bank(7)),
                                       waits=[("pe", mm_last),
                                              ("act", pstg_free[gp])],
                                       inc="dve")
                            bank7[0] = cp
                            yield
                            bcol = ki * 8 + pair
                            ev = em.op("scalar", lambda e, gp=gp, j=j,
                                       b=bcol, ds=dst: e.activation(
                                           out=ds[:, j * 512:(j + 1) * 512],
                                           in_=s_pstg[gp][:, :],
                                           func=AF.Silu,
                                           bias=s_b1c[:, b:b + 1]),
                                       waits=[("dve", cp),
                                              ("pe", attn_read_done[par])],
                                       inc="act")
                            pstg_free[gp] = ev
                            if kind == "v":
                                vdef = (j, ev)
                            g += 1
                            yield
                        rhs_free[jp] = mm_last
                    if pend_vt is not None:
                        vj, vev = pend_vt
                        tlast = None
                        for c in range(4):
                            tlast = em.op("tensor",
                                lambda e, vj=vj, c=c: e.matmul(
                                    p_all[:, 3584 + c * 128:
                                          3584 + (c + 1) * 128],
                                    s_vfm[:, vj * 512 + c * 128:
                                          vj * 512 + (c + 1) * 128],
                                    s_id[:, :], start=True, stop=True),
                                waits=([("act", vev), ("dve", bank7[0]),
                                        ("ldc", LDC)] if c == 0 else []),
                                inc="pe")
                        yield
                        bank7[0] = em.op("vector", lambda e, vj=vj, p=par:
                                         e.tensor_copy(
                                             s_vsq[p][:, vj * 512:
                                                      (vj + 1) * 512],
                                             bank(7)),
                                         waits=[("pe", tlast),
                                                ("pe", attn_read_done[par])],
                                         inc="dve")
                        yield
                    if j < 16:
                        pend_vt = vdef
                proj_done[pair] = (em.cnt["act"], em.cnt["dve"])
                w1_free = em.cnt["pe"]

            if NPAIR > 0:
                for _ in proj_steps(0):
                    pass
            for pair in range(NPAIR):
                par = pair % 2
                bg = (proj_steps(pair + 1) if pair + 1 < NPAIR else iter(()))
                bg_alive = [True]

                def pop_bg(n):
                    for _ in range(n):
                        if not bg_alive[0]:
                            return
                        try:
                            next(bg)
                        except StopIteration:
                            bg_alive[0] = False
                            return

                if TRUNC < 4:
                    pop_bg(1 << 30)
                    continue
                NU = 512
                qk_val = [0] * NU
                av_val = [0] * NU
                silu_val = [0] * NU
                pd = proj_done[pair]
                for i in range(NU + 2):
                    if i < NU:
                        sl = i % 3
                        qp, kc = i // 64, i % 64
                        w = [("act", silu_val[i - 3] if i >= 3 else 0)]
                        if i == 0:
                            w += [("act", pd[0]), ("dve", pd[1])]
                        em.op("tensor", lambda e, sl=sl, q=qp, k=kc, p=par:
                              e.matmul(
                                  p_all[:, 512 + sl * 1024:
                                        512 + sl * 1024 + 512],
                                  s_kT[p][0:64, k * 128:(k + 1) * 128],
                                  s_qT[p][0:64, q * 512:(q + 1) * 512],
                                  start=True, stop=True,
                                  tile_position=(0, 0)),
                              waits=w)
                        qk_val[i] = em.op("tensor",
                              lambda e, sl=sl, q=qp, k=kc, p=par: e.matmul(
                                  p_all[:, 512 + sl * 1024 + 512:
                                        512 + sl * 1024 + 1024],
                                  s_kT[p][64:128, k * 128:(k + 1) * 128],
                                  s_qT[p][64:128, q * 512:(q + 1) * 512],
                                  start=True, stop=True,
                                  tile_position=(64, 0)),
                              inc="pe")
                    u = i - 2
                    if u >= 0:
                        w = [("pe", qk_val[u]),
                             ("pe", av_val[u - 2] if u >= 2 else 0)]
                        sl = u % 3
                        silu_val[u] = em.op("scalar",
                              lambda e, sl=sl, ab=u % 2: e.activation(
                                  out=s_att[ab][:, :],
                                  in_=p_all[:, 512 + sl * 1024:
                                            512 + sl * 1024 + 1024],
                                  func=AF.Silu),
                              waits=w, inc="act")
                        qp, kc = u // 64, u % 64
                        w = [("act", silu_val[u])]
                        if kc == 0:
                            w.append(("dve", zm_val))
                        em.op("tensor", lambda e, k=kc, ab=u % 2, p=par:
                              e.matmul(
                                  p_all[0:64, 0:512],
                                  s_vsq[p][:, k * 128:k * 128 + 64],
                                  s_att[ab][:, 0:512],
                                  start=(k == 0), stop=(k == 63),
                                  tile_position=(0, 0)),
                              waits=w)
                        av_val[u] = em.op("tensor",
                              lambda e, k=kc, ab=u % 2, p=par: e.matmul(
                                  p_all[64:128, 0:512],
                                  s_vsq[p][:, k * 128 + 64:k * 128 + 128],
                                  s_att[ab][:, 512:1024],
                                  start=(k == 0), stop=(k == 63),
                                  tile_position=(0, 64)),
                              inc="pe")
                        if kc == 63:
                            zp = (qp % 2)
                            zm_val = em.op("vector",
                                  lambda e, q=qp, p=par, zp=zp:
                                  e.tensor_tensor(
                                      s_z[zp][:, :], p_all[:, 0:512],
                                      s_uT[p][:, q * 512:(q + 1) * 512],
                                      op=ALU.mult),
                                  waits=[("pe", av_val[u]),
                                         (f"stz{zp}", z_free[zp])],
                                  inc="dve")
                            z_free[zp] = em.op("gpsimd",
                                  lambda e, q=qp, pr=pair, zp=zp:
                                  e.dma_start(
                                      out=d_zscr[pr, :, q * 512:(q + 1) * 512],
                                      in_=s_z[zp][:, :]),
                                  waits=[("dve", zm_val),
                                         (f"stz{zp}", em.cnt[f"stz{zp}"])],
                                  inc=f"stz{zp}", amt=16)
                    if i % 2 == 1:
                        pop_bg(1)
                pop_bg(1 << 30)
                attn_read_done[par] = em.cnt["pe"]
        Z_ALL0 = em.cnt["stz0"]
        Z_ALL1 = em.cnt["stz1"]


        if TRUNC >= 6:
            with contextlib.ExitStack() as ts:
                s_w2 = ts.enter_context(nc.sbuf_tensor("w2b", [128, 8192], BF16))
                s_c2 = ts.enter_context(nc.sbuf_tensor("c2", [128, 8], F32))
                s_zc = [ts.enter_context(
                    nc.sbuf_tensor("zc0", [128, 4096], BF16))] * 2
                s_xc = [ts.enter_context(
                    nc.sbuf_tensor("xc0", [128, 4096], F32))] * 2
                s_zsq = ts.enter_context(nc.sbuf_tensor("zsq", [128, 4096], BF16))
                s_zln = ts.enter_context(nc.sbuf_tensor("zln", [128, 4096], BF16))
                s_y = ts.enter_context(nc.sbuf_tensor("yb", [128, 4096], BF16))
                s_ysq = ts.enter_context(nc.sbuf_tensor("ysq", [128, 4096], BF16))
                s_x1 = ts.enter_context(nc.sbuf_tensor("x1", [128, 4096], F32))
                s_x1b = ts.enter_context(nc.sbuf_tensor("x1b", [128, 4096], BF16))
                s_h = ts.enter_context(nc.sbuf_tensor("hb", [128, 4096], BF16))
                s_mag = ts.enter_context(nc.sbuf_tensor("mag", [128, 4096], BF16))
                s_m2 = ts.enter_context(nc.sbuf_tensor("m2", [128, 4096], BF16))
                s_sqa = ts.enter_context(nc.sbuf_tensor("sqa", [128, 512], F32))
                s_sqb = ts.enter_context(nc.sbuf_tensor("sqb", [128, 512], F32))
                s_x2 = ts.enter_context(nc.sbuf_tensor("x2", [128, 4096], F32))
                s_mu = [ts.enter_context(
                    nc.sbuf_tensor(f"mu{i}", [128, 512], F32)) for i in range(2)]
                s_vr = [ts.enter_context(
                    nc.sbuf_tensor(f"vr{i}", [128, 512], F32)) for i in range(2)]
                s_r = [ts.enter_context(
                    nc.sbuf_tensor(f"r{i}", [128, 512], F32)) for i in range(4)]
                s_mr = [ts.enter_context(
                    nc.sbuf_tensor(f"mr{i}", [128, 512], F32)) for i in range(4)]

                lw2 = em.op("gpsimd", lambda e: e.dma_start(
                    out=s_w2[:, :].rearrange("p (d c) -> p d c", d=NDT),
                    in_=d_w2.rearrange("(d p) c -> p d c", p=128)),
                    waits=[("ldc", LDC), ("ldw", em.cnt["ldw"]),
                           ("pe", em.cnt["pe"]), ("act", em.cnt["act"]),
                           ("dve", em.cnt["dve"])],
                    inc="ldw", amt=16)
                c2mm = None
                for f in range(8):
                    for d in range(NDT):
                        c2mm = em.op("tensor", lambda e, f=f, d=d: e.matmul(
                            p_all[:, 3072 + f:3072 + f + 1],
                            s_w2[:, d * 1024 + f * 128:d * 1024 + (f + 1) * 128],
                            s_betabf[:, d:d + 1],
                            start=(d == 0), stop=(d == NDT - 1)),
                            waits=([("ldw", lw2), ("ldc", LDC),
                                    ("act", em.cnt["act"]),
                                    ("dve", em.cnt["dve"])]
                                   if f == 0 and d == 0 else []),
                            inc="pe")
                c2dve = em.op("vector", lambda e: e.tensor_tensor(
                    s_c2[:, :], p_all[:, 3072:3080], s_gb[:, 16:24], op=ALU.add),
                    waits=[("pe", c2mm)], inc="dve")
                gma = None
                for d in range(NDT):
                    gma = em.op("vector", lambda e, d=d: e.tensor_scalar(
                        s_w2[:, d * 1024:(d + 1) * 1024],
                        s_w2[:, d * 1024:(d + 1) * 1024],
                        s_gb[:, d:d + 1], None, op0=ALU.mult),
                        waits=[("pe", c2mm)], inc="dve")
                W2RDY = gma if KGM else em.cnt["dve"]

                zc_free = [em.cnt["pe"], em.cnt["dve"]]
                xc_free = [em.cnt["dve"], 0]
                zsq_free = 0
                stv = 0

                def rstd_chain(mwait, cpar, rix):
                    em.op("vector", lambda e: e.tensor_copy(
                        s_mu[cpar][:, :], bank(0)),
                        waits=[("pe", mwait)], inc="dve")
                    em.op("vector", lambda e: e.tensor_tensor(
                        s_vr[cpar][:, :], s_mu[cpar][:, :], s_mu[cpar][:, :],
                        op=ALU.mult), inc="dve")
                    vfree = em.op("vector", lambda e: e.tensor_tensor(
                        s_vr[cpar][:, :], bank(1), s_vr[cpar][:, :],
                        op=ALU.subtract), inc="dve")
                    em.op("scalar", lambda e: e.activation(
                        out=s_r[rix][:, :], in_=s_vr[cpar][:, :], func=AF.Ln,
                        bias=EPSB()), waits=[("dve", vfree)], inc="act")
                    ar = em.op("scalar", lambda e: e.activation(
                        out=s_r[rix][:, :], in_=s_r[rix][:, :], func=AF.Exp,
                        scale=-0.5), inc="act")
                    mrv = em.op("vector", lambda e: e.tensor_tensor(
                        s_mr[rix][:, :], s_mu[cpar][:, :], s_r[rix][:, :],
                        op=ALU.mult), waits=[("act", ar)], inc="dve")
                    return vfree, mrv

                stats_free = c2dve
                for n in range(TAILN):
                    c0 = n * 512
                    np_ = n % 2
                    lz = em.op("sync", lambda e, c=c0, p=np_: e.dma_start(
                        out=s_zc[p][:, :].rearrange("p (d t) -> p d t", d=NP),
                        in_=d_zscr[:, :, c:c + 512].rearrange("d p t -> p d t")),
                        waits=[("stz0", Z_ALL0), ("stz1", Z_ALL1),
                               ("pe", zc_free[0]), ("dve", zc_free[1]),
                               ("ldz", em.cnt["ldz"])],
                        inc="ldz", amt=16)
                    lx = em.op("sync", lambda e, c=c0, p=np_: e.dma_start(
                        out=s_xc[p][:, :].rearrange("p (d t) -> p d t", d=NDT),
                        in_=d_xT[:, c:c + 512]
                        .rearrange("(d p) t -> p d t", p=128)),
                        waits=[("dve", xc_free[0]), ("ldxc", em.cnt["ldxc"])],
                        inc="ldxc", amt=16)
                    if KSTEP <= 1: continue
                    zq = em.op("scalar", lambda e, p=np_: e.activation(
                        out=s_zsq[:, :], in_=s_zc[p][:, :], func=AF.Square),
                        waits=[("ldz", lz), ("dve", zsq_free)], inc="act")
                    for d in range(NDT):
                        em.op("tensor", lambda e, d=d, p=np_: e.matmul(
                            bank(0), s_ones[:, :],
                            s_zc[p][:, d * 512:(d + 1) * 512],
                            start=(d == 0), stop=(d == NDT - 1)),
                            waits=([("ldz", lz), ("dve", stats_free)]
                                   if d == 0 else []))
                    st2 = None
                    for d in range(NDT):
                        st2 = em.op("tensor", lambda e, d=d: e.matmul(
                            bank(1), s_ones[:, :],
                            s_zsq[:, d * 512:(d + 1) * 512],
                            start=(d == 0), stop=(d == NDT - 1)),
                            waits=([("act", zq)] if d == 0 else []), inc="pe")
                    stats_free, mr2 = rstd_chain(st2, 0, 0)
                    if KSTEP <= 2: continue
                    zl = None
                    for d in range(NDT):
                        em.op("vector", lambda e, d=d, p=np_: e.tensor_tensor(
                            s_zsq[:, d * 512:(d + 1) * 512],
                            s_zc[p][:, d * 512:(d + 1) * 512], s_r[0][:, :],
                            op=ALU.mult), waits=[("dve", mr2)], inc="dve")
                        zl = em.op("vector", lambda e, d=d: e.tensor_tensor(
                            s_zln[:, d * 512:(d + 1) * 512],
                            s_zsq[:, d * 512:(d + 1) * 512], s_mr[0][:, :],
                            op=ALU.subtract), inc="dve")
                    if KSTEP <= 3: continue
                    ysq_l = None
                    ypar_dve = [0, 0]
                    for f in range(8):
                        mm = None
                        for d in range(NDT):
                            w = []
                            if d == 0:
                                w = [("dve", zl), ("dve", W2RDY),
                                     ("dve", ypar_dve[f % 2])]
                            mm = em.op("tensor", lambda e, f=f, d=d: e.matmul(
                                bank(2 + f % 2),
                                s_w2[:, d * 1024 + f * 128:
                                     d * 1024 + (f + 1) * 128],
                                s_zln[:, d * 512:(d + 1) * 512],
                                start=(d == 0), stop=(d == NDT - 1)),
                                waits=w, inc="pe")
                            if f == 0 and d == NDT - 1:
                                zc_free[0] = mm
                                zc_free[1] = zl
                        yev = em.op("vector", lambda e, f=f: e.tensor_tensor(
                            s_y[:, f * 512:(f + 1) * 512],
                            bank(2 + f % 2),
                            s_c2[:, f:f + 1].broadcast_to([128, 512]),
                            op=ALU.add),
                            waits=[("pe", mm), ("dve", c2dve)], inc="dve")
                        ysq_l = em.op("scalar", lambda e, f=f: e.activation(
                            out=s_ysq[:, f * 512:(f + 1) * 512],
                            in_=s_y[:, f * 512:(f + 1) * 512],
                            func=AF.Square),
                            waits=[("dve", yev)], inc="act")
                        ypar_dve[f % 2] = yev
                    if KSTEP <= 4: continue
                    for f in range(8):
                        em.op("tensor", lambda e, f=f: e.matmul(
                            bank(0), s_ones[:, :], s_y[:, f * 512:(f + 1) * 512],
                            start=(f == 0), stop=(f == 7)),
                            waits=([("dve", yev), ("dve", stats_free)]
                                   if f == 0 else []))
                    sty = None
                    for f in range(8):
                        sty = em.op("tensor", lambda e, f=f: e.matmul(
                            bank(1), s_ones[:, :], s_ysq[:, f * 512:(f + 1) * 512],
                            start=(f == 0), stop=(f == 7)),
                            waits=([("act", ysq_l)] if f == 0 else []), inc="pe")
                    stats_free, mry = rstd_chain(sty, 1, 1)
                    if KSTEP <= 5: continue
                    x1b_l = None
                    x1sq_l = None
                    for f in range(8):
                        em.op("vector", lambda e, f=f: e.tensor_tensor(
                            s_zsq[:, f * 512:(f + 1) * 512],
                            s_y[:, f * 512:(f + 1) * 512], s_r[1][:, :],
                            op=ALU.mult),
                            waits=[("dve", mry), ("ldxc", lx)], inc="dve")
                        em.op("vector", lambda e, f=f: e.tensor_tensor(
                            s_zsq[:, f * 512:(f + 1) * 512],
                            s_zsq[:, f * 512:(f + 1) * 512], s_mr[1][:, :],
                            op=ALU.subtract), inc="dve")
                        x1v = em.op("vector", lambda e, f=f, p=np_:
                                    e.tensor_tensor(
                                        s_x1[:, f * 512:(f + 1) * 512],
                                        s_zsq[:, f * 512:(f + 1) * 512],
                                        s_xc[p][:, f * 512:(f + 1) * 512],
                                        op=ALU.add), inc="dve")
                        x1b_l = em.op("scalar", lambda e, f=f: e.activation(
                            out=s_x1b[:, f * 512:(f + 1) * 512],
                            in_=s_x1[:, f * 512:(f + 1) * 512], func=AF.Copy),
                            waits=[("dve", x1v)], inc="act")
                        x1sq_l = em.op("scalar", lambda e, f=f: e.activation(
                            out=s_ysq[:, f * 512:(f + 1) * 512],
                            in_=s_x1[:, f * 512:(f + 1) * 512], func=AF.Square),
                            inc="act")
                    xc_free[0] = em.cnt["dve"]
                    if KSTEP <= 6: continue
                    for f in range(8):
                        em.op("tensor", lambda e, f=f: e.matmul(
                            bank(0), s_ones[:, :], s_x1b[:, f * 512:(f + 1) * 512],
                            start=(f == 0), stop=(f == 7)),
                            waits=([("act", x1b_l), ("dve", stats_free)]
                                   if f == 0 else []))
                    st3 = None
                    for f in range(8):
                        st3 = em.op("tensor", lambda e, f=f: e.matmul(
                            bank(1), s_ones[:, :], s_ysq[:, f * 512:(f + 1) * 512],
                            start=(f == 0), stop=(f == 7)),
                            waits=([("act", x1sq_l)] if f == 0 else []), inc="pe")
                    stats_free, mr3 = rstd_chain(st3, 0, 2)
                    if KSTEP <= 7: continue
                    hl = None
                    for f in range(8):
                        em.op("vector", lambda e, f=f: e.tensor_tensor(
                            s_zsq[:, f * 512:(f + 1) * 512],
                            s_x1b[:, f * 512:(f + 1) * 512], s_r[2][:, :],
                            op=ALU.mult), waits=[("dve", mr3)], inc="dve")
                        hl = em.op("vector", lambda e, f=f: e.tensor_tensor(
                            s_h[:, f * 512:(f + 1) * 512],
                            s_zsq[:, f * 512:(f + 1) * 512], s_mr[2][:, :],
                            op=ALU.subtract), inc="dve")
                    if KSTEP <= 8: continue
                    ri_free = [0, 0, 0, 0]
                    m2_l = 0
                    mag_l = None
                    for fo in range(8):
                        rb_ = 4 + (fo % 2) * 2
                        ib_ = 5 + (fo % 2) * 2
                        mmr = None
                        for d in range(NDT):
                            w = []
                            if d == 0:
                                w = [("dve", hl), ("act", ri_free[2 * (fo % 2)])]
                            mmr = em.op("tensor", lambda e, d=d, f=fo, b=rb_:
                                        e.matmul(
                                            bank(b),
                                            s_cos[:, d * 1024 + f * 128:
                                                  d * 1024 + (f + 1) * 128],
                                            s_h[:, d * 512:(d + 1) * 512],
                                            start=(d == 0), stop=(d == NDT - 1)),
                                        waits=w, inc="pe")
                        mmi = None
                        for d in range(NDT):
                            w = []
                            if d == 0:
                                w = [("act", ri_free[2 * (fo % 2) + 1])]
                            mmi = em.op("tensor", lambda e, d=d, f=fo, b=ib_:
                                        e.matmul(
                                            bank(b),
                                            s_sin[:, d * 1024 + f * 128:
                                                  d * 1024 + (f + 1) * 128],
                                            s_h[:, d * 512:(d + 1) * 512],
                                            start=(d == 0), stop=(d == NDT - 1)),
                                        waits=w, inc="pe")
                        sa = em.op("scalar", lambda e, b=rb_: e.activation(
                            out=s_sqa[:, :], in_=bank(b), func=AF.Square),
                            waits=[("pe", mmr), ("dve", m2_l)], inc="act")
                        ri_free[2 * (fo % 2)] = sa
                        sb_ = em.op("scalar", lambda e, b=ib_: e.activation(
                            out=s_sqb[:, :], in_=bank(b), func=AF.Square),
                            waits=[("pe", mmi)], inc="act")
                        ri_free[2 * (fo % 2) + 1] = sb_
                        m2_l = em.op("vector", lambda e, f=fo: e.tensor_tensor(
                            s_m2[:, f * 512:(f + 1) * 512], s_sqa[:, :],
                            s_sqb[:, :], op=ALU.add),
                            waits=[("act", sb_)], inc="dve")
                        em.op("scalar", lambda e, f=fo: e.activation(
                            out=s_mag[:, f * 512:(f + 1) * 512],
                            in_=s_m2[:, f * 512:(f + 1) * 512], func=AF.Ln,
                            bias=TINY()), waits=[("dve", m2_l)], inc="act")
                        mag_l = em.op("scalar", lambda e, f=fo: e.activation(
                            out=s_mag[:, f * 512:(f + 1) * 512],
                            in_=s_mag[:, f * 512:(f + 1) * 512], func=AF.Exp,
                            scale=0.5), inc="act")
                    if KSTEP <= 9: continue
                    for f in range(8):
                        em.op("tensor", lambda e, f=f: e.matmul(
                            bank(0), s_ones[:, :], s_mag[:, f * 512:(f + 1) * 512],
                            start=(f == 0), stop=(f == 7)),
                            waits=([("act", mag_l), ("dve", stats_free)]
                                   if f == 0 else []))
                    st4 = None
                    for f in range(8):
                        st4 = em.op("tensor", lambda e, f=f: e.matmul(
                            bank(1), s_ones[:, :], s_m2[:, f * 512:(f + 1) * 512],
                            start=(f == 0), stop=(f == 7)),
                            waits=([("dve", m2_l)] if f == 0 else []), inc="pe")
                    stats_free, mr4 = rstd_chain(st4, 1, 3)
                    if KSTEP <= 10: continue
                    prev_st = stv
                    xl = None
                    for f in range(8):
                        em.op("vector", lambda e, f=f: e.tensor_tensor(
                            s_zsq[:, f * 512:(f + 1) * 512],
                            s_mag[:, f * 512:(f + 1) * 512], s_r[3][:, :],
                            op=ALU.mult),
                            waits=[("dve", mr4), ("stout", prev_st)], inc="dve")
                        em.op("vector", lambda e, f=f: e.tensor_tensor(
                            s_zsq[:, f * 512:(f + 1) * 512],
                            s_zsq[:, f * 512:(f + 1) * 512], s_mr[3][:, :],
                            op=ALU.subtract), inc="dve")
                        xl = em.op("vector", lambda e, f=f: e.tensor_tensor(
                            s_x2[:, f * 512:(f + 1) * 512],
                            s_zsq[:, f * 512:(f + 1) * 512],
                            s_x1[:, f * 512:(f + 1) * 512], op=ALU.add),
                            inc="dve")
                    zsq_free = xl
                    stv = em.op("gpsimd", lambda e, c=c0: e.dma_start(
                        out=d_out[:, c:c + 512].rearrange("(f p) t -> p f t",
                                                          p=128),
                        in_=s_x2[:, :].rearrange("p (f t) -> p f t", f=NDT)),
                        waits=[("dve", xl), ("stout", em.cnt["stout"])],
                        inc="stout", amt=16)
                em.op("gpsimd", lambda e: e.nop(),
                      waits=[("stout", em.cnt["stout"])])
            if int(_os.environ.get("KEXTRA2", "0")):
                em.op("tensor", lambda e: e.matmul(
                    bank(2), s_cos[:, 0:128], s_cos[:, 0:512],
                    start=True, stop=True), inc="pe")
            if TAILN < 8:
                waits = [(s_, em.cnt[s_]) for s_ in em.cnt]
                _mk = _os.environ.get("KMARK", "cos")
                msrc = {"cos": s_cos[:, 0:4096], "zln": s_zln[:, :],
                        "zc": s_zc[0][:, :], "w2": s_w2[:, 0:4096],
                        "r0": s_r[0][:, :], "mr0": s_mr[0][:, :]}[_mk]
                em.op("gpsimd", lambda e, m=msrc: e.dma_start(
                    out=d_out[0:128, 0:m.shape[-1]], in_=m),
                    waits=waits, inc="stout", amt=16)
                em.op("gpsimd", lambda e: e.nop(),
                      waits=[("stout", em.cnt["stout"])])
        else:
            waits = [(s_, em.cnt[s_]) for s_ in em.cnt]
            em.op("gpsimd", lambda e: e.dma_start(
                out=d_out[0:128, :], in_=s_cos[:, 0:4096]),
                waits=waits, inc="stout", amt=16)
            em.op("gpsimd", lambda e: e.nop(),
                  waits=[("stout", em.cnt["stout"])])

        for name in em.cnt:
            sems[name] = es.enter_context(nc.semaphore(name))
        with nc.Block() as block:
            def mk(eng_name):
                def body(eng):
                    for waits, fn, inc, amt in em.ops[eng_name]:
                        for sname, val in waits:
                            eng.wait_ge(sems[sname], val)
                        r = fn(eng)
                        if inc is not None:
                            assert r is not None, f"{eng_name} inc={inc}"
                            r.then_inc(sems[inc], amt)
                return body
            block.sync(mk("sync"))
            block.gpsimd(mk("gpsimd"))
            block.tensor(mk("tensor"))
            block.scalar(mk("scalar"))
            block.vector(mk("vector"))

    return nc


def kernel(L0_attn_x, L0_attn_W1, L0_attn_b1, L0_attn_W2, L0_attn_b2,
           L0_attn_gamma, L0_attn_beta, L0_ffn_theta, L0_ffn_phi,
           L0_ffn_theta_noise, L0_ffn_phi_noise):
    x = np.asarray(L0_attn_x, np.float32)
    W1 = np.asarray(L0_attn_W1, np.float32)
    b1 = np.asarray(L0_attn_b1, np.float32)
    W2 = np.ascontiguousarray(np.asarray(L0_attn_W2, np.float32))
    b2 = np.asarray(L0_attn_b2, np.float32)
    gamma = np.asarray(L0_attn_gamma, np.float32)
    beta = np.asarray(L0_attn_beta, np.float32)
    th = np.ascontiguousarray(np.asarray(L0_ffn_theta, np.float32))
    thn = np.ascontiguousarray(np.asarray(L0_ffn_theta_noise, np.float32))
    ph = np.ascontiguousarray(np.asarray(L0_ffn_phi, np.float32))
    phn = np.ascontiguousarray(np.asarray(L0_ffn_phi_noise, np.float32))

    xT = np.ascontiguousarray(x.transpose(0, 2, 1))

    w1p = np.empty((NP, D, 512), np.float32)
    for p in range(NP):
        for k in range(4):
            w1p[p, :, k * 128:(k + 1) * 128] = \
                W1[:, k * D + p * 128:k * D + (p + 1) * 128]
    b1c = np.ascontiguousarray(
        b1.reshape(4, 8, 128).transpose(2, 0, 1).reshape(128, 32))
    gb = np.ascontiguousarray(np.concatenate(
        [gamma.reshape(8, 128).T, beta.reshape(8, 128).T,
         b2.reshape(8, 128).T], axis=1))
    consts = np.concatenate(
        [np.eye(128, dtype=np.float32) / S,
         np.ones((128, 128), np.float32) / D], axis=1).astype(BF)
    cb = np.tile(np.array([[math.pi / 2, EPS, 1e-30, 1024.0,
                            1.0, 0.0, 0.0, 0.0]], np.float32), (128, 1))

    nc = build_nc()
    in_maps = []
    for c in range(8):
        b, half = c // 2, c % 2
        off = half * SH
        xrot = np.ascontiguousarray(
            np.concatenate([xT[b][:, off:], xT[b][:, :off]], axis=1))
        in_maps.append(dict(
            xT=xrot, w1p=w1p, b1c=b1c, gb=gb, consts=consts, cb=cb,
            w2=W2, th=th, thn=thn, ph=ph, phn=phn))
    res = run_bass_kernel_spmd(nc, in_maps, core_ids=list(range(8)))
    out = np.empty_like(x)
    for c in range(8):
        b, half = c // 2, c % 2
        out[b, half * SH:(half + 1) * SH, :] = res.results[c]["out"].T
    return out


if __name__ == "__main__":
    import reference as _r
    inp = _r.setup_inputs(0)
    want = np.asarray(_r.reference(**inp))
    got = kernel(**{k: np.asarray(v) for k, v in inp.items()})
    err = got - want
    rvr = float((err ** 2).mean() / (want ** 2).mean())
    print("rvr", rvr, "max", float(np.abs(err).max()))


# version 08
# speedup vs baseline: 1.0493x; 1.0154x over previous
import contextlib
import math
import numpy as np
import ml_dtypes
import concourse.bass as bass
import concourse.mybir as mybir
from concourse.bass_utils import run_bass_kernel_spmd

F32 = mybir.dt.float32
BF16 = mybir.dt.bfloat16
AF = mybir.ActivationFunctionType
ALU = mybir.AluOpType
BF = ml_dtypes.bfloat16

D = 1024
S = 8192
SH = 4096
NDT = 8
NP = 8
EPS = 1e-5

ENGINES = ("sync", "gpsimd", "tensor", "scalar", "vector")
SELF_SEM = {"vector": "dve", "scalar": "act"}
import os as _os
TRUNC = int(_os.environ.get("KTRUNC", "6"))
TAILN = int(_os.environ.get("KTAILN", "8"))
KC2 = _os.environ.get("KC2", "1") == "1"
KGM = _os.environ.get("KGM", "1") == "1"
KW2 = _os.environ.get("KW2", "1") == "1"
KSTEP = int(_os.environ.get("KSTEP", "99"))
KYALT = int(_os.environ.get("KYALT", "0"))
KYB = int(_os.environ.get("KYB", "2"))
KYF = int(_os.environ.get("KYF", "8"))
KYSRC = _os.environ.get("KYSRC", "w2")
KYRHS = _os.environ.get("KYRHS", "zln")
KYW = _os.environ.get("KYW", "1") == "1"
KYD = int(_os.environ.get("KYD", "8"))
KYN = int(_os.environ.get("KYN", "512"))


class Em:

    def __init__(self):
        self.ops = {e: [] for e in ENGINES}
        self.cnt = {}
        self.last_wait = {e: {} for e in ENGINES}

    def sem(self, name):
        self.cnt.setdefault(name, 0)

    def op(self, eng, fn, waits=(), inc=None, amt=1):
        waits = list(waits)
        if eng in SELF_SEM:
            assert inc == SELF_SEM[eng], f"{eng} op must inc {SELF_SEM[eng]}"
            waits.append((SELF_SEM[eng], self.cnt[SELF_SEM[eng]]))
        w = []
        for sname, val in waits:
            assert val <= self.cnt[sname], (
                f"deadlock: {eng} waits {sname}>={val} > {self.cnt[sname]}")
            if val <= 0 or self.last_wait[eng].get(sname, -1) >= val:
                continue
            self.last_wait[eng][sname] = val
            w.append((sname, val))
        v = None
        if inc is not None:
            self.cnt[inc] += amt
            v = self.cnt[inc]
        self.ops[eng].append((w, fn, inc, amt))
        return v


def build_nc():
    nc = bass.Bass("TRN2", target_bir_lowering=False)

    d_xT = nc.dram_tensor("xT", [D, S], F32, kind="ExternalInput")
    d_w1p = nc.dram_tensor("w1p", [NP, D, 512], F32, kind="ExternalInput")
    d_b1c = nc.dram_tensor("b1c", [128, 32], F32, kind="ExternalInput")
    d_gb = nc.dram_tensor("gb", [128, 24], F32, kind="ExternalInput")
    d_consts = nc.dram_tensor("consts", [128, 256], BF16, kind="ExternalInput")
    d_cb = nc.dram_tensor("cb", [128, 8], F32, kind="ExternalInput")
    d_w2 = nc.dram_tensor("w2", [D, D], F32, kind="ExternalInput")
    d_th = nc.dram_tensor("th", [D, D], F32, kind="ExternalInput")
    d_thn = nc.dram_tensor("thn", [D, D], F32, kind="ExternalInput")
    d_ph = nc.dram_tensor("ph", [D, D], F32, kind="ExternalInput")
    d_phn = nc.dram_tensor("phn", [D, D], F32, kind="ExternalInput")
    d_out = nc.dram_tensor("out", [D, SH], F32, kind="ExternalOutput")
    d_ln1 = nc.dram_tensor("ln1scr", [NDT, 128, S], BF16)
    d_zscr = nc.dram_tensor("zscr", [NP, 128, SH], BF16)

    em = Em()
    for s in ("pe", "act", "dve", "ldc", "ldta", "ldtb", "ldx", "stln0",
              "stln1", "ldw", "ldr0", "ldr1", "stz0", "stz1", "ldz0", "ldz1",
              "ldxc", "stout"):
        em.sem(s)

    with contextlib.ExitStack() as es:
        s_id = es.enter_context(nc.sbuf_tensor("c_id", [128, 128], BF16))
        s_ones = es.enter_context(nc.sbuf_tensor("c_ones", [128, 128], BF16))
        s_b1c = es.enter_context(nc.sbuf_tensor("c_b1c", [128, 32], F32))
        s_gb = es.enter_context(nc.sbuf_tensor("c_gb", [128, 24], F32))
        s_cb = es.enter_context(nc.sbuf_tensor("c_cb", [128, 8], F32))
        s_betabf = es.enter_context(nc.sbuf_tensor("c_bb", [128, 8], BF16))
        s_cos = es.enter_context(nc.sbuf_tensor("cosm", [128, 8192], BF16))
        s_sin = es.enter_context(nc.sbuf_tensor("sinm", [128, 8192], BF16))
        p_all = es.enter_context(nc.psum_tensor("p_all", [128, 4096], F32))
        sems = {}

        def bank(i, w=512):
            return p_all[:, i * 512:i * 512 + w]

        for dst_, src_ in ((s_id, lambda: d_consts[:, 0:128]),
                           (s_ones, lambda: d_consts[:, 128:256]),
                           (s_b1c, lambda: d_b1c[:, :]),
                           (s_gb, lambda: d_gb[:, :]),
                           (s_cb, lambda: d_cb[:, :])):
            em.op("sync", lambda e, d=dst_, s=src_: e.dma_start(
                out=d[:, :], in_=s()), waits=[("ldc", em.cnt["ldc"])],
                inc="ldc", amt=16)
        em.op("gpsimd", lambda e: e.dma_start(out=s_betabf[:, :],
              in_=d_gb[:, 8:16]), waits=[("ldw", em.cnt["ldw"])],
              inc="ldw", amt=16)
        LDC = em.cnt["ldc"]
        PI2 = lambda: s_cb[:, 0:1]
        EPSB = lambda: s_cb[:, 1:2]
        TINY = lambda: s_cb[:, 2:3]
        K1024 = lambda: s_cb[:, 3:4]
        ONE = lambda: s_cb[:, 4:5]


        with contextlib.ExitStack() as ps:
            s_ta = [ps.enter_context(nc.sbuf_tensor(f"ta{i}", [128, 1024], F32))
                    for i in range(2)]
            s_tb = [ps.enter_context(nc.sbuf_tensor(f"tb{i}", [128, 1024], F32))
                    for i in range(2)]
            s_tsum = [ps.enter_context(
                nc.sbuf_tensor(f"ts{i}", [128, 1024], F32)) for i in range(2)]
            buf_free = [0, 0]
            sum_free = [0, 0]
            it = 0
            for da, db_, cosb, dst in ((d_th, d_thn, True, s_cos),
                                       (d_ph, d_phn, False, s_sin)):
                for dt_ in range(NDT):
                    par = it % 2
                    la = em.op("sync", lambda e, a=da, d=dt_, p=par:
                               e.dma_start(out=s_ta[p][:, :],
                                           in_=a[d * 128:(d + 1) * 128, :]),
                               waits=[("dve", buf_free[par]),
                                      ("ldta", em.cnt["ldta"])],
                               inc="ldta", amt=16)
                    lb = em.op("sync", lambda e, a=db_, d=dt_, p=par:
                               e.dma_start(out=s_tb[p][:, :],
                                           in_=a[d * 128:(d + 1) * 128, :]),
                               waits=[("ldtb", em.cnt["ldtb"])],
                               inc="ldtb", amt=16)
                    sm = em.op("vector", lambda e, p=par: e.tensor_tensor(
                        s_tsum[p][:, :], s_ta[p][:, :], s_tb[p][:, :],
                        op=ALU.add),
                        waits=[("ldta", la), ("ldtb", lb),
                               ("act", sum_free[par])],
                        inc="dve")
                    buf_free[par] = sm
                    if cosb:
                        a = em.op("scalar", lambda e, p=par:
                                  e.activation(
                                      out=s_tsum[p][:, :],
                                      in_=s_tsum[p][:, :], func=AF.Sin,
                                      bias=PI2()),
                                  waits=[("dve", sm), ("ldc", LDC)], inc="act")
                        sm = em.op("vector", lambda e, d=dt_, p=par, ds=dst:
                                   e.tensor_scalar(
                                       ds[:, d * 1024:(d + 1) * 1024],
                                       s_tsum[p][:, :], ONE(), None,
                                       op0=ALU.subtract),
                                   waits=[("act", a)], inc="dve")
                        buf_free[par] = sm
                        sum_free[par] = a
                        it += 1
                        continue
                    if True:
                        a = em.op("scalar", lambda e, d=dt_, p=par, ds=dst:
                                  e.activation(
                                      out=ds[:, d * 1024:(d + 1) * 1024],
                                      in_=s_tsum[p][:, :], func=AF.Sin),
                                  waits=[("dve", sm)], inc="act")
                    sum_free[par] = a
                    it += 1

        with contextlib.ExitStack() as ps:
            s_pxt = ps.enter_context(nc.sbuf_tensor("pxt", [128, 8192], F32))
            s_px = [s_pxt[:, d * 1024:(d + 1) * 1024] for d in range(NDT)]
            s_pbf = [ps.enter_context(
                nc.sbuf_tensor(f"pb{i}", [128, 1024], BF16)) for i in range(2)]
            s_psq = [ps.enter_context(
                nc.sbuf_tensor(f"pq{i}", [128, 1024], BF16)) for i in range(2)]
            s_mus = ps.enter_context(nc.sbuf_tensor("mus", [128, 1024], F32))
            s_var = ps.enter_context(nc.sbuf_tensor("var", [128, 1024], F32))
            s_rb = ps.enter_context(nc.sbuf_tensor("rb", [128, 1024], F32))
            s_t1 = ps.enter_context(nc.sbuf_tensor("t1", [128, 1024], F32))
            s_l1c = [ps.enter_context(
                nc.sbuf_tensor(f"l1{i}", [128, 1024], BF16)) for i in range(2)]
            px_free = 0
            pbf_free = [0, 0]
            sq_free = [0, 0]
            psq_free = [0, 0]
            l1_free = [0, 0]
            chain_free = 0
            mstat_done = 0
            for ch in range(8 if TRUNC >= 2 else 0):
                c0 = ch * 1024
                bar = ([("dve", em.cnt["dve"]), ("act", em.cnt["act"])]
                       if ch == 0 else [])
                ldp = em.op("sync", lambda e, c=c0: e.dma_start(
                    out=s_pxt[:, :].rearrange("p (d t) -> p d t", d=NDT),
                    in_=d_xT[:, c:c + 1024]
                    .rearrange("(d p) t -> p d t", p=128)),
                    waits=[("dve", px_free), ("ldx", em.cnt["ldx"])] + bar,
                    inc="ldx", amt=16)
                lds = [ldp] * NDT
                for d in range(NDT):
                    cv = em.op("vector", lambda e, d=d: e.tensor_copy(
                        s_pbf[d % 2][:, :], s_px[d][:, :]),
                        waits=[("ldx", lds[d]), ("pe", pbf_free[d % 2]),
                               ("act", sq_free[d % 2])],
                        inc="dve")
                    sq = em.op("scalar", lambda e, d=d: e.activation(
                        out=s_psq[d % 2][:, :], in_=s_pbf[d % 2][:, :],
                        func=AF.Square),
                        waits=[("dve", cv), ("pe", psq_free[d % 2])],
                        inc="act")
                    sq_free[d % 2] = sq
                    mlast = None
                    for h in range(2):
                        mlast = em.op("tensor", lambda e, d=d, h=h: e.matmul(
                            bank(h), s_ones[:, :],
                            s_pbf[d % 2][:, h * 512:(h + 1) * 512],
                            start=(d == 0), stop=(d == NDT - 1)),
                            waits=([("dve", cv), ("ldc", LDC),
                                    ("dve", chain_free)] if h == 0 else []),
                            inc="pe")
                    pbf_free[d % 2] = mlast
                    for h in range(2):
                        mlast = em.op("tensor", lambda e, d=d, h=h: e.matmul(
                            bank(2 + h), s_ones[:, :],
                            s_psq[d % 2][:, h * 512:(h + 1) * 512],
                            start=(d == 0), stop=(d == NDT - 1)),
                            waits=([("act", sq)] if h == 0 else []),
                            inc="pe")
                    psq_free[d % 2] = mlast
                    mstat_done = mlast
                cmu = em.op("vector", lambda e: e.tensor_copy(
                    s_mus[:, :], p_all[:, 0:1024]),
                    waits=[("pe", mstat_done)], inc="dve")
                em.op("vector", lambda e: e.tensor_tensor(
                    s_var[:, :], s_mus[:, :], s_mus[:, :], op=ALU.mult),
                    inc="dve")
                cvar = em.op("vector", lambda e: e.tensor_tensor(
                    s_var[:, :], p_all[:, 1024:2048], s_var[:, :],
                    op=ALU.subtract), inc="dve")
                chain_free = cvar
                em.op("scalar", lambda e: e.activation(
                    out=s_rb[:, :], in_=s_var[:, :], func=AF.Ln, bias=EPSB()),
                    waits=[("dve", cvar)], inc="act")
                arb = em.op("scalar", lambda e: e.activation(
                    out=s_rb[:, :], in_=s_rb[:, :], func=AF.Exp, scale=-0.5),
                    inc="act")
                for d in range(NDT):
                    t = em.op("vector", lambda e, d=d: e.tensor_tensor(
                        s_t1[:, :], s_px[d][:, :], s_mus[:, :],
                        op=ALU.subtract),
                        waits=[("act", arb)], inc="dve")
                    if d == NDT - 1:
                        px_free = t
                    lc = em.op("vector", lambda e, d=d: e.tensor_tensor(
                        s_l1c[d % 2][:, :], s_t1[:, :], s_rb[:, :],
                        op=ALU.mult),
                        waits=[(f"stln{d % 2}", l1_free[d % 2])],
                        inc="dve")
                    st = em.op("gpsimd", lambda e, d=d, c=c0: e.dma_start(
                        out=d_ln1[d, :, c:c + 1024], in_=s_l1c[d % 2][:, :]),
                        waits=[("dve", lc),
                               (f"stln{d % 2}", em.cnt[f"stln{d % 2}"])],
                        inc=f"stln{d % 2}", amt=16)
                    l1_free[d % 2] = st
        LN1_ST0 = em.cnt["stln0"]
        LN1_ST1 = em.cnt["stln1"]

        with contextlib.ExitStack() as ps:
            s_kT = [ps.enter_context(nc.sbuf_tensor(f"kT{i}", [128, S], BF16))
                    for i in range(2)]
            s_vsq = [ps.enter_context(nc.sbuf_tensor(f"vs{i}", [128, S], BF16))
                     for i in range(2)]
            s_qT = [ps.enter_context(nc.sbuf_tensor(f"qT{i}", [128, SH], BF16))
                    for i in range(2)]
            s_uT = [ps.enter_context(nc.sbuf_tensor(f"uT{i}", [128, SH], BF16))
                    for i in range(2)]
            s_vfm = ps.enter_context(nc.sbuf_tensor("vfm", [128, S], BF16))
            s_rhs = [ps.enter_context(
                nc.sbuf_tensor(f"rh{i}", [128, 4096], BF16)) for i in range(2)]
            s_w1 = ps.enter_context(nc.sbuf_tensor("w1b", [128, 4096], BF16))
            s_att = [ps.enter_context(
                nc.sbuf_tensor(f"at{i}", [128, 1024], BF16)) for i in range(2)]
            s_z = [ps.enter_context(
                nc.sbuf_tensor(f"z{i}", [128, 512], BF16)) for i in range(2)]

            w1_free = 0
            rhs_free = [0, 0]
            unit_free = [0] * 4
            vt_free = 0
            attn_read_done = [0, 0]
            z_free = [0, 0]
            zm_val = 0
            proj_done_act = 0
            proj_done_dve = 0

            s_pstg = [ps.enter_context(
                nc.sbuf_tensor(f"pst{i}", [128, 512], F32)) for i in range(4)]
            NPAIR = 0 if TRUNC < 3 else (NP if TRUNC >= 5 else 1)
            proj_done = {}
            bank7 = [0]
            pstg_free = [0, 0, 0, 0]

            def proj_steps(pair):
                nonlocal w1_free, proj_done
                par = pair % 2
                bar = ([("dve", em.cnt["dve"]), ("act", em.cnt["act"]),
                        ("pe", em.cnt["pe"])] if pair == 0 else [])
                lw = em.op("gpsimd", lambda e, p=pair: e.dma_start(
                    out=s_w1[:, :].rearrange("p (d c) -> p d c", d=NDT),
                    in_=d_w1p[p].rearrange("(d p) c -> p d c", p=128)),
                    waits=[("pe", w1_free), ("ldc", LDC),
                           ("ldw", em.cnt["ldw"])] + bar,
                    inc="ldw", amt=16)
                yield
                g = 0
                kinds_dst = {"k": (3, s_kT[par]), "v": (1, s_vfm),
                             "q": (2, s_qT[par]), "u": (0, s_uT[par])}
                pend_vt = None
                for j in range(17):
                    if j < 16:
                        jp = j % 2
                        lr = em.op("sync", lambda e, j=j, p=jp: e.dma_start(
                            out=s_rhs[p][:, :].rearrange(
                                "p (d t) -> p d t", d=NDT),
                            in_=d_ln1[:, :, j * 512:(j + 1) * 512]
                            .rearrange("d p t -> p d t")),
                            waits=[("pe", rhs_free[jp]), ("stln0", LN1_ST0),
                                   ("stln1", LN1_ST1),
                                   (f"ldr{jp}", em.cnt[f"ldr{jp}"])],
                            inc=f"ldr{jp}", amt=16)
                        yield
                        kinds = (["k", "v", "q", "u"] if j < 8
                                 else ["k", "v"])
                        mm_last = None
                        for kind in kinds:
                            ki, dst = kinds_dst[kind]
                            gp = g % 4
                            for d in range(NDT):
                                w = []
                                if d == 0:
                                    w = [(f"ldr{jp}", lr), ("ldw", lw),
                                         ("dve", bank7[0]),
                                         ("pe", attn_read_done[par])]
                                mm_last = em.op("tensor",
                                    lambda e, d=d, k=ki, p=jp: e.matmul(
                                        bank(7),
                                        s_w1[:, d * 512 + k * 128:
                                             d * 512 + k * 128 + 128],
                                        s_rhs[p][:, d * 512:(d + 1) * 512],
                                        start=(d == 0), stop=(d == NDT - 1)),
                                    waits=w, inc="pe")
                                if d % 2 == 1:
                                    yield
                            cp = em.op("vector", lambda e, gp=gp:
                                       e.tensor_copy(s_pstg[gp][:, :],
                                                     bank(7)),
                                       waits=[("pe", mm_last),
                                              ("act", pstg_free[gp])],
                                       inc="dve")
                            bank7[0] = cp
                            yield
                            bcol = ki * 8 + pair
                            ev = em.op("scalar", lambda e, gp=gp, j=j,
                                       b=bcol, ds=dst: e.activation(
                                           out=ds[:, j * 512:(j + 1) * 512],
                                           in_=s_pstg[gp][:, :],
                                           func=AF.Silu,
                                           bias=s_b1c[:, b:b + 1]),
                                       waits=[("dve", cp),
                                              ("pe", attn_read_done[par])],
                                       inc="act")
                            pstg_free[gp] = ev
                            if kind == "v":
                                vdef = (j, ev)
                            g += 1
                            yield
                        rhs_free[jp] = mm_last
                    if pend_vt is not None:
                        vj, vev = pend_vt
                        tlast = None
                        for c in range(4):
                            tlast = em.op("tensor",
                                lambda e, vj=vj, c=c: e.matmul(
                                    p_all[:, 3584 + c * 128:
                                          3584 + (c + 1) * 128],
                                    s_vfm[:, vj * 512 + c * 128:
                                          vj * 512 + (c + 1) * 128],
                                    s_id[:, :], start=True, stop=True),
                                waits=([("act", vev), ("dve", bank7[0]),
                                        ("ldc", LDC)] if c == 0 else []),
                                inc="pe")
                        yield
                        bank7[0] = em.op("vector", lambda e, vj=vj, p=par:
                                         e.tensor_copy(
                                             s_vsq[p][:, vj * 512:
                                                      (vj + 1) * 512],
                                             bank(7)),
                                         waits=[("pe", tlast),
                                                ("pe", attn_read_done[par])],
                                         inc="dve")
                        yield
                    if j < 16:
                        pend_vt = vdef
                proj_done[pair] = (em.cnt["act"], em.cnt["dve"])
                w1_free = em.cnt["pe"]

            if NPAIR > 0:
                for _ in proj_steps(0):
                    pass
            for pair in range(NPAIR):
                par = pair % 2
                bg = (proj_steps(pair + 1) if pair + 1 < NPAIR else iter(()))
                bg_alive = [True]

                def pop_bg(n):
                    for _ in range(n):
                        if not bg_alive[0]:
                            return
                        try:
                            next(bg)
                        except StopIteration:
                            bg_alive[0] = False
                            return

                if TRUNC < 4:
                    pop_bg(1 << 30)
                    continue
                NU = 512
                qk_val = [0] * NU
                av_val = [0] * NU
                silu_val = [0] * NU
                pd = proj_done[pair]
                for i in range(NU + 2):
                    if i < NU:
                        sl = i % 3
                        qp, kc = i // 64, i % 64
                        w = [("act", silu_val[i - 3] if i >= 3 else 0)]
                        if i == 0:
                            w += [("act", pd[0]), ("dve", pd[1])]
                        em.op("tensor", lambda e, sl=sl, q=qp, k=kc, p=par:
                              e.matmul(
                                  p_all[:, 512 + sl * 1024:
                                        512 + sl * 1024 + 512],
                                  s_kT[p][0:64, k * 128:(k + 1) * 128],
                                  s_qT[p][0:64, q * 512:(q + 1) * 512],
                                  start=True, stop=True,
                                  tile_position=(0, 0)),
                              waits=w)
                        qk_val[i] = em.op("tensor",
                              lambda e, sl=sl, q=qp, k=kc, p=par: e.matmul(
                                  p_all[:, 512 + sl * 1024 + 512:
                                        512 + sl * 1024 + 1024],
                                  s_kT[p][64:128, k * 128:(k + 1) * 128],
                                  s_qT[p][64:128, q * 512:(q + 1) * 512],
                                  start=True, stop=True,
                                  tile_position=(64, 0)),
                              inc="pe")
                    u = i - 2
                    if u >= 0:
                        w = [("pe", qk_val[u]),
                             ("pe", av_val[u - 2] if u >= 2 else 0)]
                        sl = u % 3
                        silu_val[u] = em.op("scalar",
                              lambda e, sl=sl, ab=u % 2: e.activation(
                                  out=s_att[ab][:, :],
                                  in_=p_all[:, 512 + sl * 1024:
                                            512 + sl * 1024 + 1024],
                                  func=AF.Silu),
                              waits=w, inc="act")
                        qp, kc = u // 64, u % 64
                        w = [("act", silu_val[u])]
                        if kc == 0:
                            w.append(("dve", zm_val))
                        em.op("tensor", lambda e, k=kc, ab=u % 2, p=par:
                              e.matmul(
                                  p_all[0:64, 0:512],
                                  s_vsq[p][:, k * 128:k * 128 + 64],
                                  s_att[ab][:, 0:512],
                                  start=(k == 0), stop=(k == 63),
                                  tile_position=(0, 0)),
                              waits=w)
                        av_val[u] = em.op("tensor",
                              lambda e, k=kc, ab=u % 2, p=par: e.matmul(
                                  p_all[64:128, 0:512],
                                  s_vsq[p][:, k * 128 + 64:k * 128 + 128],
                                  s_att[ab][:, 512:1024],
                                  start=(k == 0), stop=(k == 63),
                                  tile_position=(0, 64)),
                              inc="pe")
                        if kc == 63:
                            zp = (qp % 2)
                            zm_val = em.op("vector",
                                  lambda e, q=qp, p=par, zp=zp:
                                  e.tensor_tensor(
                                      s_z[zp][:, :], p_all[:, 0:512],
                                      s_uT[p][:, q * 512:(q + 1) * 512],
                                      op=ALU.mult),
                                  waits=[("pe", av_val[u]),
                                         (f"stz{zp}", z_free[zp])],
                                  inc="dve")
                            z_free[zp] = em.op("gpsimd",
                                  lambda e, q=qp, pr=pair, zp=zp:
                                  e.dma_start(
                                      out=d_zscr[pr, :, q * 512:(q + 1) * 512],
                                      in_=s_z[zp][:, :]),
                                  waits=[("dve", zm_val),
                                         (f"stz{zp}", em.cnt[f"stz{zp}"])],
                                  inc=f"stz{zp}", amt=16)
                    if i % 2 == 1:
                        pop_bg(1)
                pop_bg(1 << 30)
                attn_read_done[par] = em.cnt["pe"]
        Z_ALL0 = em.cnt["stz0"]
        Z_ALL1 = em.cnt["stz1"]


        if TRUNC >= 6:
            with contextlib.ExitStack() as ts:
                s_w2 = ts.enter_context(nc.sbuf_tensor("w2b", [128, 8192], BF16))
                s_c2 = ts.enter_context(nc.sbuf_tensor("c2", [128, 8], F32))
                s_zc = [ts.enter_context(
                    nc.sbuf_tensor(f"zc{i}", [128, 4096], BF16)) for i in range(2)]
                s_fscr = [ts.enter_context(
                    nc.sbuf_tensor(f"fs{i}", [128, 4096], BF16)) for i in range(2)]
                s_zln = [ts.enter_context(
                    nc.sbuf_tensor(f"zl{i}", [128, 4096], BF16)) for i in range(2)]
                s_xc = ts.enter_context(nc.sbuf_tensor("xc0", [128, 4096], F32))
                s_bscr = ts.enter_context(nc.sbuf_tensor("bsc", [128, 4096], BF16))
                s_y = ts.enter_context(nc.sbuf_tensor("yb", [128, 4096], BF16))
                s_ysq = ts.enter_context(nc.sbuf_tensor("ysq", [128, 4096], BF16))
                s_x1 = ts.enter_context(nc.sbuf_tensor("x1", [128, 4096], F32))
                s_x1b = ts.enter_context(nc.sbuf_tensor("x1b", [128, 4096], BF16))
                s_h = ts.enter_context(nc.sbuf_tensor("hb", [128, 4096], BF16))
                s_mag = ts.enter_context(nc.sbuf_tensor("mag", [128, 4096], BF16))
                s_m2 = ts.enter_context(nc.sbuf_tensor("m2", [128, 4096], BF16))
                s_sqa = ts.enter_context(nc.sbuf_tensor("sqa", [128, 512], F32))
                s_sqb = ts.enter_context(nc.sbuf_tensor("sqb", [128, 512], F32))
                s_mu = [ts.enter_context(
                    nc.sbuf_tensor("mu0", [128, 512], F32))] * 3
                s_vr = [ts.enter_context(
                    nc.sbuf_tensor("vr0", [128, 512], F32))] * 3
                _r3 = [ts.enter_context(
                    nc.sbuf_tensor(f"r{i}", [128, 512], F32)) for i in range(3)]
                _mr3 = [ts.enter_context(
                    nc.sbuf_tensor(f"mr{i}", [128, 512], F32)) for i in range(3)]
                s_r = _r3 + [_r3[1]]
                s_mr = _mr3 + [_mr3[1]]

                lw2 = em.op("gpsimd", lambda e: e.dma_start(
                    out=s_w2[:, :].rearrange("p (d c) -> p d c", d=NDT),
                    in_=d_w2.rearrange("(d p) c -> p d c", p=128)),
                    waits=[("ldc", LDC), ("ldw", em.cnt["ldw"]),
                           ("pe", em.cnt["pe"]), ("act", em.cnt["act"]),
                           ("dve", em.cnt["dve"])],
                    inc="ldw", amt=16)
                c2mm = None
                for f in range(8):
                    for d in range(NDT):
                        c2mm = em.op("tensor", lambda e, f=f, d=d: e.matmul(
                            p_all[:, 3072 + f:3072 + f + 1],
                            s_w2[:, d * 1024 + f * 128:d * 1024 + (f + 1) * 128],
                            s_betabf[:, d:d + 1],
                            start=(d == 0), stop=(d == NDT - 1)),
                            waits=([("ldw", lw2), ("ldc", LDC),
                                    ("act", em.cnt["act"]),
                                    ("dve", em.cnt["dve"])]
                                   if f == 0 and d == 0 else []),
                            inc="pe")
                c2dve = em.op("vector", lambda e: e.tensor_tensor(
                    s_c2[:, :], p_all[:, 3072:3080], s_gb[:, 16:24], op=ALU.add),
                    waits=[("pe", c2mm)], inc="dve")
                gma = None
                for d in range(NDT):
                    gma = em.op("vector", lambda e, d=d: e.tensor_scalar(
                        s_w2[:, d * 1024:(d + 1) * 1024],
                        s_w2[:, d * 1024:(d + 1) * 1024],
                        s_gb[:, d:d + 1], None, op0=ALU.mult),
                        waits=[("pe", c2mm)], inc="dve")
                W2RDY = gma

                zc_free = [(em.cnt["pe"], em.cnt["dve"])] * 2
                fscr_free = [em.cnt["act"], em.cnt["act"]]
                fscr_free = [em.cnt["dve"]] * 2
                zln_free = [em.cnt["pe"]] * 2
                bank_free = [c2dve, c2dve]
                xc_free = em.cnt["dve"]
                x2_done = 0
                stv = 0

                def rstd_chain(mwait, base, cpar, rix, extra=0):
                    em.op("vector", lambda e, b=base: e.tensor_copy(
                        s_mu[cpar][:, :], bank(b)),
                        waits=[("pe", mwait)], inc="dve")
                    em.op("vector", lambda e: e.tensor_tensor(
                        s_vr[cpar][:, :], s_mu[cpar][:, :], s_mu[cpar][:, :],
                        op=ALU.mult), inc="dve")
                    vfree = em.op("vector", lambda e, b=base: e.tensor_tensor(
                        s_vr[cpar][:, :], bank(b + 1), s_vr[cpar][:, :],
                        op=ALU.subtract), inc="dve")
                    em.op("scalar", lambda e: e.activation(
                        out=s_r[rix][:, :], in_=s_vr[cpar][:, :], func=AF.Ln,
                        bias=EPSB()), waits=[("dve", vfree), ("dve", extra)],
                        inc="act")
                    ar = em.op("scalar", lambda e: e.activation(
                        out=s_r[rix][:, :], in_=s_r[rix][:, :], func=AF.Exp,
                        scale=-0.5), inc="act")
                    mrv = em.op("vector", lambda e: e.tensor_tensor(
                        s_mr[rix][:, :], s_mu[cpar][:, :], s_r[rix][:, :],
                        op=ALU.mult), waits=[("act", ar)], inc="dve")
                    return vfree, mrv

                fstate = {}

                def front(n):
                    c0 = n * 512
                    par = n % 2
                    base = par * 2
                    lz = em.op("sync", lambda e, c=c0, p=par: e.dma_start(
                        out=s_zc[p][:, :].rearrange("p (d t) -> p d t", d=NP),
                        in_=d_zscr[:, :, c:c + 512].rearrange("d p t -> p d t")),
                        waits=[("stz0", Z_ALL0), ("stz1", Z_ALL1),
                               ("pe", zc_free[par][0]), ("dve", zc_free[par][1]),
                               (f"ldz{par}", em.cnt[f"ldz{par}"])],
                        inc=f"ldz{par}", amt=16)
                    zq = em.op("scalar", lambda e, p=par: e.activation(
                        out=s_fscr[p][:, :], in_=s_zc[p][:, :], func=AF.Square),
                        waits=[(f"ldz{par}", lz), ("dve", fscr_free[par])],
                        inc="act")
                    for d in range(NDT):
                        em.op("tensor", lambda e, d=d, p=par, b=base: e.matmul(
                            bank(b), s_ones[:, :],
                            s_zc[p][:, d * 512:(d + 1) * 512],
                            start=(d == 0), stop=(d == NDT - 1)),
                            waits=([(f"ldz{par}", lz), ("dve", bank_free[par])]
                                   if d == 0 else []))
                    st2 = None
                    for d in range(NDT):
                        st2 = em.op("tensor", lambda e, d=d, p=par, b=base:
                                    e.matmul(
                                        bank(b + 1), s_ones[:, :],
                                        s_fscr[p][:, d * 512:(d + 1) * 512],
                                        start=(d == 0), stop=(d == NDT - 1)),
                                    waits=([("act", zq)] if d == 0 else []),
                                    inc="pe")
                    bfree, mr2 = rstd_chain(st2, base, 2, 0)
                    bank_free[par] = bfree
                    zl = None
                    for d in range(NDT):
                        em.op("vector", lambda e, d=d, p=par: e.tensor_tensor(
                            s_fscr[p][:, d * 512:(d + 1) * 512],
                            s_zc[p][:, d * 512:(d + 1) * 512], s_r[0][:, :],
                            op=ALU.mult),
                            waits=[("dve", mr2), ("pe", zln_free[par])],
                            inc="dve")
                        zl = em.op("vector", lambda e, d=d, p=par: e.tensor_tensor(
                            s_zln[p][:, d * 512:(d + 1) * 512],
                            s_fscr[p][:, d * 512:(d + 1) * 512], s_mr[0][:, :],
                            op=ALU.subtract), inc="dve")
                    fscr_free[par] = zl
                    fstate[n] = (zl, lz)

                front(0)
                for n in range(TAILN):
                    c0 = n * 512
                    par = n % 2
                    base = par * 2
                    zl, lz = fstate[n]
                    ysq_l = None
                    ypar_dve = [0, 0]
                    for f in range(8):
                        mm = None
                        for d in range(NDT):
                            w = []
                            if d == 0:
                                w = [("dve", zl), ("dve", W2RDY),
                                     ("dve", bank_free[par]),
                                     ("dve", ypar_dve[f % 2])]
                            mm = em.op("tensor", lambda e, f=f, d=d, p=par,
                                       b=base: e.matmul(
                                           bank(b + f % 2),
                                           s_w2[:, d * 1024 + f * 128:
                                                d * 1024 + (f + 1) * 128],
                                           s_zln[p][:, d * 512:(d + 1) * 512],
                                           start=(d == 0), stop=(d == NDT - 1)),
                                       waits=w, inc="pe")
                        if f == 7:
                            zc_free[par] = (mm, zl)
                            zln_free[par] = mm
                        yev = em.op("vector", lambda e, f=f, b=base:
                                    e.tensor_tensor(
                                        s_y[:, f * 512:(f + 1) * 512],
                                        bank(b + f % 2),
                                        s_c2[:, f:f + 1].broadcast_to([128, 512]),
                                        op=ALU.add),
                                    waits=[("pe", mm), ("dve", c2dve)], inc="dve")
                        ysq_l = em.op("scalar", lambda e, f=f: e.activation(
                            out=s_ysq[:, f * 512:(f + 1) * 512],
                            in_=s_y[:, f * 512:(f + 1) * 512],
                            func=AF.Square),
                            waits=[("dve", yev)], inc="act")
                        ypar_dve[f % 2] = yev
                    for f in range(8):
                        em.op("tensor", lambda e, f=f, b=base: e.matmul(
                            bank(b), s_ones[:, :], s_y[:, f * 512:(f + 1) * 512],
                            start=(f == 0), stop=(f == 7)),
                            waits=([("dve", yev), ("dve", ypar_dve[0]),
                                    ("dve", ypar_dve[1])] if f == 0 else []))
                    sty = None
                    for f in range(8):
                        sty = em.op("tensor", lambda e, f=f, b=base: e.matmul(
                            bank(b + 1), s_ones[:, :],
                            s_ysq[:, f * 512:(f + 1) * 512],
                            start=(f == 0), stop=(f == 7)),
                            waits=([("act", ysq_l)] if f == 0 else []), inc="pe")
                    bfree, mry = rstd_chain(sty, base, 1, 1, extra=x2_done)
                    lx = em.op("sync", lambda e, c=c0: e.dma_start(
                        out=s_xc[:, :].rearrange("p (d t) -> p d t", d=NDT),
                        in_=d_xT[:, c:c + 512]
                        .rearrange("(d p) t -> p d t", p=128)),
                        waits=[("dve", xc_free), ("ldxc", em.cnt["ldxc"]),
                               ("stout", em.cnt["stout"])],
                        inc="ldxc", amt=16)
                    x1b_l = None
                    x1sq_l = None
                    for f in range(8):
                        em.op("vector", lambda e, f=f: e.tensor_tensor(
                            s_bscr[:, f * 512:(f + 1) * 512],
                            s_y[:, f * 512:(f + 1) * 512], s_r[1][:, :],
                            op=ALU.mult),
                            waits=[("dve", mry), ("ldxc", lx)], inc="dve")
                        em.op("vector", lambda e, f=f: e.tensor_tensor(
                            s_bscr[:, f * 512:(f + 1) * 512],
                            s_bscr[:, f * 512:(f + 1) * 512], s_mr[1][:, :],
                            op=ALU.subtract), inc="dve")
                        x1v = em.op("vector", lambda e, f=f: e.tensor_tensor(
                            s_x1[:, f * 512:(f + 1) * 512],
                            s_bscr[:, f * 512:(f + 1) * 512],
                            s_xc[:, f * 512:(f + 1) * 512],
                            op=ALU.add), inc="dve")
                        x1b_l = em.op("scalar", lambda e, f=f: e.activation(
                            out=s_x1b[:, f * 512:(f + 1) * 512],
                            in_=s_x1[:, f * 512:(f + 1) * 512], func=AF.Copy),
                            waits=[("dve", x1v)], inc="act")
                        x1sq_l = em.op("scalar", lambda e, f=f: e.activation(
                            out=s_ysq[:, f * 512:(f + 1) * 512],
                            in_=s_x1[:, f * 512:(f + 1) * 512], func=AF.Square),
                            inc="act")
                    xc_free = em.cnt["dve"]
                    for f in range(8):
                        em.op("tensor", lambda e, f=f, b=base: e.matmul(
                            bank(b), s_ones[:, :],
                            s_x1b[:, f * 512:(f + 1) * 512],
                            start=(f == 0), stop=(f == 7)),
                            waits=([("act", x1b_l), ("dve", bfree)]
                                   if f == 0 else []))
                    st3 = None
                    for f in range(8):
                        st3 = em.op("tensor", lambda e, f=f, b=base: e.matmul(
                            bank(b + 1), s_ones[:, :],
                            s_ysq[:, f * 512:(f + 1) * 512],
                            start=(f == 0), stop=(f == 7)),
                            waits=([("act", x1sq_l)] if f == 0 else []), inc="pe")
                    bfree, mr3 = rstd_chain(st3, base, 0, 2)
                    hl = None
                    for f in range(8):
                        em.op("vector", lambda e, f=f: e.tensor_tensor(
                            s_bscr[:, f * 512:(f + 1) * 512],
                            s_x1b[:, f * 512:(f + 1) * 512], s_r[2][:, :],
                            op=ALU.mult), waits=[("dve", mr3)], inc="dve")
                        hl = em.op("vector", lambda e, f=f: e.tensor_tensor(
                            s_h[:, f * 512:(f + 1) * 512],
                            s_bscr[:, f * 512:(f + 1) * 512], s_mr[2][:, :],
                            op=ALU.subtract), inc="dve")
                    bank_free[par] = bfree
                    if n + 1 < TAILN:
                        front(n + 1)
                    ri_free = [0, 0, 0, 0]
                    m2_l = 0
                    mag_l = None
                    for fo in range(8):
                        rb_ = 4 + (fo % 2) * 2
                        ib_ = 5 + (fo % 2) * 2
                        mmr = None
                        for d in range(NDT):
                            w = []
                            if d == 0:
                                w = [("dve", hl), ("act", ri_free[2 * (fo % 2)])]
                            mmr = em.op("tensor", lambda e, d=d, f=fo, b=rb_:
                                        e.matmul(
                                            bank(b),
                                            s_cos[:, d * 1024 + f * 128:
                                                  d * 1024 + (f + 1) * 128],
                                            s_h[:, d * 512:(d + 1) * 512],
                                            start=(d == 0), stop=(d == NDT - 1)),
                                        waits=w, inc="pe")
                        mmi = None
                        for d in range(NDT):
                            w = []
                            if d == 0:
                                w = [("act", ri_free[2 * (fo % 2) + 1])]
                            mmi = em.op("tensor", lambda e, d=d, f=fo, b=ib_:
                                        e.matmul(
                                            bank(b),
                                            s_sin[:, d * 1024 + f * 128:
                                                  d * 1024 + (f + 1) * 128],
                                            s_h[:, d * 512:(d + 1) * 512],
                                            start=(d == 0), stop=(d == NDT - 1)),
                                        waits=w, inc="pe")
                        sa = em.op("scalar", lambda e, b=rb_: e.activation(
                            out=s_sqa[:, :], in_=bank(b), func=AF.Square),
                            waits=[("pe", mmr), ("dve", m2_l)], inc="act")
                        ri_free[2 * (fo % 2)] = sa
                        sb_ = em.op("scalar", lambda e, b=ib_: e.activation(
                            out=s_sqb[:, :], in_=bank(b), func=AF.Square),
                            waits=[("pe", mmi)], inc="act")
                        ri_free[2 * (fo % 2) + 1] = sb_
                        m2_l = em.op("vector", lambda e, f=fo: e.tensor_tensor(
                            s_m2[:, f * 512:(f + 1) * 512], s_sqa[:, :],
                            s_sqb[:, :], op=ALU.add),
                            waits=[("act", sb_)], inc="dve")
                        em.op("scalar", lambda e, f=fo: e.activation(
                            out=s_mag[:, f * 512:(f + 1) * 512],
                            in_=s_m2[:, f * 512:(f + 1) * 512], func=AF.Ln,
                            bias=TINY()), waits=[("dve", m2_l)], inc="act")
                        mag_l = em.op("scalar", lambda e, f=fo: e.activation(
                            out=s_mag[:, f * 512:(f + 1) * 512],
                            in_=s_mag[:, f * 512:(f + 1) * 512], func=AF.Exp,
                            scale=0.5), inc="act")
                    for f in range(8):
                        em.op("tensor", lambda e, f=f, b=base: e.matmul(
                            bank(b), s_ones[:, :],
                            s_mag[:, f * 512:(f + 1) * 512],
                            start=(f == 0), stop=(f == 7)),
                            waits=([("act", mag_l), ("dve", bank_free[par])]
                                   if f == 0 else []))
                    st4 = None
                    for f in range(8):
                        st4 = em.op("tensor", lambda e, f=f, b=base: e.matmul(
                            bank(b + 1), s_ones[:, :],
                            s_m2[:, f * 512:(f + 1) * 512],
                            start=(f == 0), stop=(f == 7)),
                            waits=([("dve", m2_l)] if f == 0 else []), inc="pe")
                    bfree, mr4 = rstd_chain(st4, base, 1, 3)
                    bank_free[par] = bfree
                    prev_st = stv
                    xl = None
                    for f in range(8):
                        em.op("vector", lambda e, f=f: e.tensor_tensor(
                            s_bscr[:, f * 512:(f + 1) * 512],
                            s_mag[:, f * 512:(f + 1) * 512], s_r[3][:, :],
                            op=ALU.mult),
                            waits=[("dve", mr4), ("stout", prev_st)], inc="dve")
                        em.op("vector", lambda e, f=f: e.tensor_tensor(
                            s_bscr[:, f * 512:(f + 1) * 512],
                            s_bscr[:, f * 512:(f + 1) * 512], s_mr[3][:, :],
                            op=ALU.subtract), inc="dve")
                        xl = em.op("vector", lambda e, f=f: e.tensor_tensor(
                            s_xc[:, f * 512:(f + 1) * 512],
                            s_bscr[:, f * 512:(f + 1) * 512],
                            s_x1[:, f * 512:(f + 1) * 512], op=ALU.add),
                            inc="dve")
                    x2_done = xl
                    stv = em.op("gpsimd", lambda e, c=c0: e.dma_start(
                        out=d_out[:, c:c + 512].rearrange("(f p) t -> p f t",
                                                          p=128),
                        in_=s_xc[:, :].rearrange("p (f t) -> p f t", f=NDT)),
                        waits=[("dve", xl), ("stout", em.cnt["stout"])],
                        inc="stout", amt=16)
                em.op("gpsimd", lambda e: e.nop(),
                      waits=[("stout", em.cnt["stout"])])
                if int(_os.environ.get("KEXTRA2", "0")):
                    em.op("tensor", lambda e: e.matmul(
                        bank(2), s_cos[:, 0:128], s_cos[:, 0:512],
                        start=True, stop=True), inc="pe")
                if TAILN < 8:
                    waits = [(s_, em.cnt[s_]) for s_ in em.cnt]
                    _mk = _os.environ.get("KMARK", "cos")
                    msrc = {"cos": s_cos[:, 0:4096], "zln": s_zln[0][:, :],
                            "zc": s_zc[0][:, :], "w2": s_w2[:, 0:4096],
                            "r0": s_r[0][:, :], "mr0": s_mr[0][:, :]}[_mk]
                    em.op("gpsimd", lambda e, m=msrc: e.dma_start(
                        out=d_out[0:128, 0:m.shape[-1]], in_=m),
                        waits=waits, inc="stout", amt=16)
                    em.op("gpsimd", lambda e: e.nop(),
                          waits=[("stout", em.cnt["stout"])])
        else:
            waits = [(s_, em.cnt[s_]) for s_ in em.cnt]
            em.op("gpsimd", lambda e: e.dma_start(
                out=d_out[0:128, :], in_=s_cos[:, 0:4096]),
                waits=waits, inc="stout", amt=16)
            em.op("gpsimd", lambda e: e.nop(),
                  waits=[("stout", em.cnt["stout"])])

        for name in em.cnt:
            sems[name] = es.enter_context(nc.semaphore(name))
        with nc.Block() as block:
            def mk(eng_name):
                def body(eng):
                    for waits, fn, inc, amt in em.ops[eng_name]:
                        for sname, val in waits:
                            eng.wait_ge(sems[sname], val)
                        r = fn(eng)
                        if inc is not None:
                            assert r is not None, f"{eng_name} inc={inc}"
                            r.then_inc(sems[inc], amt)
                return body
            block.sync(mk("sync"))
            block.gpsimd(mk("gpsimd"))
            block.tensor(mk("tensor"))
            block.scalar(mk("scalar"))
            block.vector(mk("vector"))

    return nc


def kernel(L0_attn_x, L0_attn_W1, L0_attn_b1, L0_attn_W2, L0_attn_b2,
           L0_attn_gamma, L0_attn_beta, L0_ffn_theta, L0_ffn_phi,
           L0_ffn_theta_noise, L0_ffn_phi_noise):
    x = np.asarray(L0_attn_x, np.float32)
    W1 = np.asarray(L0_attn_W1, np.float32)
    b1 = np.asarray(L0_attn_b1, np.float32)
    W2 = np.ascontiguousarray(np.asarray(L0_attn_W2, np.float32))
    b2 = np.asarray(L0_attn_b2, np.float32)
    gamma = np.asarray(L0_attn_gamma, np.float32)
    beta = np.asarray(L0_attn_beta, np.float32)
    th = np.ascontiguousarray(np.asarray(L0_ffn_theta, np.float32))
    thn = np.ascontiguousarray(np.asarray(L0_ffn_theta_noise, np.float32))
    ph = np.ascontiguousarray(np.asarray(L0_ffn_phi, np.float32))
    phn = np.ascontiguousarray(np.asarray(L0_ffn_phi_noise, np.float32))

    xT = np.ascontiguousarray(x.transpose(0, 2, 1))

    w1p = np.empty((NP, D, 512), np.float32)
    for p in range(NP):
        for k in range(4):
            w1p[p, :, k * 128:(k + 1) * 128] = \
                W1[:, k * D + p * 128:k * D + (p + 1) * 128]
    b1c = np.ascontiguousarray(
        b1.reshape(4, 8, 128).transpose(2, 0, 1).reshape(128, 32))
    gb = np.ascontiguousarray(np.concatenate(
        [gamma.reshape(8, 128).T, beta.reshape(8, 128).T,
         b2.reshape(8, 128).T], axis=1))
    consts = np.concatenate(
        [np.eye(128, dtype=np.float32) / S,
         np.ones((128, 128), np.float32) / D], axis=1).astype(BF)
    cb = np.tile(np.array([[math.pi / 2, EPS, 1e-30, 1024.0,
                            1.0, 0.0, 0.0, 0.0]], np.float32), (128, 1))

    nc = build_nc()
    in_maps = []
    for c in range(8):
        b, half = c // 2, c % 2
        off = half * SH
        xrot = np.ascontiguousarray(
            np.concatenate([xT[b][:, off:], xT[b][:, :off]], axis=1))
        in_maps.append(dict(
            xT=xrot, w1p=w1p, b1c=b1c, gb=gb, consts=consts, cb=cb,
            w2=W2, th=th, thn=thn, ph=ph, phn=phn))
    res = run_bass_kernel_spmd(nc, in_maps, core_ids=list(range(8)))
    out = np.empty_like(x)
    for c in range(8):
        b, half = c // 2, c % 2
        out[b, half * SH:(half + 1) * SH, :] = res.results[c]["out"].T
    return out


if __name__ == "__main__":
    import reference as _r
    inp = _r.setup_inputs(0)
    want = np.asarray(_r.reference(**inp))
    got = kernel(**{k: np.asarray(v) for k, v in inp.items()})
    err = got - want
    rvr = float((err ** 2).mean() / (want ** 2).mean())
    print("rvr", rvr, "max", float(np.abs(err).max()))


# version 09
# speedup vs baseline: 1.0543x; 1.0047x over previous
import contextlib
import math
import numpy as np
import ml_dtypes
import concourse.bass as bass
import concourse.mybir as mybir
from concourse.bass_utils import run_bass_kernel_spmd

F32 = mybir.dt.float32
BF16 = mybir.dt.bfloat16
AF = mybir.ActivationFunctionType
ALU = mybir.AluOpType
BF = ml_dtypes.bfloat16

D = 1024
S = 8192
SH = 4096
NDT = 8
NP = 8
EPS = 1e-5

ENGINES = ("sync", "gpsimd", "tensor", "scalar", "vector")
SELF_SEM = {"vector": "dve", "scalar": "act"}
import os as _os
TRUNC = int(_os.environ.get("KTRUNC", "6"))
TAILN = int(_os.environ.get("KTAILN", "8"))
KC2 = _os.environ.get("KC2", "1") == "1"
KGM = _os.environ.get("KGM", "1") == "1"
KW2 = _os.environ.get("KW2", "1") == "1"
KSTEP = int(_os.environ.get("KSTEP", "99"))
KYALT = int(_os.environ.get("KYALT", "0"))
KYB = int(_os.environ.get("KYB", "2"))
KYF = int(_os.environ.get("KYF", "8"))
KYSRC = _os.environ.get("KYSRC", "w2")
KYRHS = _os.environ.get("KYRHS", "zln")
KYW = _os.environ.get("KYW", "1") == "1"
KYD = int(_os.environ.get("KYD", "8"))
KYN = int(_os.environ.get("KYN", "512"))


class Em:

    def __init__(self):
        self.ops = {e: [] for e in ENGINES}
        self.cnt = {}
        self.last_wait = {e: {} for e in ENGINES}

    def sem(self, name):
        self.cnt.setdefault(name, 0)

    def op(self, eng, fn, waits=(), inc=None, amt=1):
        waits = list(waits)
        if eng in SELF_SEM:
            assert inc == SELF_SEM[eng], f"{eng} op must inc {SELF_SEM[eng]}"
            waits.append((SELF_SEM[eng], self.cnt[SELF_SEM[eng]]))
        w = []
        for sname, val in waits:
            assert val <= self.cnt[sname], (
                f"deadlock: {eng} waits {sname}>={val} > {self.cnt[sname]}")
            if val <= 0 or self.last_wait[eng].get(sname, -1) >= val:
                continue
            self.last_wait[eng][sname] = val
            w.append((sname, val))
        v = None
        if inc is not None:
            self.cnt[inc] += amt
            v = self.cnt[inc]
        self.ops[eng].append((w, fn, inc, amt))
        return v


def build_nc():
    nc = bass.Bass("TRN2", target_bir_lowering=False)

    d_xT = nc.dram_tensor("xT", [D, S], F32, kind="ExternalInput")
    d_w1p = nc.dram_tensor("w1p", [NP, D, 512], F32, kind="ExternalInput")
    d_b1c = nc.dram_tensor("b1c", [128, 32], F32, kind="ExternalInput")
    d_gb = nc.dram_tensor("gb", [128, 24], F32, kind="ExternalInput")
    d_consts = nc.dram_tensor("consts", [128, 256], BF16, kind="ExternalInput")
    d_cb = nc.dram_tensor("cb", [128, 8], F32, kind="ExternalInput")
    d_w2 = nc.dram_tensor("w2", [D, D], F32, kind="ExternalInput")
    d_th = nc.dram_tensor("th", [D, D], F32, kind="ExternalInput")
    d_thn = nc.dram_tensor("thn", [D, D], F32, kind="ExternalInput")
    d_ph = nc.dram_tensor("ph", [D, D], F32, kind="ExternalInput")
    d_phn = nc.dram_tensor("phn", [D, D], F32, kind="ExternalInput")
    d_out = nc.dram_tensor("out", [D, SH], F32, kind="ExternalOutput")
    d_ln1 = nc.dram_tensor("ln1scr", [NDT, 128, S], BF16)
    d_zscr = nc.dram_tensor("zscr", [NP, 128, SH], BF16)

    em = Em()
    for s in ("pe", "act", "dve", "ldc", "ldta", "ldtb", "ldx", "stln0",
              "stln1", "ldw", "ldr0", "ldr1", "stz0", "stz1", "ldz0", "ldz1",
              "ldxc", "stout"):
        em.sem(s)

    with contextlib.ExitStack() as es:
        s_id = es.enter_context(nc.sbuf_tensor("c_id", [128, 128], BF16))
        s_ones = es.enter_context(nc.sbuf_tensor("c_ones", [128, 128], BF16))
        s_b1c = es.enter_context(nc.sbuf_tensor("c_b1c", [128, 32], F32))
        s_gb = es.enter_context(nc.sbuf_tensor("c_gb", [128, 24], F32))
        s_cb = es.enter_context(nc.sbuf_tensor("c_cb", [128, 8], F32))
        s_betabf = es.enter_context(nc.sbuf_tensor("c_bb", [128, 8], BF16))
        s_cos = es.enter_context(nc.sbuf_tensor("cosm", [128, 8192], BF16))
        s_sin = es.enter_context(nc.sbuf_tensor("sinm", [128, 8192], BF16))
        p_all = es.enter_context(nc.psum_tensor("p_all", [128, 4096], F32))
        sems = {}

        def bank(i, w=512):
            return p_all[:, i * 512:i * 512 + w]

        for dst_, src_ in ((s_id, lambda: d_consts[:, 0:128]),
                           (s_ones, lambda: d_consts[:, 128:256]),
                           (s_b1c, lambda: d_b1c[:, :]),
                           (s_gb, lambda: d_gb[:, :]),
                           (s_cb, lambda: d_cb[:, :])):
            em.op("sync", lambda e, d=dst_, s=src_: e.dma_start(
                out=d[:, :], in_=s()), waits=[("ldc", em.cnt["ldc"])],
                inc="ldc", amt=16)
        em.op("gpsimd", lambda e: e.dma_start(out=s_betabf[:, :],
              in_=d_gb[:, 8:16]), waits=[("ldw", em.cnt["ldw"])],
              inc="ldw", amt=16)
        LDC = em.cnt["ldc"]
        PI2 = lambda: s_cb[:, 0:1]
        EPSB = lambda: s_cb[:, 1:2]
        TINY = lambda: s_cb[:, 2:3]
        K1024 = lambda: s_cb[:, 3:4]
        ONE = lambda: s_cb[:, 4:5]

        with contextlib.ExitStack() as ps:
            s_pxt = ps.enter_context(nc.sbuf_tensor("pxt", [128, 8192], F32))
            s_px = [s_pxt[:, d * 1024:(d + 1) * 1024] for d in range(NDT)]
            s_pbf = [ps.enter_context(
                nc.sbuf_tensor(f"pb{i}", [128, 1024], BF16)) for i in range(2)]
            s_psq = [ps.enter_context(
                nc.sbuf_tensor(f"pq{i}", [128, 1024], BF16)) for i in range(2)]
            s_mus = ps.enter_context(nc.sbuf_tensor("mus", [128, 1024], F32))
            s_var = ps.enter_context(nc.sbuf_tensor("var", [128, 1024], F32))
            s_rb = ps.enter_context(nc.sbuf_tensor("rb", [128, 1024], F32))
            s_t1 = ps.enter_context(nc.sbuf_tensor("t1", [128, 1024], F32))
            s_l1c = [ps.enter_context(
                nc.sbuf_tensor(f"l1{i}", [128, 1024], BF16)) for i in range(2)]
            px_free = 0
            pbf_free = [0, 0]
            sq_free = [0, 0]
            psq_free = [0, 0]
            l1_free = [0, 0]
            chain_free = 0
            mstat_done = 0
            for ch in range(8 if TRUNC >= 2 else 0):
                c0 = ch * 1024
                bar = ([("dve", em.cnt["dve"]), ("act", em.cnt["act"])]
                       if ch == 0 else [])
                ldp = em.op("sync", lambda e, c=c0: e.dma_start(
                    out=s_pxt[:, :].rearrange("p (d t) -> p d t", d=NDT),
                    in_=d_xT[:, c:c + 1024]
                    .rearrange("(d p) t -> p d t", p=128)),
                    waits=[("dve", px_free), ("ldx", em.cnt["ldx"])] + bar,
                    inc="ldx", amt=16)
                lds = [ldp] * NDT
                for d in range(NDT):
                    cv = em.op("vector", lambda e, d=d: e.tensor_copy(
                        s_pbf[d % 2][:, :], s_px[d][:, :]),
                        waits=[("ldx", lds[d]), ("pe", pbf_free[d % 2]),
                               ("act", sq_free[d % 2])],
                        inc="dve")
                    sq = em.op("scalar", lambda e, d=d: e.activation(
                        out=s_psq[d % 2][:, :], in_=s_pbf[d % 2][:, :],
                        func=AF.Square),
                        waits=[("dve", cv), ("pe", psq_free[d % 2])],
                        inc="act")
                    sq_free[d % 2] = sq
                    mlast = None
                    for h in range(2):
                        mlast = em.op("tensor", lambda e, d=d, h=h: e.matmul(
                            bank(h), s_ones[:, :],
                            s_pbf[d % 2][:, h * 512:(h + 1) * 512],
                            start=(d == 0), stop=(d == NDT - 1)),
                            waits=([("dve", cv), ("ldc", LDC),
                                    ("dve", chain_free)] if h == 0 else []),
                            inc="pe")
                    pbf_free[d % 2] = mlast
                    for h in range(2):
                        mlast = em.op("tensor", lambda e, d=d, h=h: e.matmul(
                            bank(2 + h), s_ones[:, :],
                            s_psq[d % 2][:, h * 512:(h + 1) * 512],
                            start=(d == 0), stop=(d == NDT - 1)),
                            waits=([("act", sq)] if h == 0 else []),
                            inc="pe")
                    psq_free[d % 2] = mlast
                    mstat_done = mlast
                cmu = em.op("vector", lambda e: e.tensor_copy(
                    s_mus[:, :], p_all[:, 0:1024]),
                    waits=[("pe", mstat_done)], inc="dve")
                em.op("vector", lambda e: e.tensor_tensor(
                    s_var[:, :], s_mus[:, :], s_mus[:, :], op=ALU.mult),
                    inc="dve")
                cvar = em.op("vector", lambda e: e.tensor_tensor(
                    s_var[:, :], p_all[:, 1024:2048], s_var[:, :],
                    op=ALU.subtract), inc="dve")
                chain_free = cvar
                em.op("scalar", lambda e: e.activation(
                    out=s_rb[:, :], in_=s_var[:, :], func=AF.Ln, bias=EPSB()),
                    waits=[("dve", cvar)], inc="act")
                arb = em.op("scalar", lambda e: e.activation(
                    out=s_rb[:, :], in_=s_rb[:, :], func=AF.Exp, scale=-0.5),
                    inc="act")
                for d in range(NDT):
                    t = em.op("vector", lambda e, d=d: e.tensor_tensor(
                        s_t1[:, :], s_px[d][:, :], s_mus[:, :],
                        op=ALU.subtract),
                        waits=[("act", arb)], inc="dve")
                    if d == NDT - 1:
                        px_free = t
                    lc = em.op("vector", lambda e, d=d: e.tensor_tensor(
                        s_l1c[d % 2][:, :], s_t1[:, :], s_rb[:, :],
                        op=ALU.mult),
                        waits=[(f"stln{d % 2}", l1_free[d % 2])],
                        inc="dve")
                    st = em.op("gpsimd", lambda e, d=d, c=c0: e.dma_start(
                        out=d_ln1[d, :, c:c + 1024], in_=s_l1c[d % 2][:, :]),
                        waits=[("dve", lc),
                               (f"stln{d % 2}", em.cnt[f"stln{d % 2}"])],
                        inc=f"stln{d % 2}", amt=16)
                    l1_free[d % 2] = st
        LN1_ST0 = em.cnt["stln0"]
        LN1_ST1 = em.cnt["stln1"]

        with contextlib.ExitStack() as ps:
            s_kT = [ps.enter_context(nc.sbuf_tensor(f"kT{i}", [128, S], BF16))
                    for i in range(2)]
            s_vsq = [ps.enter_context(nc.sbuf_tensor(f"vs{i}", [128, S], BF16))
                     for i in range(2)]
            s_qT = [ps.enter_context(nc.sbuf_tensor(f"qT{i}", [128, SH], BF16))
                    for i in range(2)]
            s_uT = [ps.enter_context(nc.sbuf_tensor(f"uT{i}", [128, SH], BF16))
                    for i in range(2)]
            s_vfm = ps.enter_context(nc.sbuf_tensor("vfm", [128, S], BF16))
            s_rhs = [ps.enter_context(
                nc.sbuf_tensor(f"rh{i}", [128, 4096], BF16)) for i in range(2)]
            s_w1 = ps.enter_context(nc.sbuf_tensor("w1b", [128, 4096], BF16))
            s_att = [ps.enter_context(
                nc.sbuf_tensor(f"at{i}", [128, 1024], BF16)) for i in range(2)]
            s_z = [ps.enter_context(
                nc.sbuf_tensor(f"z{i}", [128, 512], BF16)) for i in range(2)]

            w1_free = 0
            rhs_free = [0, 0]
            unit_free = [0] * 4
            vt_free = 0
            attn_read_done = [0, 0]
            z_free = [0, 0]
            zm_val = 0
            proj_done_act = 0
            proj_done_dve = 0

            s_ta = [ps.enter_context(nc.sbuf_tensor(f"ta{i}", [128, 1024], F32))
                    for i in range(2)]
            s_tb = [ps.enter_context(nc.sbuf_tensor(f"tb{i}", [128, 1024], F32))
                    for i in range(2)]
            s_tsum = [ps.enter_context(
                nc.sbuf_tensor(f"ts{i}", [128, 1024], F32)) for i in range(2)]
            s_pstg = [ps.enter_context(
                nc.sbuf_tensor(f"pst{i}", [128, 512], F32)) for i in range(4)]
            NPAIR = 0 if TRUNC < 3 else (NP if TRUNC >= 5 else 1)
            proj_done = {}
            bank7 = [0]
            pstg_free = [0, 0, 0, 0]

            def proj_steps(pair):
                nonlocal w1_free, proj_done
                par = pair % 2
                bar = ([("dve", P2_DVE), ("act", P2_ACT),
                        ("pe", em.cnt["pe"])] if pair == 0 else [])
                lw = em.op("gpsimd", lambda e, p=pair: e.dma_start(
                    out=s_w1[:, :].rearrange("p (d c) -> p d c", d=NDT),
                    in_=d_w1p[p].rearrange("(d p) c -> p d c", p=128)),
                    waits=[("pe", w1_free), ("ldc", LDC),
                           ("ldw", em.cnt["ldw"])] + bar,
                    inc="ldw", amt=16)
                yield
                g = 0
                kinds_dst = {"k": (3, s_kT[par]), "v": (1, s_vfm),
                             "q": (2, s_qT[par]), "u": (0, s_uT[par])}
                pend_vt = None
                for j in range(17):
                    if j < 16:
                        jp = j % 2
                        lr = em.op("sync", lambda e, j=j, p=jp: e.dma_start(
                            out=s_rhs[p][:, :].rearrange(
                                "p (d t) -> p d t", d=NDT),
                            in_=d_ln1[:, :, j * 512:(j + 1) * 512]
                            .rearrange("d p t -> p d t")),
                            waits=[("pe", rhs_free[jp]),
                                   ("stln0", min(LN1_ST0, 64 * (j // 2 + 1))),
                                   ("stln1", min(LN1_ST1, 64 * (j // 2 + 1))),
                                   (f"ldr{jp}", em.cnt[f"ldr{jp}"])],
                            inc=f"ldr{jp}", amt=16)
                        yield
                        kinds = (["k", "v", "q", "u"] if j < 8
                                 else ["k", "v"])
                        mm_last = None
                        for kind in kinds:
                            ki, dst = kinds_dst[kind]
                            gp = g % 4
                            for d in range(NDT):
                                w = []
                                if d == 0:
                                    w = [(f"ldr{jp}", lr), ("ldw", lw),
                                         ("dve", bank7[0]),
                                         ("pe", attn_read_done[par])]
                                mm_last = em.op("tensor",
                                    lambda e, d=d, k=ki, p=jp: e.matmul(
                                        bank(7),
                                        s_w1[:, d * 512 + k * 128:
                                             d * 512 + k * 128 + 128],
                                        s_rhs[p][:, d * 512:(d + 1) * 512],
                                        start=(d == 0), stop=(d == NDT - 1)),
                                    waits=w, inc="pe")
                                if d % 2 == 1:
                                    yield
                            cp = em.op("vector", lambda e, gp=gp:
                                       e.tensor_copy(s_pstg[gp][:, :],
                                                     bank(7)),
                                       waits=[("pe", mm_last),
                                              ("act", pstg_free[gp])],
                                       inc="dve")
                            bank7[0] = cp
                            yield
                            bcol = ki * 8 + pair
                            ev = em.op("scalar", lambda e, gp=gp, j=j,
                                       b=bcol, ds=dst: e.activation(
                                           out=ds[:, j * 512:(j + 1) * 512],
                                           in_=s_pstg[gp][:, :],
                                           func=AF.Silu,
                                           bias=s_b1c[:, b:b + 1]),
                                       waits=[("dve", cp),
                                              ("pe", attn_read_done[par])],
                                       inc="act")
                            pstg_free[gp] = ev
                            if kind == "v":
                                vdef = (j, ev)
                            g += 1
                            yield
                        rhs_free[jp] = mm_last
                    if pend_vt is not None:
                        vj, vev = pend_vt
                        tlast = None
                        for c in range(4):
                            tlast = em.op("tensor",
                                lambda e, vj=vj, c=c: e.matmul(
                                    p_all[:, 3584 + c * 128:
                                          3584 + (c + 1) * 128],
                                    s_vfm[:, vj * 512 + c * 128:
                                          vj * 512 + (c + 1) * 128],
                                    s_id[:, :], start=True, stop=True),
                                waits=([("act", vev), ("dve", bank7[0]),
                                        ("ldc", LDC)] if c == 0 else []),
                                inc="pe")
                        yield
                        bank7[0] = em.op("vector", lambda e, vj=vj, p=par:
                                         e.tensor_copy(
                                             s_vsq[p][:, vj * 512:
                                                      (vj + 1) * 512],
                                             bank(7)),
                                         waits=[("pe", tlast),
                                                ("pe", attn_read_done[par])],
                                         inc="dve")
                        yield
                    if j < 16:
                        pend_vt = vdef
                proj_done[pair] = (em.cnt["act"], em.cnt["dve"])
                w1_free = em.cnt["pe"]

            P2_DVE = em.cnt["dve"]
            P2_ACT = em.cnt["act"]

        if True:
            buf_free = [P2_DVE, P2_DVE]
            sum_free = [P2_ACT, P2_ACT]
            it = 0
            for da, db_, cosb, dst in ((d_th, d_thn, True, s_cos),
                                       (d_ph, d_phn, False, s_sin)):
                for dt_ in range(NDT):
                    par = it % 2
                    la = em.op("sync", lambda e, a=da, d=dt_, p=par:
                               e.dma_start(out=s_ta[p][:, :],
                                           in_=a[d * 128:(d + 1) * 128, :]),
                               waits=[("dve", buf_free[par]),
                                      ("act", sum_free[par]),
                                      ("ldta", em.cnt["ldta"])],
                               inc="ldta", amt=16)
                    lb = em.op("sync", lambda e, a=db_, d=dt_, p=par:
                               e.dma_start(out=s_tb[p][:, :],
                                           in_=a[d * 128:(d + 1) * 128, :]),
                               waits=[("dve", buf_free[par]),
                                      ("act", sum_free[par]),
                                      ("ldtb", em.cnt["ldtb"])],
                               inc="ldtb", amt=16)
                    sm = em.op("vector", lambda e, p=par: e.tensor_tensor(
                        s_tsum[p][:, :], s_ta[p][:, :], s_tb[p][:, :],
                        op=ALU.add),
                        waits=[("ldta", la), ("ldtb", lb),
                               ("act", sum_free[par])],
                        inc="dve")
                    buf_free[par] = sm
                    if cosb:
                        a = em.op("scalar", lambda e, p=par:
                                  e.activation(
                                      out=s_tsum[p][:, :],
                                      in_=s_tsum[p][:, :], func=AF.Sin,
                                      bias=PI2()),
                                  waits=[("dve", sm), ("ldc", LDC)], inc="act")
                        sm = em.op("vector", lambda e, d=dt_, p=par, ds=dst:
                                   e.tensor_scalar(
                                       ds[:, d * 1024:(d + 1) * 1024],
                                       s_tsum[p][:, :], ONE(), None,
                                       op0=ALU.subtract),
                                   waits=[("act", a)], inc="dve")
                        buf_free[par] = sm
                        sum_free[par] = a
                        it += 1
                        continue
                    if True:
                        a = em.op("scalar", lambda e, d=dt_, p=par, ds=dst:
                                  e.activation(
                                      out=ds[:, d * 1024:(d + 1) * 1024],
                                      in_=s_tsum[p][:, :], func=AF.Sin),
                                  waits=[("dve", sm)], inc="act")
                    sum_free[par] = a
                    it += 1


            if NPAIR > 0:
                for _ in proj_steps(0):
                    pass
            for pair in range(NPAIR):
                par = pair % 2
                bg = (proj_steps(pair + 1) if pair + 1 < NPAIR else iter(()))
                bg_alive = [True]

                def pop_bg(n):
                    for _ in range(n):
                        if not bg_alive[0]:
                            return
                        try:
                            next(bg)
                        except StopIteration:
                            bg_alive[0] = False
                            return

                if TRUNC < 4:
                    pop_bg(1 << 30)
                    continue
                NU = 512
                qk_val = [0] * NU
                av_val = [0] * NU
                silu_val = [0] * NU
                pd = proj_done[pair]
                for i in range(NU + 2):
                    if i < NU:
                        sl = i % 3
                        qp, kc = i // 64, i % 64
                        w = [("act", silu_val[i - 3] if i >= 3 else 0)]
                        if i == 0:
                            w += [("act", pd[0]), ("dve", pd[1])]
                        em.op("tensor", lambda e, sl=sl, q=qp, k=kc, p=par:
                              e.matmul(
                                  p_all[:, 512 + sl * 1024:
                                        512 + sl * 1024 + 512],
                                  s_kT[p][0:64, k * 128:(k + 1) * 128],
                                  s_qT[p][0:64, q * 512:(q + 1) * 512],
                                  start=True, stop=True,
                                  tile_position=(0, 0)),
                              waits=w)
                        qk_val[i] = em.op("tensor",
                              lambda e, sl=sl, q=qp, k=kc, p=par: e.matmul(
                                  p_all[:, 512 + sl * 1024 + 512:
                                        512 + sl * 1024 + 1024],
                                  s_kT[p][64:128, k * 128:(k + 1) * 128],
                                  s_qT[p][64:128, q * 512:(q + 1) * 512],
                                  start=True, stop=True,
                                  tile_position=(64, 0)),
                              inc="pe")
                    u = i - 2
                    if u >= 0:
                        w = [("pe", qk_val[u]),
                             ("pe", av_val[u - 2] if u >= 2 else 0)]
                        sl = u % 3
                        silu_val[u] = em.op("scalar",
                              lambda e, sl=sl, ab=u % 2: e.activation(
                                  out=s_att[ab][:, :],
                                  in_=p_all[:, 512 + sl * 1024:
                                            512 + sl * 1024 + 1024],
                                  func=AF.Silu),
                              waits=w, inc="act")
                        qp, kc = u // 64, u % 64
                        w = [("act", silu_val[u])]
                        if kc == 0:
                            w.append(("dve", zm_val))
                        em.op("tensor", lambda e, k=kc, ab=u % 2, p=par:
                              e.matmul(
                                  p_all[0:64, 0:512],
                                  s_vsq[p][:, k * 128:k * 128 + 64],
                                  s_att[ab][:, 0:512],
                                  start=(k == 0), stop=(k == 63),
                                  tile_position=(0, 0)),
                              waits=w)
                        av_val[u] = em.op("tensor",
                              lambda e, k=kc, ab=u % 2, p=par: e.matmul(
                                  p_all[64:128, 0:512],
                                  s_vsq[p][:, k * 128 + 64:k * 128 + 128],
                                  s_att[ab][:, 512:1024],
                                  start=(k == 0), stop=(k == 63),
                                  tile_position=(0, 64)),
                              inc="pe")
                        if kc == 63:
                            zp = (qp % 2)
                            zm_val = em.op("vector",
                                  lambda e, q=qp, p=par, zp=zp:
                                  e.tensor_tensor(
                                      s_z[zp][:, :], p_all[:, 0:512],
                                      s_uT[p][:, q * 512:(q + 1) * 512],
                                      op=ALU.mult),
                                  waits=[("pe", av_val[u]),
                                         (f"stz{zp}", z_free[zp])],
                                  inc="dve")
                            z_free[zp] = em.op("gpsimd",
                                  lambda e, q=qp, pr=pair, zp=zp:
                                  e.dma_start(
                                      out=d_zscr[pr, :, q * 512:(q + 1) * 512],
                                      in_=s_z[zp][:, :]),
                                  waits=[("dve", zm_val),
                                         (f"stz{zp}", em.cnt[f"stz{zp}"])],
                                  inc=f"stz{zp}", amt=16)
                    if i % 2 == 1:
                        pop_bg(1)
                pop_bg(1 << 30)
                attn_read_done[par] = em.cnt["pe"]
        Z_ALL0 = em.cnt["stz0"]
        Z_ALL1 = em.cnt["stz1"]


        if TRUNC >= 6:
            with contextlib.ExitStack() as ts:
                s_w2 = ts.enter_context(nc.sbuf_tensor("w2b", [128, 8192], BF16))
                s_c2 = ts.enter_context(nc.sbuf_tensor("c2", [128, 8], F32))
                s_zc = [ts.enter_context(
                    nc.sbuf_tensor(f"zc{i}", [128, 4096], BF16)) for i in range(2)]
                s_fscr = [ts.enter_context(
                    nc.sbuf_tensor(f"fs{i}", [128, 4096], BF16)) for i in range(2)]
                s_zln = [ts.enter_context(
                    nc.sbuf_tensor(f"zl{i}", [128, 4096], BF16)) for i in range(2)]
                s_xc = ts.enter_context(nc.sbuf_tensor("xc0", [128, 4096], F32))
                s_bscr = ts.enter_context(nc.sbuf_tensor("bsc", [128, 4096], BF16))
                s_y = ts.enter_context(nc.sbuf_tensor("yb", [128, 4096], BF16))
                s_ysq = ts.enter_context(nc.sbuf_tensor("ysq", [128, 4096], BF16))
                s_x1 = ts.enter_context(nc.sbuf_tensor("x1", [128, 4096], F32))
                s_x1b = ts.enter_context(nc.sbuf_tensor("x1b", [128, 4096], BF16))
                s_h = ts.enter_context(nc.sbuf_tensor("hb", [128, 4096], BF16))
                s_mag = ts.enter_context(nc.sbuf_tensor("mag", [128, 4096], BF16))
                s_m2 = ts.enter_context(nc.sbuf_tensor("m2", [128, 4096], BF16))
                s_sqa = ts.enter_context(nc.sbuf_tensor("sqa", [128, 512], F32))
                s_sqb = ts.enter_context(nc.sbuf_tensor("sqb", [128, 512], F32))
                s_mu = [ts.enter_context(
                    nc.sbuf_tensor("mu0", [128, 512], F32))] * 3
                s_vr = [ts.enter_context(
                    nc.sbuf_tensor("vr0", [128, 512], F32))] * 3
                _r3 = [ts.enter_context(
                    nc.sbuf_tensor(f"r{i}", [128, 512], F32)) for i in range(3)]
                _mr3 = [ts.enter_context(
                    nc.sbuf_tensor(f"mr{i}", [128, 512], F32)) for i in range(3)]
                s_r = _r3 + [_r3[1]]
                s_mr = _mr3 + [_mr3[1]]

                lw2 = em.op("gpsimd", lambda e: e.dma_start(
                    out=s_w2[:, :].rearrange("p (d c) -> p d c", d=NDT),
                    in_=d_w2.rearrange("(d p) c -> p d c", p=128)),
                    waits=[("ldc", LDC), ("ldw", em.cnt["ldw"]),
                           ("pe", em.cnt["pe"]), ("act", em.cnt["act"]),
                           ("dve", em.cnt["dve"])],
                    inc="ldw", amt=16)
                c2mm = None
                for f in range(8):
                    for d in range(NDT):
                        c2mm = em.op("tensor", lambda e, f=f, d=d: e.matmul(
                            p_all[:, 3072 + f:3072 + f + 1],
                            s_w2[:, d * 1024 + f * 128:d * 1024 + (f + 1) * 128],
                            s_betabf[:, d:d + 1],
                            start=(d == 0), stop=(d == NDT - 1)),
                            waits=([("ldw", lw2), ("ldc", LDC),
                                    ("act", em.cnt["act"]),
                                    ("dve", em.cnt["dve"])]
                                   if f == 0 and d == 0 else []),
                            inc="pe")
                c2dve = em.op("vector", lambda e: e.tensor_tensor(
                    s_c2[:, :], p_all[:, 3072:3080], s_gb[:, 16:24], op=ALU.add),
                    waits=[("pe", c2mm)], inc="dve")
                gma = None
                for d in range(NDT):
                    gma = em.op("vector", lambda e, d=d: e.tensor_scalar(
                        s_w2[:, d * 1024:(d + 1) * 1024],
                        s_w2[:, d * 1024:(d + 1) * 1024],
                        s_gb[:, d:d + 1], None, op0=ALU.mult),
                        waits=[("pe", c2mm)], inc="dve")
                W2RDY = gma

                zc_free = [(em.cnt["pe"], em.cnt["dve"])] * 2
                fscr_free = [em.cnt["act"], em.cnt["act"]]
                fscr_free = [em.cnt["dve"]] * 2
                zln_free = [em.cnt["pe"]] * 2
                bank_free = [c2dve, c2dve]
                xc_free = em.cnt["dve"]
                x2_done = 0
                stv = 0

                def rstd_chain(mwait, base, cpar, rix, extra=0):
                    em.op("vector", lambda e, b=base: e.tensor_copy(
                        s_mu[cpar][:, :], bank(b)),
                        waits=[("pe", mwait)], inc="dve")
                    em.op("vector", lambda e: e.tensor_tensor(
                        s_vr[cpar][:, :], s_mu[cpar][:, :], s_mu[cpar][:, :],
                        op=ALU.mult), inc="dve")
                    vfree = em.op("vector", lambda e, b=base: e.tensor_tensor(
                        s_vr[cpar][:, :], bank(b + 1), s_vr[cpar][:, :],
                        op=ALU.subtract), inc="dve")
                    em.op("scalar", lambda e: e.activation(
                        out=s_r[rix][:, :], in_=s_vr[cpar][:, :], func=AF.Ln,
                        bias=EPSB()), waits=[("dve", vfree), ("dve", extra)],
                        inc="act")
                    ar = em.op("scalar", lambda e: e.activation(
                        out=s_r[rix][:, :], in_=s_r[rix][:, :], func=AF.Exp,
                        scale=-0.5), inc="act")
                    mrv = em.op("vector", lambda e: e.tensor_tensor(
                        s_mr[rix][:, :], s_mu[cpar][:, :], s_r[rix][:, :],
                        op=ALU.mult), waits=[("act", ar)], inc="dve")
                    return vfree, mrv

                fstate = {}

                def front(n):
                    c0 = n * 512
                    par = n % 2
                    base = par * 2
                    lz = em.op("sync", lambda e, c=c0, p=par: e.dma_start(
                        out=s_zc[p][:, :].rearrange("p (d t) -> p d t", d=NP),
                        in_=d_zscr[:, :, c:c + 512].rearrange("d p t -> p d t")),
                        waits=[("stz0", Z_ALL0), ("stz1", Z_ALL1),
                               ("pe", zc_free[par][0]), ("dve", zc_free[par][1]),
                               (f"ldz{par}", em.cnt[f"ldz{par}"])],
                        inc=f"ldz{par}", amt=16)
                    zq = em.op("scalar", lambda e, p=par: e.activation(
                        out=s_fscr[p][:, :], in_=s_zc[p][:, :], func=AF.Square),
                        waits=[(f"ldz{par}", lz), ("dve", fscr_free[par])],
                        inc="act")
                    for d in range(NDT):
                        em.op("tensor", lambda e, d=d, p=par, b=base: e.matmul(
                            bank(b), s_ones[:, :],
                            s_zc[p][:, d * 512:(d + 1) * 512],
                            start=(d == 0), stop=(d == NDT - 1)),
                            waits=([(f"ldz{par}", lz), ("dve", bank_free[par])]
                                   if d == 0 else []))
                    st2 = None
                    for d in range(NDT):
                        st2 = em.op("tensor", lambda e, d=d, p=par, b=base:
                                    e.matmul(
                                        bank(b + 1), s_ones[:, :],
                                        s_fscr[p][:, d * 512:(d + 1) * 512],
                                        start=(d == 0), stop=(d == NDT - 1)),
                                    waits=([("act", zq)] if d == 0 else []),
                                    inc="pe")
                    bfree, mr2 = rstd_chain(st2, base, 2, 0)
                    bank_free[par] = bfree
                    zl = None
                    for d in range(NDT):
                        em.op("vector", lambda e, d=d, p=par: e.tensor_tensor(
                            s_fscr[p][:, d * 512:(d + 1) * 512],
                            s_zc[p][:, d * 512:(d + 1) * 512], s_r[0][:, :],
                            op=ALU.mult),
                            waits=[("dve", mr2), ("pe", zln_free[par])],
                            inc="dve")
                        zl = em.op("vector", lambda e, d=d, p=par: e.tensor_tensor(
                            s_zln[p][:, d * 512:(d + 1) * 512],
                            s_fscr[p][:, d * 512:(d + 1) * 512], s_mr[0][:, :],
                            op=ALU.subtract), inc="dve")
                    fscr_free[par] = zl
                    fstate[n] = (zl, lz)

                front(0)
                for n in range(TAILN):
                    c0 = n * 512
                    par = n % 2
                    base = par * 2
                    zl, lz = fstate[n]
                    ysq_l = None
                    ypar_dve = [0, 0]
                    for f in range(8):
                        mm = None
                        for d in range(NDT):
                            w = []
                            if d == 0:
                                w = [("dve", zl), ("dve", W2RDY),
                                     ("dve", bank_free[par]),
                                     ("dve", ypar_dve[f % 2])]
                            mm = em.op("tensor", lambda e, f=f, d=d, p=par,
                                       b=base: e.matmul(
                                           bank(b + f % 2),
                                           s_w2[:, d * 1024 + f * 128:
                                                d * 1024 + (f + 1) * 128],
                                           s_zln[p][:, d * 512:(d + 1) * 512],
                                           start=(d == 0), stop=(d == NDT - 1)),
                                       waits=w, inc="pe")
                        if f == 7:
                            zc_free[par] = (mm, zl)
                            zln_free[par] = mm
                        yev = em.op("vector", lambda e, f=f, b=base:
                                    e.tensor_tensor(
                                        s_y[:, f * 512:(f + 1) * 512],
                                        bank(b + f % 2),
                                        s_c2[:, f:f + 1].broadcast_to([128, 512]),
                                        op=ALU.add),
                                    waits=[("pe", mm), ("dve", c2dve)], inc="dve")
                        ysq_l = em.op("scalar", lambda e, f=f: e.activation(
                            out=s_ysq[:, f * 512:(f + 1) * 512],
                            in_=s_y[:, f * 512:(f + 1) * 512],
                            func=AF.Square),
                            waits=[("dve", yev)], inc="act")
                        ypar_dve[f % 2] = yev
                    for f in range(8):
                        em.op("tensor", lambda e, f=f, b=base: e.matmul(
                            bank(b), s_ones[:, :], s_y[:, f * 512:(f + 1) * 512],
                            start=(f == 0), stop=(f == 7)),
                            waits=([("dve", yev), ("dve", ypar_dve[0]),
                                    ("dve", ypar_dve[1])] if f == 0 else []))
                    sty = None
                    for f in range(8):
                        sty = em.op("tensor", lambda e, f=f, b=base: e.matmul(
                            bank(b + 1), s_ones[:, :],
                            s_ysq[:, f * 512:(f + 1) * 512],
                            start=(f == 0), stop=(f == 7)),
                            waits=([("act", ysq_l)] if f == 0 else []), inc="pe")
                    bfree, mry = rstd_chain(sty, base, 1, 1, extra=x2_done)
                    lx = em.op("sync", lambda e, c=c0: e.dma_start(
                        out=s_xc[:, :].rearrange("p (d t) -> p d t", d=NDT),
                        in_=d_xT[:, c:c + 512]
                        .rearrange("(d p) t -> p d t", p=128)),
                        waits=[("dve", xc_free), ("ldxc", em.cnt["ldxc"]),
                               ("stout", em.cnt["stout"])],
                        inc="ldxc", amt=16)
                    x1b_l = None
                    x1sq_l = None
                    for f in range(8):
                        em.op("vector", lambda e, f=f: e.tensor_tensor(
                            s_bscr[:, f * 512:(f + 1) * 512],
                            s_y[:, f * 512:(f + 1) * 512], s_r[1][:, :],
                            op=ALU.mult),
                            waits=[("dve", mry), ("ldxc", lx)], inc="dve")
                        em.op("vector", lambda e, f=f: e.tensor_tensor(
                            s_bscr[:, f * 512:(f + 1) * 512],
                            s_bscr[:, f * 512:(f + 1) * 512], s_mr[1][:, :],
                            op=ALU.subtract), inc="dve")
                        x1v = em.op("vector", lambda e, f=f: e.tensor_tensor(
                            s_x1[:, f * 512:(f + 1) * 512],
                            s_bscr[:, f * 512:(f + 1) * 512],
                            s_xc[:, f * 512:(f + 1) * 512],
                            op=ALU.add), inc="dve")
                        x1b_l = em.op("scalar", lambda e, f=f: e.activation(
                            out=s_x1b[:, f * 512:(f + 1) * 512],
                            in_=s_x1[:, f * 512:(f + 1) * 512], func=AF.Copy),
                            waits=[("dve", x1v)], inc="act")
                        x1sq_l = em.op("scalar", lambda e, f=f: e.activation(
                            out=s_ysq[:, f * 512:(f + 1) * 512],
                            in_=s_x1[:, f * 512:(f + 1) * 512], func=AF.Square),
                            inc="act")
                    xc_free = em.cnt["dve"]
                    for f in range(8):
                        em.op("tensor", lambda e, f=f, b=base: e.matmul(
                            bank(b), s_ones[:, :],
                            s_x1b[:, f * 512:(f + 1) * 512],
                            start=(f == 0), stop=(f == 7)),
                            waits=([("act", x1b_l), ("dve", bfree)]
                                   if f == 0 else []))
                    st3 = None
                    for f in range(8):
                        st3 = em.op("tensor", lambda e, f=f, b=base: e.matmul(
                            bank(b + 1), s_ones[:, :],
                            s_ysq[:, f * 512:(f + 1) * 512],
                            start=(f == 0), stop=(f == 7)),
                            waits=([("act", x1sq_l)] if f == 0 else []), inc="pe")
                    bfree, mr3 = rstd_chain(st3, base, 0, 2)
                    hl = None
                    for f in range(8):
                        em.op("vector", lambda e, f=f: e.tensor_tensor(
                            s_bscr[:, f * 512:(f + 1) * 512],
                            s_x1b[:, f * 512:(f + 1) * 512], s_r[2][:, :],
                            op=ALU.mult), waits=[("dve", mr3)], inc="dve")
                        hl = em.op("vector", lambda e, f=f: e.tensor_tensor(
                            s_h[:, f * 512:(f + 1) * 512],
                            s_bscr[:, f * 512:(f + 1) * 512], s_mr[2][:, :],
                            op=ALU.subtract), inc="dve")
                    bank_free[par] = bfree
                    if n + 1 < TAILN:
                        front(n + 1)
                    ri_free = [0, 0, 0, 0]
                    m2_l = 0
                    mag_l = None
                    for fo in range(8):
                        rb_ = 4 + (fo % 2) * 2
                        ib_ = 5 + (fo % 2) * 2
                        mmr = None
                        for d in range(NDT):
                            w = []
                            if d == 0:
                                w = [("dve", hl), ("act", ri_free[2 * (fo % 2)])]
                            mmr = em.op("tensor", lambda e, d=d, f=fo, b=rb_:
                                        e.matmul(
                                            bank(b),
                                            s_cos[:, d * 1024 + f * 128:
                                                  d * 1024 + (f + 1) * 128],
                                            s_h[:, d * 512:(d + 1) * 512],
                                            start=(d == 0), stop=(d == NDT - 1)),
                                        waits=w, inc="pe")
                        mmi = None
                        for d in range(NDT):
                            w = []
                            if d == 0:
                                w = [("act", ri_free[2 * (fo % 2) + 1])]
                            mmi = em.op("tensor", lambda e, d=d, f=fo, b=ib_:
                                        e.matmul(
                                            bank(b),
                                            s_sin[:, d * 1024 + f * 128:
                                                  d * 1024 + (f + 1) * 128],
                                            s_h[:, d * 512:(d + 1) * 512],
                                            start=(d == 0), stop=(d == NDT - 1)),
                                        waits=w, inc="pe")
                        sa = em.op("scalar", lambda e, b=rb_: e.activation(
                            out=s_sqa[:, :], in_=bank(b), func=AF.Square),
                            waits=[("pe", mmr), ("dve", m2_l)], inc="act")
                        ri_free[2 * (fo % 2)] = sa
                        sb_ = em.op("scalar", lambda e, b=ib_: e.activation(
                            out=s_sqb[:, :], in_=bank(b), func=AF.Square),
                            waits=[("pe", mmi)], inc="act")
                        ri_free[2 * (fo % 2) + 1] = sb_
                        m2_l = em.op("vector", lambda e, f=fo: e.tensor_tensor(
                            s_m2[:, f * 512:(f + 1) * 512], s_sqa[:, :],
                            s_sqb[:, :], op=ALU.add),
                            waits=[("act", sb_)], inc="dve")
                        em.op("scalar", lambda e, f=fo: e.activation(
                            out=s_mag[:, f * 512:(f + 1) * 512],
                            in_=s_m2[:, f * 512:(f + 1) * 512], func=AF.Ln,
                            bias=TINY()), waits=[("dve", m2_l)], inc="act")
                        mag_l = em.op("scalar", lambda e, f=fo: e.activation(
                            out=s_mag[:, f * 512:(f + 1) * 512],
                            in_=s_mag[:, f * 512:(f + 1) * 512], func=AF.Exp,
                            scale=0.5), inc="act")
                    for f in range(8):
                        em.op("tensor", lambda e, f=f, b=base: e.matmul(
                            bank(b), s_ones[:, :],
                            s_mag[:, f * 512:(f + 1) * 512],
                            start=(f == 0), stop=(f == 7)),
                            waits=([("act", mag_l), ("dve", bank_free[par])]
                                   if f == 0 else []))
                    st4 = None
                    for f in range(8):
                        st4 = em.op("tensor", lambda e, f=f, b=base: e.matmul(
                            bank(b + 1), s_ones[:, :],
                            s_m2[:, f * 512:(f + 1) * 512],
                            start=(f == 0), stop=(f == 7)),
                            waits=([("dve", m2_l)] if f == 0 else []), inc="pe")
                    bfree, mr4 = rstd_chain(st4, base, 1, 3)
                    bank_free[par] = bfree
                    prev_st = stv
                    xl = None
                    for f in range(8):
                        em.op("vector", lambda e, f=f: e.tensor_tensor(
                            s_bscr[:, f * 512:(f + 1) * 512],
                            s_mag[:, f * 512:(f + 1) * 512], s_r[3][:, :],
                            op=ALU.mult),
                            waits=[("dve", mr4), ("stout", prev_st)], inc="dve")
                        em.op("vector", lambda e, f=f: e.tensor_tensor(
                            s_bscr[:, f * 512:(f + 1) * 512],
                            s_bscr[:, f * 512:(f + 1) * 512], s_mr[3][:, :],
                            op=ALU.subtract), inc="dve")
                        xl = em.op("vector", lambda e, f=f: e.tensor_tensor(
                            s_xc[:, f * 512:(f + 1) * 512],
                            s_bscr[:, f * 512:(f + 1) * 512],
                            s_x1[:, f * 512:(f + 1) * 512], op=ALU.add),
                            inc="dve")
                    x2_done = xl
                    stv = em.op("gpsimd", lambda e, c=c0: e.dma_start(
                        out=d_out[:, c:c + 512].rearrange("(f p) t -> p f t",
                                                          p=128),
                        in_=s_xc[:, :].rearrange("p (f t) -> p f t", f=NDT)),
                        waits=[("dve", xl), ("stout", em.cnt["stout"])],
                        inc="stout", amt=16)
                em.op("gpsimd", lambda e: e.nop(),
                      waits=[("stout", em.cnt["stout"])])
                if int(_os.environ.get("KEXTRA2", "0")):
                    em.op("tensor", lambda e: e.matmul(
                        bank(2), s_cos[:, 0:128], s_cos[:, 0:512],
                        start=True, stop=True), inc="pe")
                if TAILN < 8:
                    waits = [(s_, em.cnt[s_]) for s_ in em.cnt]
                    _mk = _os.environ.get("KMARK", "cos")
                    msrc = {"cos": s_cos[:, 0:4096], "zln": s_zln[0][:, :],
                            "zc": s_zc[0][:, :], "w2": s_w2[:, 0:4096],
                            "r0": s_r[0][:, :], "mr0": s_mr[0][:, :]}[_mk]
                    em.op("gpsimd", lambda e, m=msrc: e.dma_start(
                        out=d_out[0:128, 0:m.shape[-1]], in_=m),
                        waits=waits, inc="stout", amt=16)
                    em.op("gpsimd", lambda e: e.nop(),
                          waits=[("stout", em.cnt["stout"])])
        else:
            waits = [(s_, em.cnt[s_]) for s_ in em.cnt]
            em.op("gpsimd", lambda e: e.dma_start(
                out=d_out[0:128, :], in_=s_cos[:, 0:4096]),
                waits=waits, inc="stout", amt=16)
            em.op("gpsimd", lambda e: e.nop(),
                  waits=[("stout", em.cnt["stout"])])

        for name in em.cnt:
            sems[name] = es.enter_context(nc.semaphore(name))
        with nc.Block() as block:
            def mk(eng_name):
                def body(eng):
                    for waits, fn, inc, amt in em.ops[eng_name]:
                        for sname, val in waits:
                            eng.wait_ge(sems[sname], val)
                        r = fn(eng)
                        if inc is not None:
                            assert r is not None, f"{eng_name} inc={inc}"
                            r.then_inc(sems[inc], amt)
                return body
            block.sync(mk("sync"))
            block.gpsimd(mk("gpsimd"))
            block.tensor(mk("tensor"))
            block.scalar(mk("scalar"))
            block.vector(mk("vector"))

    return nc


def kernel(L0_attn_x, L0_attn_W1, L0_attn_b1, L0_attn_W2, L0_attn_b2,
           L0_attn_gamma, L0_attn_beta, L0_ffn_theta, L0_ffn_phi,
           L0_ffn_theta_noise, L0_ffn_phi_noise):
    x = np.asarray(L0_attn_x, np.float32)
    W1 = np.asarray(L0_attn_W1, np.float32)
    b1 = np.asarray(L0_attn_b1, np.float32)
    W2 = np.ascontiguousarray(np.asarray(L0_attn_W2, np.float32))
    b2 = np.asarray(L0_attn_b2, np.float32)
    gamma = np.asarray(L0_attn_gamma, np.float32)
    beta = np.asarray(L0_attn_beta, np.float32)
    th = np.ascontiguousarray(np.asarray(L0_ffn_theta, np.float32))
    thn = np.ascontiguousarray(np.asarray(L0_ffn_theta_noise, np.float32))
    ph = np.ascontiguousarray(np.asarray(L0_ffn_phi, np.float32))
    phn = np.ascontiguousarray(np.asarray(L0_ffn_phi_noise, np.float32))

    xT = np.ascontiguousarray(x.transpose(0, 2, 1))

    w1p = np.empty((NP, D, 512), np.float32)
    for p in range(NP):
        for k in range(4):
            w1p[p, :, k * 128:(k + 1) * 128] = \
                W1[:, k * D + p * 128:k * D + (p + 1) * 128]
    b1c = np.ascontiguousarray(
        b1.reshape(4, 8, 128).transpose(2, 0, 1).reshape(128, 32))
    gb = np.ascontiguousarray(np.concatenate(
        [gamma.reshape(8, 128).T, beta.reshape(8, 128).T,
         b2.reshape(8, 128).T], axis=1))
    consts = np.concatenate(
        [np.eye(128, dtype=np.float32) / S,
         np.ones((128, 128), np.float32) / D], axis=1).astype(BF)
    cb = np.tile(np.array([[math.pi / 2, EPS, 1e-30, 1024.0,
                            1.0, 0.0, 0.0, 0.0]], np.float32), (128, 1))

    nc = build_nc()
    in_maps = []
    for c in range(8):
        b, half = c // 2, c % 2
        off = half * SH
        xrot = np.ascontiguousarray(
            np.concatenate([xT[b][:, off:], xT[b][:, :off]], axis=1))
        in_maps.append(dict(
            xT=xrot, w1p=w1p, b1c=b1c, gb=gb, consts=consts, cb=cb,
            w2=W2, th=th, thn=thn, ph=ph, phn=phn))
    res = run_bass_kernel_spmd(nc, in_maps, core_ids=list(range(8)))
    out = np.empty_like(x)
    for c in range(8):
        b, half = c // 2, c % 2
        out[b, half * SH:(half + 1) * SH, :] = res.results[c]["out"].T
    return out


if __name__ == "__main__":
    import reference as _r
    inp = _r.setup_inputs(0)
    want = np.asarray(_r.reference(**inp))
    got = kernel(**{k: np.asarray(v) for k, v in inp.items()})
    err = got - want
    rvr = float((err ** 2).mean() / (want ** 2).mean())
    print("rvr", rvr, "max", float(np.abs(err).max()))


# version 11 (final)
# speedup vs baseline: 1.0696x; 1.0145x over previous
import contextlib
import math
import numpy as np
import ml_dtypes
import concourse.bass as bass
import concourse.mybir as mybir
from concourse.bass_utils import run_bass_kernel_spmd

F32 = mybir.dt.float32
BF16 = mybir.dt.bfloat16
AF = mybir.ActivationFunctionType
ALU = mybir.AluOpType
BF = ml_dtypes.bfloat16

D = 1024
S = 8192
SH = 4096
NDT = 8
NP = 8
EPS = 1e-5

ENGINES = ("sync", "gpsimd", "tensor", "scalar", "vector")
SELF_SEM = {"vector": "dve", "scalar": "act"}
import os as _os
TRUNC = int(_os.environ.get("KTRUNC", "6"))
TAILN = int(_os.environ.get("KTAILN", "8"))
KC2 = _os.environ.get("KC2", "1") == "1"
KGM = _os.environ.get("KGM", "1") == "1"
KW2 = _os.environ.get("KW2", "1") == "1"
KSTEP = int(_os.environ.get("KSTEP", "99"))
KYALT = int(_os.environ.get("KYALT", "0"))
KYB = int(_os.environ.get("KYB", "2"))
KYF = int(_os.environ.get("KYF", "8"))
KYSRC = _os.environ.get("KYSRC", "w2")
KYRHS = _os.environ.get("KYRHS", "zln")
KYW = _os.environ.get("KYW", "1") == "1"
KYD = int(_os.environ.get("KYD", "8"))
KYN = int(_os.environ.get("KYN", "512"))


class Em:

    def __init__(self):
        self.ops = {e: [] for e in ENGINES}
        self.cnt = {}
        self.last_wait = {e: {} for e in ENGINES}

    def sem(self, name):
        self.cnt.setdefault(name, 0)

    def op(self, eng, fn, waits=(), inc=None, amt=1):
        waits = list(waits)
        if eng in SELF_SEM:
            assert inc == SELF_SEM[eng], f"{eng} op must inc {SELF_SEM[eng]}"
            waits.append((SELF_SEM[eng], self.cnt[SELF_SEM[eng]]))
        w = []
        for sname, val in waits:
            assert val <= self.cnt[sname], (
                f"deadlock: {eng} waits {sname}>={val} > {self.cnt[sname]}")
            if val <= 0 or self.last_wait[eng].get(sname, -1) >= val:
                continue
            self.last_wait[eng][sname] = val
            w.append((sname, val))
        v = None
        if inc is not None:
            self.cnt[inc] += amt
            v = self.cnt[inc]
        self.ops[eng].append((w, fn, inc, amt))
        return v


def build_nc():
    nc = bass.Bass("TRN2", target_bir_lowering=False)

    d_xT = nc.dram_tensor("xT", [D, S], F32, kind="ExternalInput")
    d_w1p = nc.dram_tensor("w1p", [NP, D, 512], F32, kind="ExternalInput")
    d_b1c = nc.dram_tensor("b1c", [128, 32], F32, kind="ExternalInput")
    d_gb = nc.dram_tensor("gb", [128, 24], F32, kind="ExternalInput")
    d_consts = nc.dram_tensor("consts", [128, 256], BF16, kind="ExternalInput")
    d_cb = nc.dram_tensor("cb", [128, 8], F32, kind="ExternalInput")
    d_w2 = nc.dram_tensor("w2", [D, D], F32, kind="ExternalInput")
    d_th = nc.dram_tensor("th", [D, D], F32, kind="ExternalInput")
    d_thn = nc.dram_tensor("thn", [D, D], F32, kind="ExternalInput")
    d_ph = nc.dram_tensor("ph", [D, D], F32, kind="ExternalInput")
    d_phn = nc.dram_tensor("phn", [D, D], F32, kind="ExternalInput")
    d_out = nc.dram_tensor("out", [D, SH], F32, kind="ExternalOutput")
    d_ln1 = nc.dram_tensor("ln1scr", [NDT, 128, S], BF16)
    d_zscr = nc.dram_tensor("zscr", [NP, 128, SH], BF16)

    em = Em()
    for s in ("pe", "act", "dve", "ldc", "ldta", "ldtb", "ldx", "stln0",
              "stln1", "ldw", "ldr0", "ldr1", "stz0", "stz1", "ldz0", "ldz1",
              "ldxc", "stout"):
        em.sem(s)

    with contextlib.ExitStack() as es:
        s_id = es.enter_context(nc.sbuf_tensor("c_id", [128, 128], BF16))
        s_ones = es.enter_context(nc.sbuf_tensor("c_ones", [128, 128], BF16))
        s_b1c = es.enter_context(nc.sbuf_tensor("c_b1c", [128, 32], F32))
        s_gb = es.enter_context(nc.sbuf_tensor("c_gb", [128, 24], F32))
        s_cb = es.enter_context(nc.sbuf_tensor("c_cb", [128, 8], F32))
        s_betabf = es.enter_context(nc.sbuf_tensor("c_bb", [128, 8], BF16))
        s_cos = es.enter_context(nc.sbuf_tensor("cosm", [128, 8192], BF16))
        s_sin = es.enter_context(nc.sbuf_tensor("sinm", [128, 8192], BF16))
        p_all = es.enter_context(nc.psum_tensor("p_all", [128, 4096], F32))
        sems = {}

        def bank(i, w=512):
            return p_all[:, i * 512:i * 512 + w]

        for dst_, src_ in ((s_id, lambda: d_consts[:, 0:128]),
                           (s_ones, lambda: d_consts[:, 128:256]),
                           (s_b1c, lambda: d_b1c[:, :]),
                           (s_gb, lambda: d_gb[:, :]),
                           (s_cb, lambda: d_cb[:, :])):
            em.op("sync", lambda e, d=dst_, s=src_: e.dma_start(
                out=d[:, :], in_=s()), waits=[("ldc", em.cnt["ldc"])],
                inc="ldc", amt=16)
        em.op("gpsimd", lambda e: e.dma_start(out=s_betabf[:, :],
              in_=d_gb[:, 8:16]), waits=[("ldw", em.cnt["ldw"])],
              inc="ldw", amt=16)
        LDC = em.cnt["ldc"]
        PI2 = lambda: s_cb[:, 0:1]
        EPSB = lambda: s_cb[:, 1:2]
        TINY = lambda: s_cb[:, 2:3]
        K1024 = lambda: s_cb[:, 3:4]
        ONE = lambda: s_cb[:, 4:5]

        with contextlib.ExitStack() as ps:
            s_pxt = ps.enter_context(nc.sbuf_tensor("pxt", [128, 8192], F32))
            s_px = [s_pxt[:, d * 1024:(d + 1) * 1024] for d in range(NDT)]
            s_pbf = [ps.enter_context(
                nc.sbuf_tensor(f"pb{i}", [128, 1024], BF16)) for i in range(2)]
            s_psq = [ps.enter_context(
                nc.sbuf_tensor(f"pq{i}", [128, 1024], BF16)) for i in range(2)]
            s_mus = ps.enter_context(nc.sbuf_tensor("mus", [128, 1024], F32))
            s_var = ps.enter_context(nc.sbuf_tensor("var", [128, 1024], F32))
            s_rb = ps.enter_context(nc.sbuf_tensor("rb", [128, 1024], F32))
            s_t1 = ps.enter_context(nc.sbuf_tensor("t1", [128, 1024], F32))
            s_l1c = [ps.enter_context(
                nc.sbuf_tensor(f"l1{i}", [128, 1024], BF16)) for i in range(2)]
            px_free = 0
            pbf_free = [0, 0]
            sq_free = [0, 0]
            psq_free = [0, 0]
            l1_free = [0, 0]
            chain_free = 0
            mstat_done = 0
            for ch in range(8 if TRUNC >= 2 else 0):
                c0 = ch * 1024
                bar = ([("dve", em.cnt["dve"]), ("act", em.cnt["act"])]
                       if ch == 0 else [])
                ldp = em.op("sync", lambda e, c=c0: e.dma_start(
                    out=s_pxt[:, :].rearrange("p (d t) -> p d t", d=NDT),
                    in_=d_xT[:, c:c + 1024]
                    .rearrange("(d p) t -> p d t", p=128)),
                    waits=[("dve", px_free), ("ldx", em.cnt["ldx"])] + bar,
                    inc="ldx", amt=16)
                lds = [ldp] * NDT
                for d in range(NDT):
                    cv = em.op("vector", lambda e, d=d: e.tensor_copy(
                        s_pbf[d % 2][:, :], s_px[d][:, :]),
                        waits=[("ldx", lds[d]), ("pe", pbf_free[d % 2]),
                               ("act", sq_free[d % 2])],
                        inc="dve")
                    sq = em.op("scalar", lambda e, d=d: e.activation(
                        out=s_psq[d % 2][:, :], in_=s_pbf[d % 2][:, :],
                        func=AF.Square),
                        waits=[("dve", cv), ("pe", psq_free[d % 2])],
                        inc="act")
                    sq_free[d % 2] = sq
                    mlast = None
                    for h in range(2):
                        mlast = em.op("tensor", lambda e, d=d, h=h: e.matmul(
                            bank(h), s_ones[:, :],
                            s_pbf[d % 2][:, h * 512:(h + 1) * 512],
                            start=(d == 0), stop=(d == NDT - 1)),
                            waits=([("dve", cv), ("ldc", LDC),
                                    ("dve", chain_free)] if h == 0 else []),
                            inc="pe")
                    pbf_free[d % 2] = mlast
                    for h in range(2):
                        mlast = em.op("tensor", lambda e, d=d, h=h: e.matmul(
                            bank(2 + h), s_ones[:, :],
                            s_psq[d % 2][:, h * 512:(h + 1) * 512],
                            start=(d == 0), stop=(d == NDT - 1)),
                            waits=([("act", sq)] if h == 0 else []),
                            inc="pe")
                    psq_free[d % 2] = mlast
                    mstat_done = mlast
                cmu = em.op("vector", lambda e: e.tensor_copy(
                    s_mus[:, :], p_all[:, 0:1024]),
                    waits=[("pe", mstat_done)], inc="dve")
                em.op("vector", lambda e: e.tensor_tensor(
                    s_var[:, :], s_mus[:, :], s_mus[:, :], op=ALU.mult),
                    inc="dve")
                cvar = em.op("vector", lambda e: e.tensor_tensor(
                    s_var[:, :], p_all[:, 1024:2048], s_var[:, :],
                    op=ALU.subtract), inc="dve")
                chain_free = cvar
                em.op("scalar", lambda e: e.activation(
                    out=s_rb[:, :], in_=s_var[:, :], func=AF.Ln, bias=EPSB()),
                    waits=[("dve", cvar)], inc="act")
                arb = em.op("scalar", lambda e: e.activation(
                    out=s_rb[:, :], in_=s_rb[:, :], func=AF.Exp, scale=-0.5),
                    inc="act")
                for d in range(NDT):
                    t = em.op("vector", lambda e, d=d: e.tensor_tensor(
                        s_t1[:, :], s_px[d][:, :], s_mus[:, :],
                        op=ALU.subtract),
                        waits=[("act", arb)], inc="dve")
                    if d == NDT - 1:
                        px_free = t
                    lc = em.op("vector", lambda e, d=d: e.tensor_tensor(
                        s_l1c[d % 2][:, :], s_t1[:, :], s_rb[:, :],
                        op=ALU.mult),
                        waits=[(f"stln{d % 2}", l1_free[d % 2])],
                        inc="dve")
                    st = em.op("gpsimd", lambda e, d=d, c=c0: e.dma_start(
                        out=d_ln1[d, :, c:c + 1024], in_=s_l1c[d % 2][:, :]),
                        waits=[("dve", lc),
                               (f"stln{d % 2}", em.cnt[f"stln{d % 2}"])],
                        inc=f"stln{d % 2}", amt=16)
                    l1_free[d % 2] = st
        LN1_ST0 = em.cnt["stln0"]
        LN1_ST1 = em.cnt["stln1"]

        with contextlib.ExitStack() as ps:
            s_kT = [ps.enter_context(nc.sbuf_tensor(f"kT{i}", [128, S], BF16))
                    for i in range(2)]
            s_vsq = [ps.enter_context(nc.sbuf_tensor(f"vs{i}", [128, S], BF16))
                     for i in range(2)]
            s_qT = [ps.enter_context(nc.sbuf_tensor(f"qT{i}", [128, SH], BF16))
                    for i in range(2)]
            s_uT = [ps.enter_context(nc.sbuf_tensor(f"uT{i}", [128, SH], BF16))
                    for i in range(2)]
            s_vfm = ps.enter_context(nc.sbuf_tensor("vfm", [128, S], BF16))
            s_rhs = [ps.enter_context(
                nc.sbuf_tensor(f"rh{i}", [128, 4096], BF16)) for i in range(2)]
            s_w1 = ps.enter_context(nc.sbuf_tensor("w1b", [128, 4096], BF16))
            s_att = [ps.enter_context(
                nc.sbuf_tensor(f"at{i}", [128, 1024], BF16)) for i in range(3)]
            s_z = [ps.enter_context(
                nc.sbuf_tensor(f"z{i}", [128, 512], BF16)) for i in range(2)]

            w1_free = 0
            rhs_free = [0, 0]
            unit_free = [0] * 4
            vt_free = 0
            attn_read_done = [0, 0]
            z_free = [0, 0]
            zm_val = 0
            proj_done_act = 0
            proj_done_dve = 0

            s_ta = [ps.enter_context(nc.sbuf_tensor(f"ta{i}", [128, 1024], F32))
                    for i in range(2)]
            s_tb = [ps.enter_context(nc.sbuf_tensor(f"tb{i}", [128, 1024], F32))
                    for i in range(2)]
            s_tsum = [ps.enter_context(
                nc.sbuf_tensor(f"ts{i}", [128, 1024], F32)) for i in range(2)]
            s_pstg = [ps.enter_context(
                nc.sbuf_tensor(f"pst{i}", [128, 512], F32)) for i in range(3)]
            NPAIR = 0 if TRUNC < 3 else (NP if TRUNC >= 5 else 1)
            proj_done = {}
            bank7 = [0]
            pstg_free = [0, 0, 0]

            def proj_steps(pair):
                nonlocal w1_free, proj_done
                par = pair % 2
                bar = ([("dve", P2_DVE), ("act", P2_ACT),
                        ("pe", em.cnt["pe"])] if pair == 0 else [])
                lw = em.op("gpsimd", lambda e, p=pair: e.dma_start(
                    out=s_w1[:, :].rearrange("p (d c) -> p d c", d=NDT),
                    in_=d_w1p[p].rearrange("(d p) c -> p d c", p=128)),
                    waits=[("pe", w1_free), ("ldc", LDC),
                           ("ldw", em.cnt["ldw"])] + bar,
                    inc="ldw", amt=16)
                yield
                g = 0
                kinds_dst = {"k": (3, s_kT[par]), "v": (1, s_vfm),
                             "q": (2, s_qT[par]), "u": (0, s_uT[par])}
                pend_vt = None
                for j in range(17):
                    if j < 16:
                        jp = j % 2
                        lr = em.op("sync", lambda e, j=j, p=jp: e.dma_start(
                            out=s_rhs[p][:, :].rearrange(
                                "p (d t) -> p d t", d=NDT),
                            in_=d_ln1[:, :, j * 512:(j + 1) * 512]
                            .rearrange("d p t -> p d t")),
                            waits=[("pe", rhs_free[jp]),
                                   ("stln0", min(LN1_ST0, 64 * (j // 2 + 1))),
                                   ("stln1", min(LN1_ST1, 64 * (j // 2 + 1))),
                                   (f"ldr{jp}", em.cnt[f"ldr{jp}"])],
                            inc=f"ldr{jp}", amt=16)
                        yield
                        kinds = (["k", "v", "q", "u"] if j < 8
                                 else ["k", "v"])
                        mm_last = None
                        for kind in kinds:
                            ki, dst = kinds_dst[kind]
                            gp = g % 3
                            for d in range(NDT):
                                w = []
                                if d == 0:
                                    w = [(f"ldr{jp}", lr), ("ldw", lw),
                                         ("dve", bank7[0]),
                                         ("pe", attn_read_done[par])]
                                mm_last = em.op("tensor",
                                    lambda e, d=d, k=ki, p=jp: e.matmul(
                                        bank(7),
                                        s_w1[:, d * 512 + k * 128:
                                             d * 512 + k * 128 + 128],
                                        s_rhs[p][:, d * 512:(d + 1) * 512],
                                        start=(d == 0), stop=(d == NDT - 1)),
                                    waits=w, inc="pe")
                                if d % 2 == 1:
                                    yield
                            cp = em.op("vector", lambda e, gp=gp:
                                       e.tensor_copy(s_pstg[gp][:, :],
                                                     bank(7)),
                                       waits=[("pe", mm_last),
                                              ("act", pstg_free[gp])],
                                       inc="dve")
                            bank7[0] = cp
                            yield
                            bcol = ki * 8 + pair
                            ev = em.op("scalar", lambda e, gp=gp, j=j,
                                       b=bcol, ds=dst: e.activation(
                                           out=ds[:, j * 512:(j + 1) * 512],
                                           in_=s_pstg[gp][:, :],
                                           func=AF.Silu,
                                           bias=s_b1c[:, b:b + 1]),
                                       waits=[("dve", cp),
                                              ("pe", attn_read_done[par])],
                                       inc="act")
                            pstg_free[gp] = ev
                            if kind == "v":
                                vdef = (j, ev)
                            g += 1
                            yield
                        rhs_free[jp] = mm_last
                    if pend_vt is not None:
                        vj, vev = pend_vt
                        tlast = None
                        for c in range(4):
                            tlast = em.op("tensor",
                                lambda e, vj=vj, c=c: e.matmul(
                                    p_all[:, 3584 + c * 128:
                                          3584 + (c + 1) * 128],
                                    s_vfm[:, vj * 512 + c * 128:
                                          vj * 512 + (c + 1) * 128],
                                    s_id[:, :], start=True, stop=True),
                                waits=([("act", vev), ("dve", bank7[0]),
                                        ("ldc", LDC)] if c == 0 else []),
                                inc="pe")
                        yield
                        bank7[0] = em.op("vector", lambda e, vj=vj, p=par:
                                         e.tensor_copy(
                                             s_vsq[p][:, vj * 512:
                                                      (vj + 1) * 512],
                                             bank(7)),
                                         waits=[("pe", tlast),
                                                ("pe", attn_read_done[par])],
                                         inc="dve")
                        yield
                    if j < 16:
                        pend_vt = vdef
                proj_done[pair] = (em.cnt["act"], em.cnt["dve"])
                w1_free = em.cnt["pe"]

            P2_DVE = em.cnt["dve"]
            P2_ACT = em.cnt["act"]

        if True:
            buf_free = [P2_DVE, P2_DVE]
            sum_free = [P2_ACT, P2_ACT]
            it = 0
            for da, db_, cosb, dst in ((d_th, d_thn, True, s_cos),
                                       (d_ph, d_phn, False, s_sin)):
                for dt_ in range(NDT):
                    par = it % 2
                    la = em.op("sync", lambda e, a=da, d=dt_, p=par:
                               e.dma_start(out=s_ta[p][:, :],
                                           in_=a[d * 128:(d + 1) * 128, :]),
                               waits=[("dve", buf_free[par]),
                                      ("act", sum_free[par]),
                                      ("ldta", em.cnt["ldta"])],
                               inc="ldta", amt=16)
                    lb = em.op("sync", lambda e, a=db_, d=dt_, p=par:
                               e.dma_start(out=s_tb[p][:, :],
                                           in_=a[d * 128:(d + 1) * 128, :]),
                               waits=[("dve", buf_free[par]),
                                      ("act", sum_free[par]),
                                      ("ldtb", em.cnt["ldtb"])],
                               inc="ldtb", amt=16)
                    sm = em.op("vector", lambda e, p=par: e.tensor_tensor(
                        s_tsum[p][:, :], s_ta[p][:, :], s_tb[p][:, :],
                        op=ALU.add),
                        waits=[("ldta", la), ("ldtb", lb),
                               ("act", sum_free[par])],
                        inc="dve")
                    buf_free[par] = sm
                    if cosb:
                        a = em.op("scalar", lambda e, p=par:
                                  e.activation(
                                      out=s_tsum[p][:, :],
                                      in_=s_tsum[p][:, :], func=AF.Sin,
                                      bias=PI2()),
                                  waits=[("dve", sm), ("ldc", LDC)], inc="act")
                        sm = em.op("vector", lambda e, d=dt_, p=par, ds=dst:
                                   e.tensor_scalar(
                                       ds[:, d * 1024:(d + 1) * 1024],
                                       s_tsum[p][:, :], ONE(), None,
                                       op0=ALU.subtract),
                                   waits=[("act", a)], inc="dve")
                        buf_free[par] = sm
                        sum_free[par] = a
                        it += 1
                        continue
                    if True:
                        a = em.op("scalar", lambda e, d=dt_, p=par, ds=dst:
                                  e.activation(
                                      out=ds[:, d * 1024:(d + 1) * 1024],
                                      in_=s_tsum[p][:, :], func=AF.Sin),
                                  waits=[("dve", sm)], inc="act")
                    sum_free[par] = a
                    it += 1


            if NPAIR > 0:
                for _ in proj_steps(0):
                    pass
            for pair in range(NPAIR):
                par = pair % 2
                bg = (proj_steps(pair + 1) if pair + 1 < NPAIR else iter(()))
                bg_alive = [True]

                def pop_bg(n):
                    for _ in range(n):
                        if not bg_alive[0]:
                            return
                        try:
                            next(bg)
                        except StopIteration:
                            bg_alive[0] = False
                            return

                if TRUNC < 4:
                    pop_bg(1 << 30)
                    continue
                NU = 512
                qk_val = [0] * NU
                av_val = [0] * NU
                silu_val = [0] * NU
                pd = proj_done[pair]
                for i in range(NU + 2):
                    if i < NU:
                        sl = i % 3
                        qp, kc = i // 64, i % 64
                        w = [("act", silu_val[i - 3] if i >= 3 else 0)]
                        if i == 0:
                            w += [("act", pd[0]), ("dve", pd[1])]
                        em.op("tensor", lambda e, sl=sl, q=qp, k=kc, p=par:
                              e.matmul(
                                  p_all[:, 512 + sl * 1024:
                                        512 + sl * 1024 + 512],
                                  s_kT[p][0:64, k * 128:(k + 1) * 128],
                                  s_qT[p][0:64, q * 512:(q + 1) * 512],
                                  start=True, stop=True,
                                  tile_position=(0, 0)),
                              waits=w)
                        qk_val[i] = em.op("tensor",
                              lambda e, sl=sl, q=qp, k=kc, p=par: e.matmul(
                                  p_all[:, 512 + sl * 1024 + 512:
                                        512 + sl * 1024 + 1024],
                                  s_kT[p][64:128, k * 128:(k + 1) * 128],
                                  s_qT[p][64:128, q * 512:(q + 1) * 512],
                                  start=True, stop=True,
                                  tile_position=(64, 0)),
                              inc="pe")
                    u = i - 2
                    if u >= 0:
                        w = [("pe", qk_val[u]),
                             ("pe", av_val[u - 3] if u >= 3 else 0)]
                        sl = u % 3
                        silu_val[u] = em.op("scalar",
                              lambda e, sl=sl, ab=u % 3: e.activation(
                                  out=s_att[ab][:, :],
                                  in_=p_all[:, 512 + sl * 1024:
                                            512 + sl * 1024 + 1024],
                                  func=AF.Silu),
                              waits=w, inc="act")
                        qp, kc = u // 64, u % 64
                        w = [("act", silu_val[u])]
                        if kc == 0:
                            w.append(("dve", zm_val))
                        em.op("tensor", lambda e, k=kc, ab=u % 3, p=par:
                              e.matmul(
                                  p_all[0:64, 0:512],
                                  s_vsq[p][:, k * 128:k * 128 + 64],
                                  s_att[ab][:, 0:512],
                                  start=(k == 0), stop=(k == 63),
                                  tile_position=(0, 0)),
                              waits=w)
                        av_val[u] = em.op("tensor",
                              lambda e, k=kc, ab=u % 3, p=par: e.matmul(
                                  p_all[64:128, 0:512],
                                  s_vsq[p][:, k * 128 + 64:k * 128 + 128],
                                  s_att[ab][:, 512:1024],
                                  start=(k == 0), stop=(k == 63),
                                  tile_position=(0, 64)),
                              inc="pe")
                        if kc == 63:
                            zp = (qp % 2)
                            zm_val = em.op("vector",
                                  lambda e, q=qp, p=par, zp=zp:
                                  e.tensor_tensor(
                                      s_z[zp][:, :], p_all[:, 0:512],
                                      s_uT[p][:, q * 512:(q + 1) * 512],
                                      op=ALU.mult),
                                  waits=[("pe", av_val[u]),
                                         (f"stz{zp}", z_free[zp])],
                                  inc="dve")
                            z_free[zp] = em.op("gpsimd",
                                  lambda e, q=qp, pr=pair, zp=zp:
                                  e.dma_start(
                                      out=d_zscr[pr, :, q * 512:(q + 1) * 512],
                                      in_=s_z[zp][:, :]),
                                  waits=[("dve", zm_val),
                                         (f"stz{zp}", em.cnt[f"stz{zp}"])],
                                  inc=f"stz{zp}", amt=16)
                    if i % 2 == 1:
                        pop_bg(1)
                pop_bg(1 << 30)
                attn_read_done[par] = em.cnt["pe"]
        Z_ALL0 = em.cnt["stz0"]
        Z_ALL1 = em.cnt["stz1"]


        if TRUNC >= 6:
            with contextlib.ExitStack() as ts:
                s_w2 = ts.enter_context(nc.sbuf_tensor("w2b", [128, 8192], BF16))
                s_c2 = ts.enter_context(nc.sbuf_tensor("c2", [128, 8], F32))
                s_zc = [ts.enter_context(
                    nc.sbuf_tensor(f"zc{i}", [128, 4096], BF16)) for i in range(2)]
                s_fscr = [ts.enter_context(
                    nc.sbuf_tensor(f"fs{i}", [128, 4096], BF16)) for i in range(2)]
                s_zln = [ts.enter_context(
                    nc.sbuf_tensor(f"zl{i}", [128, 4096], BF16)) for i in range(2)]
                s_xc = ts.enter_context(nc.sbuf_tensor("xc0", [128, 4096], F32))
                s_bscr = ts.enter_context(nc.sbuf_tensor("bsc", [128, 4096], BF16))
                s_y = ts.enter_context(nc.sbuf_tensor("yb", [128, 4096], BF16))
                s_ysq = ts.enter_context(nc.sbuf_tensor("ysq", [128, 4096], BF16))
                s_x1 = ts.enter_context(nc.sbuf_tensor("x1", [128, 4096], F32))
                s_x1b = ts.enter_context(nc.sbuf_tensor("x1b", [128, 4096], BF16))
                s_h = ts.enter_context(nc.sbuf_tensor("hb", [128, 4096], BF16))
                s_mag = ts.enter_context(nc.sbuf_tensor("mag", [128, 4096], BF16))
                s_m2 = ts.enter_context(nc.sbuf_tensor("m2", [128, 4096], BF16))
                s_sqa = ts.enter_context(nc.sbuf_tensor("sqa", [128, 512], F32))
                s_sqb = ts.enter_context(nc.sbuf_tensor("sqb", [128, 512], F32))
                s_mu = [ts.enter_context(
                    nc.sbuf_tensor("mu0", [128, 512], F32))] * 3
                s_vr = [ts.enter_context(
                    nc.sbuf_tensor("vr0", [128, 512], F32))] * 3
                _r3 = [ts.enter_context(
                    nc.sbuf_tensor(f"r{i}", [128, 512], F32)) for i in range(3)]
                _mr3 = [ts.enter_context(
                    nc.sbuf_tensor(f"mr{i}", [128, 512], F32)) for i in range(3)]
                s_r = _r3 + [_r3[1]]
                s_mr = _mr3 + [_mr3[1]]

                lw2 = em.op("gpsimd", lambda e: e.dma_start(
                    out=s_w2[:, :].rearrange("p (d c) -> p d c", d=NDT),
                    in_=d_w2.rearrange("(d p) c -> p d c", p=128)),
                    waits=[("ldc", LDC), ("ldw", em.cnt["ldw"]),
                           ("pe", em.cnt["pe"]), ("act", em.cnt["act"]),
                           ("dve", em.cnt["dve"])],
                    inc="ldw", amt=16)
                c2mm = None
                for f in range(8):
                    for d in range(NDT):
                        c2mm = em.op("tensor", lambda e, f=f, d=d: e.matmul(
                            p_all[:, 3072 + f:3072 + f + 1],
                            s_w2[:, d * 1024 + f * 128:d * 1024 + (f + 1) * 128],
                            s_betabf[:, d:d + 1],
                            start=(d == 0), stop=(d == NDT - 1)),
                            waits=([("ldw", lw2), ("ldc", LDC),
                                    ("act", em.cnt["act"]),
                                    ("dve", em.cnt["dve"])]
                                   if f == 0 and d == 0 else []),
                            inc="pe")
                c2dve = em.op("vector", lambda e: e.tensor_tensor(
                    s_c2[:, :], p_all[:, 3072:3080], s_gb[:, 16:24], op=ALU.add),
                    waits=[("pe", c2mm)], inc="dve")
                gma = None
                for d in range(NDT):
                    gma = em.op("vector", lambda e, d=d: e.tensor_scalar(
                        s_w2[:, d * 1024:(d + 1) * 1024],
                        s_w2[:, d * 1024:(d + 1) * 1024],
                        s_gb[:, d:d + 1], None, op0=ALU.mult),
                        waits=[("pe", c2mm)], inc="dve")
                W2RDY = gma

                zc_free = [(em.cnt["pe"], em.cnt["dve"])] * 2
                fscr_free = [em.cnt["act"], em.cnt["act"]]
                fscr_free = [em.cnt["dve"]] * 2
                zln_free = [em.cnt["pe"]] * 2
                bank_free = [c2dve, c2dve]
                xc_free = em.cnt["dve"]
                x2_done = 0
                stv = 0

                def rstd_chain(mwait, base, cpar, rix, extra=0):
                    em.op("vector", lambda e, b=base: e.tensor_copy(
                        s_mu[cpar][:, :], bank(b)),
                        waits=[("pe", mwait)], inc="dve")
                    em.op("vector", lambda e: e.tensor_tensor(
                        s_vr[cpar][:, :], s_mu[cpar][:, :], s_mu[cpar][:, :],
                        op=ALU.mult), inc="dve")
                    vfree = em.op("vector", lambda e, b=base: e.tensor_tensor(
                        s_vr[cpar][:, :], bank(b + 1), s_vr[cpar][:, :],
                        op=ALU.subtract), inc="dve")
                    em.op("scalar", lambda e: e.activation(
                        out=s_r[rix][:, :], in_=s_vr[cpar][:, :], func=AF.Ln,
                        bias=EPSB()), waits=[("dve", vfree), ("dve", extra)],
                        inc="act")
                    ar = em.op("scalar", lambda e: e.activation(
                        out=s_r[rix][:, :], in_=s_r[rix][:, :], func=AF.Exp,
                        scale=-0.5), inc="act")
                    mrv = em.op("vector", lambda e: e.tensor_tensor(
                        s_mr[rix][:, :], s_mu[cpar][:, :], s_r[rix][:, :],
                        op=ALU.mult), waits=[("act", ar)], inc="dve")
                    return vfree, mrv

                fstate = {}

                def front(n):
                    c0 = n * 512
                    par = n % 2
                    base = par * 2
                    lz = em.op("sync", lambda e, c=c0, p=par: e.dma_start(
                        out=s_zc[p][:, :].rearrange("p (d t) -> p d t", d=NP),
                        in_=d_zscr[:, :, c:c + 512].rearrange("d p t -> p d t")),
                        waits=[("stz0", Z_ALL0), ("stz1", Z_ALL1),
                               ("pe", zc_free[par][0]), ("dve", zc_free[par][1]),
                               (f"ldz{par}", em.cnt[f"ldz{par}"])],
                        inc=f"ldz{par}", amt=16)
                    zq = em.op("scalar", lambda e, p=par: e.activation(
                        out=s_fscr[p][:, :], in_=s_zc[p][:, :], func=AF.Square),
                        waits=[(f"ldz{par}", lz), ("dve", fscr_free[par])],
                        inc="act")
                    for d in range(NDT):
                        em.op("tensor", lambda e, d=d, p=par, b=base: e.matmul(
                            bank(b), s_ones[:, :],
                            s_zc[p][:, d * 512:(d + 1) * 512],
                            start=(d == 0), stop=(d == NDT - 1)),
                            waits=([(f"ldz{par}", lz), ("dve", bank_free[par])]
                                   if d == 0 else []))
                    st2 = None
                    for d in range(NDT):
                        st2 = em.op("tensor", lambda e, d=d, p=par, b=base:
                                    e.matmul(
                                        bank(b + 1), s_ones[:, :],
                                        s_fscr[p][:, d * 512:(d + 1) * 512],
                                        start=(d == 0), stop=(d == NDT - 1)),
                                    waits=([("act", zq)] if d == 0 else []),
                                    inc="pe")
                    bfree, mr2 = rstd_chain(st2, base, 2, 0)
                    bank_free[par] = bfree
                    zl = None
                    for d in range(NDT):
                        em.op("vector", lambda e, d=d, p=par: e.tensor_tensor(
                            s_fscr[p][:, d * 512:(d + 1) * 512],
                            s_zc[p][:, d * 512:(d + 1) * 512], s_r[0][:, :],
                            op=ALU.mult),
                            waits=[("dve", mr2), ("pe", zln_free[par])],
                            inc="dve")
                        zl = em.op("vector", lambda e, d=d, p=par: e.tensor_tensor(
                            s_zln[p][:, d * 512:(d + 1) * 512],
                            s_fscr[p][:, d * 512:(d + 1) * 512], s_mr[0][:, :],
                            op=ALU.subtract), inc="dve")
                    fscr_free[par] = zl
                    fstate[n] = (zl, lz)

                front(0)
                for n in range(TAILN):
                    c0 = n * 512
                    par = n % 2
                    base = par * 2
                    zl, lz = fstate[n]
                    ysq_l = None
                    ypar_dve = [0, 0]
                    for f in range(8):
                        mm = None
                        for d in range(NDT):
                            w = []
                            if d == 0:
                                w = [("dve", zl), ("dve", W2RDY),
                                     ("dve", bank_free[par]),
                                     ("dve", ypar_dve[f % 2])]
                            mm = em.op("tensor", lambda e, f=f, d=d, p=par,
                                       b=base: e.matmul(
                                           bank(b + f % 2),
                                           s_w2[:, d * 1024 + f * 128:
                                                d * 1024 + (f + 1) * 128],
                                           s_zln[p][:, d * 512:(d + 1) * 512],
                                           start=(d == 0), stop=(d == NDT - 1)),
                                       waits=w, inc="pe")
                        if f == 7:
                            zc_free[par] = (mm, zl)
                            zln_free[par] = mm
                        yev = em.op("vector", lambda e, f=f, b=base:
                                    e.tensor_tensor(
                                        s_y[:, f * 512:(f + 1) * 512],
                                        bank(b + f % 2),
                                        s_c2[:, f:f + 1].broadcast_to([128, 512]),
                                        op=ALU.add),
                                    waits=[("pe", mm), ("dve", c2dve)], inc="dve")
                        ysq_l = em.op("scalar", lambda e, f=f: e.activation(
                            out=s_ysq[:, f * 512:(f + 1) * 512],
                            in_=s_y[:, f * 512:(f + 1) * 512],
                            func=AF.Square),
                            waits=[("dve", yev)], inc="act")
                        ypar_dve[f % 2] = yev
                    for f in range(8):
                        em.op("tensor", lambda e, f=f, b=base: e.matmul(
                            bank(b), s_ones[:, :], s_y[:, f * 512:(f + 1) * 512],
                            start=(f == 0), stop=(f == 7)),
                            waits=([("dve", yev), ("dve", ypar_dve[0]),
                                    ("dve", ypar_dve[1])] if f == 0 else []))
                    sty = None
                    for f in range(8):
                        sty = em.op("tensor", lambda e, f=f, b=base: e.matmul(
                            bank(b + 1), s_ones[:, :],
                            s_ysq[:, f * 512:(f + 1) * 512],
                            start=(f == 0), stop=(f == 7)),
                            waits=([("act", ysq_l)] if f == 0 else []), inc="pe")
                    bfree, mry = rstd_chain(sty, base, 1, 1, extra=x2_done)
                    lx = em.op("sync", lambda e, c=c0: e.dma_start(
                        out=s_xc[:, :].rearrange("p (d t) -> p d t", d=NDT),
                        in_=d_xT[:, c:c + 512]
                        .rearrange("(d p) t -> p d t", p=128)),
                        waits=[("dve", xc_free), ("ldxc", em.cnt["ldxc"]),
                               ("stout", em.cnt["stout"])],
                        inc="ldxc", amt=16)
                    x1b_l = None
                    x1sq_l = None
                    for f in range(8):
                        em.op("vector", lambda e, f=f: e.tensor_tensor(
                            s_bscr[:, f * 512:(f + 1) * 512],
                            s_y[:, f * 512:(f + 1) * 512], s_r[1][:, :],
                            op=ALU.mult),
                            waits=[("dve", mry), ("ldxc", lx)], inc="dve")
                        em.op("vector", lambda e, f=f: e.tensor_tensor(
                            s_bscr[:, f * 512:(f + 1) * 512],
                            s_bscr[:, f * 512:(f + 1) * 512], s_mr[1][:, :],
                            op=ALU.subtract), inc="dve")
                        x1v = em.op("vector", lambda e, f=f: e.tensor_tensor(
                            s_x1[:, f * 512:(f + 1) * 512],
                            s_bscr[:, f * 512:(f + 1) * 512],
                            s_xc[:, f * 512:(f + 1) * 512],
                            op=ALU.add), inc="dve")
                        x1b_l = em.op("scalar", lambda e, f=f: e.activation(
                            out=s_x1b[:, f * 512:(f + 1) * 512],
                            in_=s_x1[:, f * 512:(f + 1) * 512], func=AF.Copy),
                            waits=[("dve", x1v)], inc="act")
                        x1sq_l = em.op("scalar", lambda e, f=f: e.activation(
                            out=s_ysq[:, f * 512:(f + 1) * 512],
                            in_=s_x1[:, f * 512:(f + 1) * 512], func=AF.Square),
                            inc="act")
                    xc_free = em.cnt["dve"]
                    for f in range(8):
                        em.op("tensor", lambda e, f=f, b=base: e.matmul(
                            bank(b), s_ones[:, :],
                            s_x1b[:, f * 512:(f + 1) * 512],
                            start=(f == 0), stop=(f == 7)),
                            waits=([("act", x1b_l), ("dve", bfree)]
                                   if f == 0 else []))
                    st3 = None
                    for f in range(8):
                        st3 = em.op("tensor", lambda e, f=f, b=base: e.matmul(
                            bank(b + 1), s_ones[:, :],
                            s_ysq[:, f * 512:(f + 1) * 512],
                            start=(f == 0), stop=(f == 7)),
                            waits=([("act", x1sq_l)] if f == 0 else []), inc="pe")
                    bfree, mr3 = rstd_chain(st3, base, 0, 2)
                    hl = None
                    for f in range(8):
                        em.op("vector", lambda e, f=f: e.tensor_tensor(
                            s_bscr[:, f * 512:(f + 1) * 512],
                            s_x1b[:, f * 512:(f + 1) * 512], s_r[2][:, :],
                            op=ALU.mult), waits=[("dve", mr3)], inc="dve")
                        hl = em.op("vector", lambda e, f=f: e.tensor_tensor(
                            s_h[:, f * 512:(f + 1) * 512],
                            s_bscr[:, f * 512:(f + 1) * 512], s_mr[2][:, :],
                            op=ALU.subtract), inc="dve")
                    bank_free[par] = bfree
                    if n + 1 < TAILN:
                        front(n + 1)
                    ri_free = [0, 0, 0, 0]
                    m2_l = 0
                    mag_l = None
                    for fo in range(8):
                        rb_ = 4 + (fo % 2) * 2
                        ib_ = 5 + (fo % 2) * 2
                        mmr = None
                        for d in range(NDT):
                            w = []
                            if d == 0:
                                w = [("dve", hl), ("act", ri_free[2 * (fo % 2)])]
                            mmr = em.op("tensor", lambda e, d=d, f=fo, b=rb_:
                                        e.matmul(
                                            bank(b),
                                            s_cos[:, d * 1024 + f * 128:
                                                  d * 1024 + (f + 1) * 128],
                                            s_h[:, d * 512:(d + 1) * 512],
                                            start=(d == 0), stop=(d == NDT - 1)),
                                        waits=w, inc="pe")
                        mmi = None
                        for d in range(NDT):
                            w = []
                            if d == 0:
                                w = [("act", ri_free[2 * (fo % 2) + 1])]
                            mmi = em.op("tensor", lambda e, d=d, f=fo, b=ib_:
                                        e.matmul(
                                            bank(b),
                                            s_sin[:, d * 1024 + f * 128:
                                                  d * 1024 + (f + 1) * 128],
                                            s_h[:, d * 512:(d + 1) * 512],
                                            start=(d == 0), stop=(d == NDT - 1)),
                                        waits=w, inc="pe")
                        sa = em.op("scalar", lambda e, b=rb_: e.activation(
                            out=s_sqa[:, :], in_=bank(b), func=AF.Square),
                            waits=[("pe", mmr), ("dve", m2_l)], inc="act")
                        ri_free[2 * (fo % 2)] = sa
                        sb_ = em.op("scalar", lambda e, b=ib_: e.activation(
                            out=s_sqb[:, :], in_=bank(b), func=AF.Square),
                            waits=[("pe", mmi)], inc="act")
                        ri_free[2 * (fo % 2) + 1] = sb_
                        m2_l = em.op("vector", lambda e, f=fo: e.tensor_tensor(
                            s_m2[:, f * 512:(f + 1) * 512], s_sqa[:, :],
                            s_sqb[:, :], op=ALU.add),
                            waits=[("act", sb_)], inc="dve")
                        em.op("scalar", lambda e, f=fo: e.activation(
                            out=s_mag[:, f * 512:(f + 1) * 512],
                            in_=s_m2[:, f * 512:(f + 1) * 512], func=AF.Ln,
                            bias=TINY()), waits=[("dve", m2_l)], inc="act")
                        mag_l = em.op("scalar", lambda e, f=fo: e.activation(
                            out=s_mag[:, f * 512:(f + 1) * 512],
                            in_=s_mag[:, f * 512:(f + 1) * 512], func=AF.Exp,
                            scale=0.5), inc="act")
                    for f in range(8):
                        em.op("tensor", lambda e, f=f, b=base: e.matmul(
                            bank(b), s_ones[:, :],
                            s_mag[:, f * 512:(f + 1) * 512],
                            start=(f == 0), stop=(f == 7)),
                            waits=([("act", mag_l), ("dve", bank_free[par])]
                                   if f == 0 else []))
                    st4 = None
                    for f in range(8):
                        st4 = em.op("tensor", lambda e, f=f, b=base: e.matmul(
                            bank(b + 1), s_ones[:, :],
                            s_m2[:, f * 512:(f + 1) * 512],
                            start=(f == 0), stop=(f == 7)),
                            waits=([("dve", m2_l)] if f == 0 else []), inc="pe")
                    bfree, mr4 = rstd_chain(st4, base, 1, 3)
                    bank_free[par] = bfree
                    prev_st = stv
                    xl = None
                    for f in range(8):
                        em.op("vector", lambda e, f=f: e.tensor_tensor(
                            s_bscr[:, f * 512:(f + 1) * 512],
                            s_mag[:, f * 512:(f + 1) * 512], s_r[3][:, :],
                            op=ALU.mult),
                            waits=[("dve", mr4), ("stout", prev_st)], inc="dve")
                        em.op("vector", lambda e, f=f: e.tensor_tensor(
                            s_bscr[:, f * 512:(f + 1) * 512],
                            s_bscr[:, f * 512:(f + 1) * 512], s_mr[3][:, :],
                            op=ALU.subtract), inc="dve")
                        xl = em.op("vector", lambda e, f=f: e.tensor_tensor(
                            s_xc[:, f * 512:(f + 1) * 512],
                            s_bscr[:, f * 512:(f + 1) * 512],
                            s_x1[:, f * 512:(f + 1) * 512], op=ALU.add),
                            inc="dve")
                    x2_done = xl
                    stv = em.op("gpsimd", lambda e, c=c0: e.dma_start(
                        out=d_out[:, c:c + 512].rearrange("(f p) t -> p f t",
                                                          p=128),
                        in_=s_xc[:, :].rearrange("p (f t) -> p f t", f=NDT)),
                        waits=[("dve", xl), ("stout", em.cnt["stout"])],
                        inc="stout", amt=16)
                em.op("gpsimd", lambda e: e.nop(),
                      waits=[("stout", em.cnt["stout"])])
                if int(_os.environ.get("KEXTRA2", "0")):
                    em.op("tensor", lambda e: e.matmul(
                        bank(2), s_cos[:, 0:128], s_cos[:, 0:512],
                        start=True, stop=True), inc="pe")
                if TAILN < 8:
                    waits = [(s_, em.cnt[s_]) for s_ in em.cnt]
                    _mk = _os.environ.get("KMARK", "cos")
                    msrc = {"cos": s_cos[:, 0:4096], "zln": s_zln[0][:, :],
                            "zc": s_zc[0][:, :], "w2": s_w2[:, 0:4096],
                            "r0": s_r[0][:, :], "mr0": s_mr[0][:, :]}[_mk]
                    em.op("gpsimd", lambda e, m=msrc: e.dma_start(
                        out=d_out[0:128, 0:m.shape[-1]], in_=m),
                        waits=waits, inc="stout", amt=16)
                    em.op("gpsimd", lambda e: e.nop(),
                          waits=[("stout", em.cnt["stout"])])
        else:
            waits = [(s_, em.cnt[s_]) for s_ in em.cnt]
            em.op("gpsimd", lambda e: e.dma_start(
                out=d_out[0:128, :], in_=s_cos[:, 0:4096]),
                waits=waits, inc="stout", amt=16)
            em.op("gpsimd", lambda e: e.nop(),
                  waits=[("stout", em.cnt["stout"])])

        for name in em.cnt:
            sems[name] = es.enter_context(nc.semaphore(name))
        with nc.Block() as block:
            def mk(eng_name):
                def body(eng):
                    for waits, fn, inc, amt in em.ops[eng_name]:
                        for sname, val in waits:
                            eng.wait_ge(sems[sname], val)
                        r = fn(eng)
                        if inc is not None:
                            assert r is not None, f"{eng_name} inc={inc}"
                            r.then_inc(sems[inc], amt)
                return body
            block.sync(mk("sync"))
            block.gpsimd(mk("gpsimd"))
            block.tensor(mk("tensor"))
            block.scalar(mk("scalar"))
            block.vector(mk("vector"))

    return nc


def kernel(L0_attn_x, L0_attn_W1, L0_attn_b1, L0_attn_W2, L0_attn_b2,
           L0_attn_gamma, L0_attn_beta, L0_ffn_theta, L0_ffn_phi,
           L0_ffn_theta_noise, L0_ffn_phi_noise):
    x = np.asarray(L0_attn_x, np.float32)
    W1 = np.asarray(L0_attn_W1, np.float32)
    b1 = np.asarray(L0_attn_b1, np.float32)
    W2 = np.ascontiguousarray(np.asarray(L0_attn_W2, np.float32))
    b2 = np.asarray(L0_attn_b2, np.float32)
    gamma = np.asarray(L0_attn_gamma, np.float32)
    beta = np.asarray(L0_attn_beta, np.float32)
    th = np.ascontiguousarray(np.asarray(L0_ffn_theta, np.float32))
    thn = np.ascontiguousarray(np.asarray(L0_ffn_theta_noise, np.float32))
    ph = np.ascontiguousarray(np.asarray(L0_ffn_phi, np.float32))
    phn = np.ascontiguousarray(np.asarray(L0_ffn_phi_noise, np.float32))

    xT = np.ascontiguousarray(x.transpose(0, 2, 1))

    w1p = np.empty((NP, D, 512), np.float32)
    for p in range(NP):
        for k in range(4):
            w1p[p, :, k * 128:(k + 1) * 128] = \
                W1[:, k * D + p * 128:k * D + (p + 1) * 128]
    b1c = np.ascontiguousarray(
        b1.reshape(4, 8, 128).transpose(2, 0, 1).reshape(128, 32))
    gb = np.ascontiguousarray(np.concatenate(
        [gamma.reshape(8, 128).T, beta.reshape(8, 128).T,
         b2.reshape(8, 128).T], axis=1))
    consts = np.concatenate(
        [np.eye(128, dtype=np.float32) / S,
         np.ones((128, 128), np.float32) / D], axis=1).astype(BF)
    cb = np.tile(np.array([[math.pi / 2, EPS, 1e-30, 1024.0,
                            1.0, 0.0, 0.0, 0.0]], np.float32), (128, 1))

    nc = build_nc()
    in_maps = []
    for c in range(8):
        b, half = c // 2, c % 2
        off = half * SH
        xrot = np.ascontiguousarray(
            np.concatenate([xT[b][:, off:], xT[b][:, :off]], axis=1))
        in_maps.append(dict(
            xT=xrot, w1p=w1p, b1c=b1c, gb=gb, consts=consts, cb=cb,
            w2=W2, th=th, thn=thn, ph=ph, phn=phn))
    res = run_bass_kernel_spmd(nc, in_maps, core_ids=list(range(8)))
    out = np.empty_like(x)
    for c in range(8):
        b, half = c // 2, c % 2
        out[b, half * SH:(half + 1) * SH, :] = res.results[c]["out"].T
    return out


if __name__ == "__main__":
    import reference as _r
    inp = _r.setup_inputs(0)
    want = np.asarray(_r.reference(**inp))
    got = kernel(**{k: np.asarray(v) for k, v in inp.items()})
    err = got - want
    rvr = float((err ** 2).mean() / (want ** 2).mean())
    print("rvr", rvr, "max", float(np.abs(err).max()))


# version 12
# speedup vs baseline: 1.0716x; 1.0019x over previous
import contextlib
import math
import numpy as np
import ml_dtypes
import concourse.bass as bass
import concourse.mybir as mybir
from concourse.bass_utils import run_bass_kernel_spmd

F32 = mybir.dt.float32
BF16 = mybir.dt.bfloat16
AF = mybir.ActivationFunctionType
ALU = mybir.AluOpType
BF = ml_dtypes.bfloat16

D = 1024
S = 8192
SH = 4096
NDT = 8
NP = 8
EPS = 1e-5

ENGINES = ("sync", "gpsimd", "tensor", "scalar", "vector")
SELF_SEM = {"vector": "dve", "scalar": "act"}
import os as _os
TRUNC = int(_os.environ.get("KTRUNC", "6"))
TAILN = int(_os.environ.get("KTAILN", "8"))
KC2 = _os.environ.get("KC2", "1") == "1"
KGM = _os.environ.get("KGM", "1") == "1"
KW2 = _os.environ.get("KW2", "1") == "1"
KSTEP = int(_os.environ.get("KSTEP", "99"))
KYALT = int(_os.environ.get("KYALT", "0"))
KYB = int(_os.environ.get("KYB", "2"))
KYF = int(_os.environ.get("KYF", "8"))
KYSRC = _os.environ.get("KYSRC", "w2")
KYRHS = _os.environ.get("KYRHS", "zln")
KYW = _os.environ.get("KYW", "1") == "1"
KYD = int(_os.environ.get("KYD", "8"))
KYN = int(_os.environ.get("KYN", "512"))


class Em:

    def __init__(self):
        self.ops = {e: [] for e in ENGINES}
        self.cnt = {}
        self.last_wait = {e: {} for e in ENGINES}

    def sem(self, name):
        self.cnt.setdefault(name, 0)

    def op(self, eng, fn, waits=(), inc=None, amt=1):
        waits = list(waits)
        if eng in SELF_SEM:
            assert inc == SELF_SEM[eng], f"{eng} op must inc {SELF_SEM[eng]}"
            waits.append((SELF_SEM[eng], self.cnt[SELF_SEM[eng]]))
        w = []
        for sname, val in waits:
            assert val <= self.cnt[sname], (
                f"deadlock: {eng} waits {sname}>={val} > {self.cnt[sname]}")
            if val <= 0 or self.last_wait[eng].get(sname, -1) >= val:
                continue
            self.last_wait[eng][sname] = val
            w.append((sname, val))
        v = None
        if inc is not None:
            self.cnt[inc] += amt
            v = self.cnt[inc]
        self.ops[eng].append((w, fn, inc, amt))
        return v


def build_nc():
    nc = bass.Bass("TRN2", target_bir_lowering=False)

    d_xT = nc.dram_tensor("xT", [D, S], F32, kind="ExternalInput")
    d_w1p = nc.dram_tensor("w1p", [NP, D, 512], F32, kind="ExternalInput")
    d_b1c = nc.dram_tensor("b1c", [128, 32], F32, kind="ExternalInput")
    d_gb = nc.dram_tensor("gb", [128, 24], F32, kind="ExternalInput")
    d_consts = nc.dram_tensor("consts", [128, 256], BF16, kind="ExternalInput")
    d_cb = nc.dram_tensor("cb", [128, 8], F32, kind="ExternalInput")
    d_w2 = nc.dram_tensor("w2", [D, D], F32, kind="ExternalInput")
    d_th = nc.dram_tensor("th", [D, D], F32, kind="ExternalInput")
    d_thn = nc.dram_tensor("thn", [D, D], F32, kind="ExternalInput")
    d_ph = nc.dram_tensor("ph", [D, D], F32, kind="ExternalInput")
    d_phn = nc.dram_tensor("phn", [D, D], F32, kind="ExternalInput")
    d_out = nc.dram_tensor("out", [D, SH], F32, kind="ExternalOutput")
    d_ln1 = nc.dram_tensor("ln1scr", [NDT, 128, S], BF16)
    d_zscr = nc.dram_tensor("zscr", [NP, 128, SH], BF16)

    em = Em()
    for s in ("pe", "act", "dve", "ldc", "ldta", "ldtb", "ldx", "stln0",
              "stln1", "ldw", "ldr0", "ldr1", "stz0", "stz1", "ldz0", "ldz1",
              "ldxc", "stout"):
        em.sem(s)

    with contextlib.ExitStack() as es:
        s_id = es.enter_context(nc.sbuf_tensor("c_id", [128, 128], BF16))
        s_ones = es.enter_context(nc.sbuf_tensor("c_ones", [128, 128], BF16))
        s_b1c = es.enter_context(nc.sbuf_tensor("c_b1c", [128, 32], F32))
        s_gb = es.enter_context(nc.sbuf_tensor("c_gb", [128, 24], F32))
        s_cb = es.enter_context(nc.sbuf_tensor("c_cb", [128, 8], F32))
        s_betabf = es.enter_context(nc.sbuf_tensor("c_bb", [128, 8], BF16))
        s_cos = es.enter_context(nc.sbuf_tensor("cosm", [128, 8192], BF16))
        s_sin = es.enter_context(nc.sbuf_tensor("sinm", [128, 8192], BF16))
        p_all = es.enter_context(nc.psum_tensor("p_all", [128, 4096], F32))
        sems = {}

        def bank(i, w=512):
            return p_all[:, i * 512:i * 512 + w]

        for dst_, src_ in ((s_id, lambda: d_consts[:, 0:128]),
                           (s_ones, lambda: d_consts[:, 128:256]),
                           (s_b1c, lambda: d_b1c[:, :]),
                           (s_gb, lambda: d_gb[:, :]),
                           (s_cb, lambda: d_cb[:, :])):
            em.op("sync", lambda e, d=dst_, s=src_: e.dma_start(
                out=d[:, :], in_=s()), waits=[("ldc", em.cnt["ldc"])],
                inc="ldc", amt=16)
        em.op("gpsimd", lambda e: e.dma_start(out=s_betabf[:, :],
              in_=d_gb[:, 8:16]), waits=[("ldw", em.cnt["ldw"])],
              inc="ldw", amt=16)
        LDC = em.cnt["ldc"]
        PI2 = lambda: s_cb[:, 0:1]
        EPSB = lambda: s_cb[:, 1:2]
        TINY = lambda: s_cb[:, 2:3]
        K1024 = lambda: s_cb[:, 3:4]
        ONE = lambda: s_cb[:, 4:5]

        with contextlib.ExitStack() as ps:
            s_pxt = ps.enter_context(nc.sbuf_tensor("pxt", [128, 8192], F32))
            s_px = [s_pxt[:, d * 1024:(d + 1) * 1024] for d in range(NDT)]
            s_pbf = [ps.enter_context(
                nc.sbuf_tensor(f"pb{i}", [128, 1024], BF16)) for i in range(2)]
            s_psq = [ps.enter_context(
                nc.sbuf_tensor(f"pq{i}", [128, 1024], BF16)) for i in range(2)]
            s_mus = ps.enter_context(nc.sbuf_tensor("mus", [128, 1024], F32))
            s_var = ps.enter_context(nc.sbuf_tensor("var", [128, 1024], F32))
            s_rb = ps.enter_context(nc.sbuf_tensor("rb", [128, 1024], F32))
            s_t1 = ps.enter_context(nc.sbuf_tensor("t1", [128, 1024], F32))
            s_l1c = [ps.enter_context(
                nc.sbuf_tensor(f"l1{i}", [128, 1024], BF16)) for i in range(2)]
            px_free = 0
            pbf_free = [0, 0]
            sq_free = [0, 0]
            psq_free = [0, 0]
            l1_free = [0, 0]
            chain_free = 0
            mstat_done = 0
            for ch in range(8 if TRUNC >= 2 else 0):
                c0 = ch * 1024
                bar = ([("dve", em.cnt["dve"]), ("act", em.cnt["act"])]
                       if ch == 0 else [])
                ldp = em.op("sync", lambda e, c=c0: e.dma_start(
                    out=s_pxt[:, :].rearrange("p (d t) -> p d t", d=NDT),
                    in_=d_xT[:, c:c + 1024]
                    .rearrange("(d p) t -> p d t", p=128)),
                    waits=[("dve", px_free), ("ldx", em.cnt["ldx"])] + bar,
                    inc="ldx", amt=16)
                lds = [ldp] * NDT
                for d in range(NDT):
                    cv = em.op("vector", lambda e, d=d: e.tensor_copy(
                        s_pbf[d % 2][:, :], s_px[d][:, :]),
                        waits=[("ldx", lds[d]), ("pe", pbf_free[d % 2]),
                               ("act", sq_free[d % 2])],
                        inc="dve")
                    sq = em.op("scalar", lambda e, d=d: e.activation(
                        out=s_psq[d % 2][:, :], in_=s_pbf[d % 2][:, :],
                        func=AF.Square),
                        waits=[("dve", cv), ("pe", psq_free[d % 2])],
                        inc="act")
                    sq_free[d % 2] = sq
                    mlast = None
                    for h in range(2):
                        mlast = em.op("tensor", lambda e, d=d, h=h: e.matmul(
                            bank(h), s_ones[:, :],
                            s_pbf[d % 2][:, h * 512:(h + 1) * 512],
                            start=(d == 0), stop=(d == NDT - 1)),
                            waits=([("dve", cv), ("ldc", LDC),
                                    ("dve", chain_free)] if h == 0 else []),
                            inc="pe")
                    pbf_free[d % 2] = mlast
                    for h in range(2):
                        mlast = em.op("tensor", lambda e, d=d, h=h: e.matmul(
                            bank(2 + h), s_ones[:, :],
                            s_psq[d % 2][:, h * 512:(h + 1) * 512],
                            start=(d == 0), stop=(d == NDT - 1)),
                            waits=([("act", sq)] if h == 0 else []),
                            inc="pe")
                    psq_free[d % 2] = mlast
                    mstat_done = mlast
                cmu = em.op("vector", lambda e: e.tensor_copy(
                    s_mus[:, :], p_all[:, 0:1024]),
                    waits=[("pe", mstat_done)], inc="dve")
                em.op("vector", lambda e: e.tensor_tensor(
                    s_var[:, :], s_mus[:, :], s_mus[:, :], op=ALU.mult),
                    inc="dve")
                cvar = em.op("vector", lambda e: e.tensor_tensor(
                    s_var[:, :], p_all[:, 1024:2048], s_var[:, :],
                    op=ALU.subtract), inc="dve")
                chain_free = cvar
                em.op("scalar", lambda e: e.activation(
                    out=s_rb[:, :], in_=s_var[:, :], func=AF.Ln, bias=EPSB()),
                    waits=[("dve", cvar)], inc="act")
                arb = em.op("scalar", lambda e: e.activation(
                    out=s_rb[:, :], in_=s_rb[:, :], func=AF.Exp, scale=-0.5),
                    inc="act")
                for d in range(NDT):
                    t = em.op("vector", lambda e, d=d: e.tensor_tensor(
                        s_t1[:, :], s_px[d][:, :], s_mus[:, :],
                        op=ALU.subtract),
                        waits=[("act", arb)], inc="dve")
                    if d == NDT - 1:
                        px_free = t
                    lc = em.op("vector", lambda e, d=d: e.tensor_tensor(
                        s_l1c[d % 2][:, :], s_t1[:, :], s_rb[:, :],
                        op=ALU.mult),
                        waits=[(f"stln{d % 2}", l1_free[d % 2])],
                        inc="dve")
                    st = em.op("gpsimd", lambda e, d=d, c=c0: e.dma_start(
                        out=d_ln1[d, :, c:c + 1024], in_=s_l1c[d % 2][:, :]),
                        waits=[("dve", lc),
                               (f"stln{d % 2}", em.cnt[f"stln{d % 2}"])],
                        inc=f"stln{d % 2}", amt=16)
                    l1_free[d % 2] = st
        LN1_ST0 = em.cnt["stln0"]
        LN1_ST1 = em.cnt["stln1"]

        with contextlib.ExitStack() as ps:
            s_kT = [ps.enter_context(nc.sbuf_tensor(f"kT{i}", [128, S], BF16))
                    for i in range(2)]
            s_vsq = [ps.enter_context(nc.sbuf_tensor(f"vs{i}", [128, S], BF16))
                     for i in range(2)]
            s_qT = [ps.enter_context(nc.sbuf_tensor(f"qT{i}", [128, SH], BF16))
                    for i in range(2)]
            s_uT = [ps.enter_context(nc.sbuf_tensor(f"uT{i}", [128, SH], BF16))
                    for i in range(2)]
            s_vfm = ps.enter_context(nc.sbuf_tensor("vfm", [128, S], BF16))
            s_rhs = [ps.enter_context(
                nc.sbuf_tensor(f"rh{i}", [128, 4096], BF16)) for i in range(2)]
            s_w1 = ps.enter_context(nc.sbuf_tensor("w1b", [128, 4096], BF16))
            s_att = [ps.enter_context(
                nc.sbuf_tensor(f"at{i}", [128, 1024], BF16)) for i in range(3)]
            s_z = [ps.enter_context(
                nc.sbuf_tensor(f"z{i}", [128, 512], BF16)) for i in range(2)]

            w1_free = 0
            rhs_free = [0, 0]
            unit_free = [0] * 4
            vt_free = 0
            attn_read_done = [0, 0]
            z_free = [0, 0]
            zm_val = 0
            proj_done_act = 0
            proj_done_dve = 0

            s_ta = [ps.enter_context(nc.sbuf_tensor(f"ta{i}", [128, 1024], F32))
                    for i in range(2)]
            s_tb = [ps.enter_context(nc.sbuf_tensor(f"tb{i}", [128, 1024], F32))
                    for i in range(2)]
            s_tsum = [ps.enter_context(
                nc.sbuf_tensor(f"ts{i}", [128, 1024], F32)) for i in range(2)]
            s_pstg = [ps.enter_context(
                nc.sbuf_tensor(f"pst{i}", [128, 512], F32)) for i in range(3)]
            NPAIR = 0 if TRUNC < 3 else (NP if TRUNC >= 5 else 1)
            proj_done = {}
            bank7 = [0]
            pstg_free = [0, 0, 0]

            def proj_steps(pair):
                nonlocal w1_free, proj_done
                par = pair % 2
                bar = ([("dve", P2_DVE), ("act", P2_ACT),
                        ("pe", em.cnt["pe"])] if pair == 0 else [])
                lw = em.op("gpsimd", lambda e, p=pair: e.dma_start(
                    out=s_w1[:, :].rearrange("p (d c) -> p d c", d=NDT),
                    in_=d_w1p[p].rearrange("(d p) c -> p d c", p=128)),
                    waits=[("pe", w1_free), ("ldc", LDC),
                           ("ldw", em.cnt["ldw"])] + bar,
                    inc="ldw", amt=16)
                yield
                g = 0
                kinds_dst = {"k": (3, s_kT[par]), "v": (1, s_vfm),
                             "q": (2, s_qT[par]), "u": (0, s_uT[par])}
                pend_vt = None
                for j in range(17):
                    if j < 16:
                        jp = j % 2
                        lr = em.op("sync", lambda e, j=j, p=jp: e.dma_start(
                            out=s_rhs[p][:, :].rearrange(
                                "p (d t) -> p d t", d=NDT),
                            in_=d_ln1[:, :, j * 512:(j + 1) * 512]
                            .rearrange("d p t -> p d t")),
                            waits=[("pe", rhs_free[jp]),
                                   ("stln0", min(LN1_ST0, 64 * (j // 2 + 1))),
                                   ("stln1", min(LN1_ST1, 64 * (j // 2 + 1))),
                                   (f"ldr{jp}", em.cnt[f"ldr{jp}"])],
                            inc=f"ldr{jp}", amt=16)
                        yield
                        kinds = (["k", "v", "q", "u"] if j < 8
                                 else ["k", "v"])
                        mm_last = None
                        for kind in kinds:
                            ki, dst = kinds_dst[kind]
                            gp = g % 3
                            for d in range(NDT):
                                w = []
                                if d == 0:
                                    w = [(f"ldr{jp}", lr), ("ldw", lw),
                                         ("dve", bank7[0]),
                                         ("pe", attn_read_done[par])]
                                mm_last = em.op("tensor",
                                    lambda e, d=d, k=ki, p=jp: e.matmul(
                                        bank(7),
                                        s_w1[:, d * 512 + k * 128:
                                             d * 512 + k * 128 + 128],
                                        s_rhs[p][:, d * 512:(d + 1) * 512],
                                        start=(d == 0), stop=(d == NDT - 1)),
                                    waits=w, inc="pe")
                                if d % 2 == 1:
                                    yield
                            cp = em.op("vector", lambda e, gp=gp:
                                       e.tensor_copy(s_pstg[gp][:, :],
                                                     bank(7)),
                                       waits=[("pe", mm_last),
                                              ("act", pstg_free[gp])],
                                       inc="dve")
                            bank7[0] = cp
                            yield
                            bcol = ki * 8 + pair
                            ev = em.op("scalar", lambda e, gp=gp, j=j,
                                       b=bcol, ds=dst: e.activation(
                                           out=ds[:, j * 512:(j + 1) * 512],
                                           in_=s_pstg[gp][:, :],
                                           func=AF.Silu,
                                           bias=s_b1c[:, b:b + 1]),
                                       waits=[("dve", cp),
                                              ("pe", attn_read_done[par])],
                                       inc="act")
                            pstg_free[gp] = ev
                            if kind == "v":
                                vdef = (j, ev)
                            g += 1
                            yield
                        rhs_free[jp] = mm_last
                    if pend_vt is not None:
                        vj, vev = pend_vt
                        tlast = None
                        for c in range(4):
                            tlast = em.op("tensor",
                                lambda e, vj=vj, c=c: e.matmul(
                                    p_all[:, 3584 + c * 128:
                                          3584 + (c + 1) * 128],
                                    s_vfm[:, vj * 512 + c * 128:
                                          vj * 512 + (c + 1) * 128],
                                    s_id[:, :], start=True, stop=True),
                                waits=([("act", vev), ("dve", bank7[0]),
                                        ("ldc", LDC)] if c == 0 else []),
                                inc="pe")
                        yield
                        bank7[0] = em.op("vector", lambda e, vj=vj, p=par:
                                         e.tensor_copy(
                                             s_vsq[p][:, vj * 512:
                                                      (vj + 1) * 512],
                                             bank(7)),
                                         waits=[("pe", tlast),
                                                ("pe", attn_read_done[par])],
                                         inc="dve")
                        yield
                    if j < 16:
                        pend_vt = vdef
                proj_done[pair] = (em.cnt["act"], em.cnt["dve"])
                w1_free = em.cnt["pe"]

            P2_DVE = em.cnt["dve"]
            P2_ACT = em.cnt["act"]

        if True:
            buf_free = [P2_DVE, P2_DVE]
            sum_free = [P2_ACT, P2_ACT]
            it = 0
            for da, db_, cosb, dst in ((d_th, d_thn, True, s_cos),
                                       (d_ph, d_phn, False, s_sin)):
                for dt_ in range(NDT):
                    par = it % 2
                    la = em.op("sync", lambda e, a=da, d=dt_, p=par:
                               e.dma_start(out=s_ta[p][:, :],
                                           in_=a[d * 128:(d + 1) * 128, :]),
                               waits=[("dve", buf_free[par]),
                                      ("act", sum_free[par]),
                                      ("ldta", em.cnt["ldta"])],
                               inc="ldta", amt=16)
                    lb = em.op("sync", lambda e, a=db_, d=dt_, p=par:
                               e.dma_start(out=s_tb[p][:, :],
                                           in_=a[d * 128:(d + 1) * 128, :]),
                               waits=[("dve", buf_free[par]),
                                      ("act", sum_free[par]),
                                      ("ldtb", em.cnt["ldtb"])],
                               inc="ldtb", amt=16)
                    sm = em.op("vector", lambda e, p=par: e.tensor_tensor(
                        s_tsum[p][:, :], s_ta[p][:, :], s_tb[p][:, :],
                        op=ALU.add),
                        waits=[("ldta", la), ("ldtb", lb),
                               ("act", sum_free[par])],
                        inc="dve")
                    buf_free[par] = sm
                    if cosb:
                        a = em.op("scalar", lambda e, p=par:
                                  e.activation(
                                      out=s_tsum[p][:, :],
                                      in_=s_tsum[p][:, :], func=AF.Sin,
                                      bias=PI2()),
                                  waits=[("dve", sm), ("ldc", LDC)], inc="act")
                        sm = em.op("vector", lambda e, d=dt_, p=par, ds=dst:
                                   e.tensor_scalar(
                                       ds[:, d * 1024:(d + 1) * 1024],
                                       s_tsum[p][:, :], ONE(), None,
                                       op0=ALU.subtract),
                                   waits=[("act", a)], inc="dve")
                        buf_free[par] = sm
                        sum_free[par] = a
                        it += 1
                        continue
                    if True:
                        a = em.op("scalar", lambda e, d=dt_, p=par, ds=dst:
                                  e.activation(
                                      out=ds[:, d * 1024:(d + 1) * 1024],
                                      in_=s_tsum[p][:, :], func=AF.Sin),
                                  waits=[("dve", sm)], inc="act")
                    sum_free[par] = a
                    it += 1


            if NPAIR > 0:
                for _ in proj_steps(0):
                    pass
            for pair in range(NPAIR):
                par = pair % 2
                bg = (proj_steps(pair + 1) if pair + 1 < NPAIR else iter(()))
                bg_alive = [True]

                def pop_bg(n):
                    for _ in range(n):
                        if not bg_alive[0]:
                            return
                        try:
                            next(bg)
                        except StopIteration:
                            bg_alive[0] = False
                            return

                if TRUNC < 4:
                    pop_bg(1 << 30)
                    continue
                NU = 512
                qk_val = [0] * NU
                av_val = [0] * NU
                silu_val = [0] * NU
                pd = proj_done[pair]
                for i in range(NU + 3):
                    u = i - 3
                    if u >= 0:
                        w = [("pe", qk_val[u]),
                             ("pe", av_val[u - 3] if u >= 3 else 0)]
                        sl = u % 3
                        silu_val[u] = em.op("scalar",
                              lambda e, sl=sl, ab=u % 3: e.activation(
                                  out=s_att[ab][:, :],
                                  in_=p_all[:, 512 + sl * 1024:
                                            512 + sl * 1024 + 1024],
                                  func=AF.Silu),
                              waits=w, inc="act")
                        qp, kc = u // 64, u % 64
                        w = [("act", silu_val[u])]
                        if kc == 0:
                            w.append(("dve", zm_val))
                        em.op("tensor", lambda e, k=kc, ab=u % 3, p=par:
                              e.matmul(
                                  p_all[0:64, 0:512],
                                  s_vsq[p][:, k * 128:k * 128 + 64],
                                  s_att[ab][:, 0:512],
                                  start=(k == 0), stop=(k == 63),
                                  tile_position=(0, 0)),
                              waits=w)
                        av_val[u] = em.op("tensor",
                              lambda e, k=kc, ab=u % 3, p=par: e.matmul(
                                  p_all[64:128, 0:512],
                                  s_vsq[p][:, k * 128 + 64:k * 128 + 128],
                                  s_att[ab][:, 512:1024],
                                  start=(k == 0), stop=(k == 63),
                                  tile_position=(0, 64)),
                              inc="pe")
                        if kc == 63:
                            zp = (qp % 2)
                            zm_val = em.op("vector",
                                  lambda e, q=qp, p=par, zp=zp:
                                  e.tensor_tensor(
                                      s_z[zp][:, :], p_all[:, 0:512],
                                      s_uT[p][:, q * 512:(q + 1) * 512],
                                      op=ALU.mult),
                                  waits=[("pe", av_val[u]),
                                         (f"stz{zp}", z_free[zp])],
                                  inc="dve")
                            z_free[zp] = em.op("gpsimd",
                                  lambda e, q=qp, pr=pair, zp=zp:
                                  e.dma_start(
                                      out=d_zscr[pr, :, q * 512:(q + 1) * 512],
                                      in_=s_z[zp][:, :]),
                                  waits=[("dve", zm_val),
                                         (f"stz{zp}", em.cnt[f"stz{zp}"])],
                                  inc=f"stz{zp}", amt=16)
                    if i < NU:
                        sl = i % 3
                        qp, kc = i // 64, i % 64
                        w = [("act", silu_val[i - 3] if i >= 3 else 0)]
                        if i == 0:
                            w += [("act", pd[0]), ("dve", pd[1])]
                        em.op("tensor", lambda e, sl=sl, q=qp, k=kc, p=par:
                              e.matmul(
                                  p_all[:, 512 + sl * 1024:
                                        512 + sl * 1024 + 512],
                                  s_kT[p][0:64, k * 128:(k + 1) * 128],
                                  s_qT[p][0:64, q * 512:(q + 1) * 512],
                                  start=True, stop=True,
                                  tile_position=(0, 0)),
                              waits=w)
                        qk_val[i] = em.op("tensor",
                              lambda e, sl=sl, q=qp, k=kc, p=par: e.matmul(
                                  p_all[:, 512 + sl * 1024 + 512:
                                        512 + sl * 1024 + 1024],
                                  s_kT[p][64:128, k * 128:(k + 1) * 128],
                                  s_qT[p][64:128, q * 512:(q + 1) * 512],
                                  start=True, stop=True,
                                  tile_position=(64, 0)),
                              inc="pe")
                    if i % 2 == 1:
                        pop_bg(1)
                pop_bg(1 << 30)
                attn_read_done[par] = em.cnt["pe"]
        Z_ALL0 = em.cnt["stz0"]
        Z_ALL1 = em.cnt["stz1"]


        if TRUNC >= 6:
            with contextlib.ExitStack() as ts:
                s_w2 = ts.enter_context(nc.sbuf_tensor("w2b", [128, 8192], BF16))
                s_c2 = ts.enter_context(nc.sbuf_tensor("c2", [128, 8], F32))
                s_zc = [ts.enter_context(
                    nc.sbuf_tensor(f"zc{i}", [128, 4096], BF16)) for i in range(2)]
                s_fscr = [ts.enter_context(
                    nc.sbuf_tensor(f"fs{i}", [128, 4096], BF16)) for i in range(2)]
                s_zln = [ts.enter_context(
                    nc.sbuf_tensor(f"zl{i}", [128, 4096], BF16)) for i in range(2)]
                s_xc = ts.enter_context(nc.sbuf_tensor("xc0", [128, 4096], F32))
                s_bscr = ts.enter_context(nc.sbuf_tensor("bsc", [128, 4096], BF16))
                s_y = ts.enter_context(nc.sbuf_tensor("yb", [128, 4096], BF16))
                s_ysq = ts.enter_context(nc.sbuf_tensor("ysq", [128, 4096], BF16))
                s_x1 = ts.enter_context(nc.sbuf_tensor("x1", [128, 4096], F32))
                s_x1b = ts.enter_context(nc.sbuf_tensor("x1b", [128, 4096], BF16))
                s_h = ts.enter_context(nc.sbuf_tensor("hb", [128, 4096], BF16))
                s_mag = ts.enter_context(nc.sbuf_tensor("mag", [128, 4096], BF16))
                s_m2 = ts.enter_context(nc.sbuf_tensor("m2", [128, 4096], BF16))
                s_sqa = ts.enter_context(nc.sbuf_tensor("sqa", [128, 512], F32))
                s_sqb = ts.enter_context(nc.sbuf_tensor("sqb", [128, 512], F32))
                s_mu = [ts.enter_context(
                    nc.sbuf_tensor("mu0", [128, 512], F32))] * 3
                s_vr = [ts.enter_context(
                    nc.sbuf_tensor("vr0", [128, 512], F32))] * 3
                _r3 = [ts.enter_context(
                    nc.sbuf_tensor(f"r{i}", [128, 512], F32)) for i in range(3)]
                _mr3 = [ts.enter_context(
                    nc.sbuf_tensor(f"mr{i}", [128, 512], F32)) for i in range(3)]
                s_r = _r3 + [_r3[1]]
                s_mr = _mr3 + [_mr3[1]]

                lw2 = em.op("gpsimd", lambda e: e.dma_start(
                    out=s_w2[:, :].rearrange("p (d c) -> p d c", d=NDT),
                    in_=d_w2.rearrange("(d p) c -> p d c", p=128)),
                    waits=[("ldc", LDC), ("ldw", em.cnt["ldw"]),
                           ("pe", em.cnt["pe"]), ("act", em.cnt["act"]),
                           ("dve", em.cnt["dve"])],
                    inc="ldw", amt=16)
                c2mm = None
                for f in range(8):
                    for d in range(NDT):
                        c2mm = em.op("tensor", lambda e, f=f, d=d: e.matmul(
                            p_all[:, 3072 + f:3072 + f + 1],
                            s_w2[:, d * 1024 + f * 128:d * 1024 + (f + 1) * 128],
                            s_betabf[:, d:d + 1],
                            start=(d == 0), stop=(d == NDT - 1)),
                            waits=([("ldw", lw2), ("ldc", LDC),
                                    ("act", em.cnt["act"]),
                                    ("dve", em.cnt["dve"])]
                                   if f == 0 and d == 0 else []),
                            inc="pe")
                c2dve = em.op("vector", lambda e: e.tensor_tensor(
                    s_c2[:, :], p_all[:, 3072:3080], s_gb[:, 16:24], op=ALU.add),
                    waits=[("pe", c2mm)], inc="dve")
                gma = None
                for d in range(NDT):
                    gma = em.op("vector", lambda e, d=d: e.tensor_scalar(
                        s_w2[:, d * 1024:(d + 1) * 1024],
                        s_w2[:, d * 1024:(d + 1) * 1024],
                        s_gb[:, d:d + 1], None, op0=ALU.mult),
                        waits=[("pe", c2mm)], inc="dve")
                W2RDY = gma

                zc_free = [(em.cnt["pe"], em.cnt["dve"])] * 2
                fscr_free = [em.cnt["act"], em.cnt["act"]]
                fscr_free = [em.cnt["dve"]] * 2
                zln_free = [em.cnt["pe"]] * 2
                bank_free = [c2dve, c2dve]
                xc_free = em.cnt["dve"]
                x2_done = 0
                stv = 0

                def rstd_chain(mwait, base, cpar, rix, extra=0):
                    em.op("vector", lambda e, b=base: e.tensor_copy(
                        s_mu[cpar][:, :], bank(b)),
                        waits=[("pe", mwait)], inc="dve")
                    em.op("vector", lambda e: e.tensor_tensor(
                        s_vr[cpar][:, :], s_mu[cpar][:, :], s_mu[cpar][:, :],
                        op=ALU.mult), inc="dve")
                    vfree = em.op("vector", lambda e, b=base: e.tensor_tensor(
                        s_vr[cpar][:, :], bank(b + 1), s_vr[cpar][:, :],
                        op=ALU.subtract), inc="dve")
                    em.op("scalar", lambda e: e.activation(
                        out=s_r[rix][:, :], in_=s_vr[cpar][:, :], func=AF.Ln,
                        bias=EPSB()), waits=[("dve", vfree), ("dve", extra)],
                        inc="act")
                    ar = em.op("scalar", lambda e: e.activation(
                        out=s_r[rix][:, :], in_=s_r[rix][:, :], func=AF.Exp,
                        scale=-0.5), inc="act")
                    mrv = em.op("vector", lambda e: e.tensor_tensor(
                        s_mr[rix][:, :], s_mu[cpar][:, :], s_r[rix][:, :],
                        op=ALU.mult), waits=[("act", ar)], inc="dve")
                    return vfree, mrv

                fstate = {}

                def front(n):
                    c0 = n * 512
                    par = n % 2
                    base = par * 2
                    lz = em.op("sync", lambda e, c=c0, p=par: e.dma_start(
                        out=s_zc[p][:, :].rearrange("p (d t) -> p d t", d=NP),
                        in_=d_zscr[:, :, c:c + 512].rearrange("d p t -> p d t")),
                        waits=[("stz0", Z_ALL0), ("stz1", Z_ALL1),
                               ("pe", zc_free[par][0]), ("dve", zc_free[par][1]),
                               (f"ldz{par}", em.cnt[f"ldz{par}"])],
                        inc=f"ldz{par}", amt=16)
                    zq = em.op("scalar", lambda e, p=par: e.activation(
                        out=s_fscr[p][:, :], in_=s_zc[p][:, :], func=AF.Square),
                        waits=[(f"ldz{par}", lz), ("dve", fscr_free[par])],
                        inc="act")
                    for d in range(NDT):
                        em.op("tensor", lambda e, d=d, p=par, b=base: e.matmul(
                            bank(b), s_ones[:, :],
                            s_zc[p][:, d * 512:(d + 1) * 512],
                            start=(d == 0), stop=(d == NDT - 1)),
                            waits=([(f"ldz{par}", lz), ("dve", bank_free[par])]
                                   if d == 0 else []))
                    st2 = None
                    for d in range(NDT):
                        st2 = em.op("tensor", lambda e, d=d, p=par, b=base:
                                    e.matmul(
                                        bank(b + 1), s_ones[:, :],
                                        s_fscr[p][:, d * 512:(d + 1) * 512],
                                        start=(d == 0), stop=(d == NDT - 1)),
                                    waits=([("act", zq)] if d == 0 else []),
                                    inc="pe")
                    bfree, mr2 = rstd_chain(st2, base, 2, 0)
                    bank_free[par] = bfree
                    zl = None
                    for d in range(NDT):
                        em.op("vector", lambda e, d=d, p=par: e.tensor_tensor(
                            s_fscr[p][:, d * 512:(d + 1) * 512],
                            s_zc[p][:, d * 512:(d + 1) * 512], s_r[0][:, :],
                            op=ALU.mult),
                            waits=[("dve", mr2), ("pe", zln_free[par])],
                            inc="dve")
                        zl = em.op("vector", lambda e, d=d, p=par: e.tensor_tensor(
                            s_zln[p][:, d * 512:(d + 1) * 512],
                            s_fscr[p][:, d * 512:(d + 1) * 512], s_mr[0][:, :],
                            op=ALU.subtract), inc="dve")
                    fscr_free[par] = zl
                    fstate[n] = (zl, lz)

                front(0)
                for n in range(TAILN):
                    c0 = n * 512
                    par = n % 2
                    base = par * 2
                    zl, lz = fstate[n]
                    ysq_l = None
                    ypar_dve = [0, 0]
                    for f in range(8):
                        mm = None
                        for d in range(NDT):
                            w = []
                            if d == 0:
                                w = [("dve", zl), ("dve", W2RDY),
                                     ("dve", bank_free[par]),
                                     ("dve", ypar_dve[f % 2])]
                            mm = em.op("tensor", lambda e, f=f, d=d, p=par,
                                       b=base: e.matmul(
                                           bank(b + f % 2),
                                           s_w2[:, d * 1024 + f * 128:
                                                d * 1024 + (f + 1) * 128],
                                           s_zln[p][:, d * 512:(d + 1) * 512],
                                           start=(d == 0), stop=(d == NDT - 1)),
                                       waits=w, inc="pe")
                        if f == 7:
                            zc_free[par] = (mm, zl)
                            zln_free[par] = mm
                        yev = em.op("vector", lambda e, f=f, b=base:
                                    e.tensor_tensor(
                                        s_y[:, f * 512:(f + 1) * 512],
                                        bank(b + f % 2),
                                        s_c2[:, f:f + 1].broadcast_to([128, 512]),
                                        op=ALU.add),
                                    waits=[("pe", mm), ("dve", c2dve)], inc="dve")
                        ysq_l = em.op("scalar", lambda e, f=f: e.activation(
                            out=s_ysq[:, f * 512:(f + 1) * 512],
                            in_=s_y[:, f * 512:(f + 1) * 512],
                            func=AF.Square),
                            waits=[("dve", yev)], inc="act")
                        ypar_dve[f % 2] = yev
                    for f in range(8):
                        em.op("tensor", lambda e, f=f, b=base: e.matmul(
                            bank(b), s_ones[:, :], s_y[:, f * 512:(f + 1) * 512],
                            start=(f == 0), stop=(f == 7)),
                            waits=([("dve", yev), ("dve", ypar_dve[0]),
                                    ("dve", ypar_dve[1])] if f == 0 else []))
                    sty = None
                    for f in range(8):
                        sty = em.op("tensor", lambda e, f=f, b=base: e.matmul(
                            bank(b + 1), s_ones[:, :],
                            s_ysq[:, f * 512:(f + 1) * 512],
                            start=(f == 0), stop=(f == 7)),
                            waits=([("act", ysq_l)] if f == 0 else []), inc="pe")
                    bfree, mry = rstd_chain(sty, base, 1, 1, extra=x2_done)
                    lx = em.op("sync", lambda e, c=c0: e.dma_start(
                        out=s_xc[:, :].rearrange("p (d t) -> p d t", d=NDT),
                        in_=d_xT[:, c:c + 512]
                        .rearrange("(d p) t -> p d t", p=128)),
                        waits=[("dve", xc_free), ("ldxc", em.cnt["ldxc"]),
                               ("stout", em.cnt["stout"])],
                        inc="ldxc", amt=16)
                    x1b_l = None
                    x1sq_l = None
                    for f in range(8):
                        em.op("vector", lambda e, f=f: e.tensor_tensor(
                            s_bscr[:, f * 512:(f + 1) * 512],
                            s_y[:, f * 512:(f + 1) * 512], s_r[1][:, :],
                            op=ALU.mult),
                            waits=[("dve", mry), ("ldxc", lx)], inc="dve")
                        em.op("vector", lambda e, f=f: e.tensor_tensor(
                            s_bscr[:, f * 512:(f + 1) * 512],
                            s_bscr[:, f * 512:(f + 1) * 512], s_mr[1][:, :],
                            op=ALU.subtract), inc="dve")
                        x1v = em.op("vector", lambda e, f=f: e.tensor_tensor(
                            s_x1[:, f * 512:(f + 1) * 512],
                            s_bscr[:, f * 512:(f + 1) * 512],
                            s_xc[:, f * 512:(f + 1) * 512],
                            op=ALU.add), inc="dve")
                        x1b_l = em.op("scalar", lambda e, f=f: e.activation(
                            out=s_x1b[:, f * 512:(f + 1) * 512],
                            in_=s_x1[:, f * 512:(f + 1) * 512], func=AF.Copy),
                            waits=[("dve", x1v)], inc="act")
                        x1sq_l = em.op("scalar", lambda e, f=f: e.activation(
                            out=s_ysq[:, f * 512:(f + 1) * 512],
                            in_=s_x1[:, f * 512:(f + 1) * 512], func=AF.Square),
                            inc="act")
                    xc_free = em.cnt["dve"]
                    for f in range(8):
                        em.op("tensor", lambda e, f=f, b=base: e.matmul(
                            bank(b), s_ones[:, :],
                            s_x1b[:, f * 512:(f + 1) * 512],
                            start=(f == 0), stop=(f == 7)),
                            waits=([("act", x1b_l), ("dve", bfree)]
                                   if f == 0 else []))
                    st3 = None
                    for f in range(8):
                        st3 = em.op("tensor", lambda e, f=f, b=base: e.matmul(
                            bank(b + 1), s_ones[:, :],
                            s_ysq[:, f * 512:(f + 1) * 512],
                            start=(f == 0), stop=(f == 7)),
                            waits=([("act", x1sq_l)] if f == 0 else []), inc="pe")
                    bfree, mr3 = rstd_chain(st3, base, 0, 2)
                    hl = None
                    for f in range(8):
                        em.op("vector", lambda e, f=f: e.tensor_tensor(
                            s_bscr[:, f * 512:(f + 1) * 512],
                            s_x1b[:, f * 512:(f + 1) * 512], s_r[2][:, :],
                            op=ALU.mult), waits=[("dve", mr3)], inc="dve")
                        hl = em.op("vector", lambda e, f=f: e.tensor_tensor(
                            s_h[:, f * 512:(f + 1) * 512],
                            s_bscr[:, f * 512:(f + 1) * 512], s_mr[2][:, :],
                            op=ALU.subtract), inc="dve")
                    bank_free[par] = bfree
                    if n + 1 < TAILN:
                        front(n + 1)
                    ri_free = [0, 0, 0, 0]
                    m2_l = 0
                    mag_l = None
                    for fo in range(8):
                        rb_ = 4 + (fo % 2) * 2
                        ib_ = 5 + (fo % 2) * 2
                        mmr = None
                        for d in range(NDT):
                            w = []
                            if d == 0:
                                w = [("dve", hl), ("act", ri_free[2 * (fo % 2)])]
                            mmr = em.op("tensor", lambda e, d=d, f=fo, b=rb_:
                                        e.matmul(
                                            bank(b),
                                            s_cos[:, d * 1024 + f * 128:
                                                  d * 1024 + (f + 1) * 128],
                                            s_h[:, d * 512:(d + 1) * 512],
                                            start=(d == 0), stop=(d == NDT - 1)),
                                        waits=w, inc="pe")
                        mmi = None
                        for d in range(NDT):
                            w = []
                            if d == 0:
                                w = [("act", ri_free[2 * (fo % 2) + 1])]
                            mmi = em.op("tensor", lambda e, d=d, f=fo, b=ib_:
                                        e.matmul(
                                            bank(b),
                                            s_sin[:, d * 1024 + f * 128:
                                                  d * 1024 + (f + 1) * 128],
                                            s_h[:, d * 512:(d + 1) * 512],
                                            start=(d == 0), stop=(d == NDT - 1)),
                                        waits=w, inc="pe")
                        sa = em.op("scalar", lambda e, b=rb_: e.activation(
                            out=s_sqa[:, :], in_=bank(b), func=AF.Square),
                            waits=[("pe", mmr), ("dve", m2_l)], inc="act")
                        ri_free[2 * (fo % 2)] = sa
                        sb_ = em.op("scalar", lambda e, b=ib_: e.activation(
                            out=s_sqb[:, :], in_=bank(b), func=AF.Square),
                            waits=[("pe", mmi)], inc="act")
                        ri_free[2 * (fo % 2) + 1] = sb_
                        m2_l = em.op("vector", lambda e, f=fo: e.tensor_tensor(
                            s_m2[:, f * 512:(f + 1) * 512], s_sqa[:, :],
                            s_sqb[:, :], op=ALU.add),
                            waits=[("act", sb_)], inc="dve")
                        em.op("scalar", lambda e, f=fo: e.activation(
                            out=s_mag[:, f * 512:(f + 1) * 512],
                            in_=s_m2[:, f * 512:(f + 1) * 512], func=AF.Ln,
                            bias=TINY()), waits=[("dve", m2_l)], inc="act")
                        mag_l = em.op("scalar", lambda e, f=fo: e.activation(
                            out=s_mag[:, f * 512:(f + 1) * 512],
                            in_=s_mag[:, f * 512:(f + 1) * 512], func=AF.Exp,
                            scale=0.5), inc="act")
                    for f in range(8):
                        em.op("tensor", lambda e, f=f, b=base: e.matmul(
                            bank(b), s_ones[:, :],
                            s_mag[:, f * 512:(f + 1) * 512],
                            start=(f == 0), stop=(f == 7)),
                            waits=([("act", mag_l), ("dve", bank_free[par])]
                                   if f == 0 else []))
                    st4 = None
                    for f in range(8):
                        st4 = em.op("tensor", lambda e, f=f, b=base: e.matmul(
                            bank(b + 1), s_ones[:, :],
                            s_m2[:, f * 512:(f + 1) * 512],
                            start=(f == 0), stop=(f == 7)),
                            waits=([("dve", m2_l)] if f == 0 else []), inc="pe")
                    bfree, mr4 = rstd_chain(st4, base, 1, 3)
                    bank_free[par] = bfree
                    prev_st = stv
                    xl = None
                    for f in range(8):
                        em.op("vector", lambda e, f=f: e.tensor_tensor(
                            s_bscr[:, f * 512:(f + 1) * 512],
                            s_mag[:, f * 512:(f + 1) * 512], s_r[3][:, :],
                            op=ALU.mult),
                            waits=[("dve", mr4), ("stout", prev_st)], inc="dve")
                        em.op("vector", lambda e, f=f: e.tensor_tensor(
                            s_bscr[:, f * 512:(f + 1) * 512],
                            s_bscr[:, f * 512:(f + 1) * 512], s_mr[3][:, :],
                            op=ALU.subtract), inc="dve")
                        xl = em.op("vector", lambda e, f=f: e.tensor_tensor(
                            s_xc[:, f * 512:(f + 1) * 512],
                            s_bscr[:, f * 512:(f + 1) * 512],
                            s_x1[:, f * 512:(f + 1) * 512], op=ALU.add),
                            inc="dve")
                    x2_done = xl
                    stv = em.op("gpsimd", lambda e, c=c0: e.dma_start(
                        out=d_out[:, c:c + 512].rearrange("(f p) t -> p f t",
                                                          p=128),
                        in_=s_xc[:, :].rearrange("p (f t) -> p f t", f=NDT)),
                        waits=[("dve", xl), ("stout", em.cnt["stout"])],
                        inc="stout", amt=16)
                em.op("gpsimd", lambda e: e.nop(),
                      waits=[("stout", em.cnt["stout"])])
                if int(_os.environ.get("KEXTRA2", "0")):
                    em.op("tensor", lambda e: e.matmul(
                        bank(2), s_cos[:, 0:128], s_cos[:, 0:512],
                        start=True, stop=True), inc="pe")
                if TAILN < 8:
                    waits = [(s_, em.cnt[s_]) for s_ in em.cnt]
                    _mk = _os.environ.get("KMARK", "cos")
                    msrc = {"cos": s_cos[:, 0:4096], "zln": s_zln[0][:, :],
                            "zc": s_zc[0][:, :], "w2": s_w2[:, 0:4096],
                            "r0": s_r[0][:, :], "mr0": s_mr[0][:, :]}[_mk]
                    em.op("gpsimd", lambda e, m=msrc: e.dma_start(
                        out=d_out[0:128, 0:m.shape[-1]], in_=m),
                        waits=waits, inc="stout", amt=16)
                    em.op("gpsimd", lambda e: e.nop(),
                          waits=[("stout", em.cnt["stout"])])
        else:
            waits = [(s_, em.cnt[s_]) for s_ in em.cnt]
            em.op("gpsimd", lambda e: e.dma_start(
                out=d_out[0:128, :], in_=s_cos[:, 0:4096]),
                waits=waits, inc="stout", amt=16)
            em.op("gpsimd", lambda e: e.nop(),
                  waits=[("stout", em.cnt["stout"])])

        for name in em.cnt:
            sems[name] = es.enter_context(nc.semaphore(name))
        with nc.Block() as block:
            def mk(eng_name):
                def body(eng):
                    for waits, fn, inc, amt in em.ops[eng_name]:
                        for sname, val in waits:
                            eng.wait_ge(sems[sname], val)
                        r = fn(eng)
                        if inc is not None:
                            assert r is not None, f"{eng_name} inc={inc}"
                            r.then_inc(sems[inc], amt)
                return body
            block.sync(mk("sync"))
            block.gpsimd(mk("gpsimd"))
            block.tensor(mk("tensor"))
            block.scalar(mk("scalar"))
            block.vector(mk("vector"))

    return nc


def kernel(L0_attn_x, L0_attn_W1, L0_attn_b1, L0_attn_W2, L0_attn_b2,
           L0_attn_gamma, L0_attn_beta, L0_ffn_theta, L0_ffn_phi,
           L0_ffn_theta_noise, L0_ffn_phi_noise):
    x = np.asarray(L0_attn_x, np.float32)
    W1 = np.asarray(L0_attn_W1, np.float32)
    b1 = np.asarray(L0_attn_b1, np.float32)
    W2 = np.ascontiguousarray(np.asarray(L0_attn_W2, np.float32))
    b2 = np.asarray(L0_attn_b2, np.float32)
    gamma = np.asarray(L0_attn_gamma, np.float32)
    beta = np.asarray(L0_attn_beta, np.float32)
    th = np.ascontiguousarray(np.asarray(L0_ffn_theta, np.float32))
    thn = np.ascontiguousarray(np.asarray(L0_ffn_theta_noise, np.float32))
    ph = np.ascontiguousarray(np.asarray(L0_ffn_phi, np.float32))
    phn = np.ascontiguousarray(np.asarray(L0_ffn_phi_noise, np.float32))

    xT = np.ascontiguousarray(x.transpose(0, 2, 1))

    w1p = np.empty((NP, D, 512), np.float32)
    for p in range(NP):
        for k in range(4):
            w1p[p, :, k * 128:(k + 1) * 128] = \
                W1[:, k * D + p * 128:k * D + (p + 1) * 128]
    b1c = np.ascontiguousarray(
        b1.reshape(4, 8, 128).transpose(2, 0, 1).reshape(128, 32))
    gb = np.ascontiguousarray(np.concatenate(
        [gamma.reshape(8, 128).T, beta.reshape(8, 128).T,
         b2.reshape(8, 128).T], axis=1))
    consts = np.concatenate(
        [np.eye(128, dtype=np.float32) / S,
         np.ones((128, 128), np.float32) / D], axis=1).astype(BF)
    cb = np.tile(np.array([[math.pi / 2, EPS, 1e-30, 1024.0,
                            1.0, 0.0, 0.0, 0.0]], np.float32), (128, 1))

    nc = build_nc()
    in_maps = []
    for c in range(8):
        b, half = c // 2, c % 2
        off = half * SH
        xrot = np.ascontiguousarray(
            np.concatenate([xT[b][:, off:], xT[b][:, :off]], axis=1))
        in_maps.append(dict(
            xT=xrot, w1p=w1p, b1c=b1c, gb=gb, consts=consts, cb=cb,
            w2=W2, th=th, thn=thn, ph=ph, phn=phn))
    res = run_bass_kernel_spmd(nc, in_maps, core_ids=list(range(8)))
    out = np.empty_like(x)
    for c in range(8):
        b, half = c // 2, c % 2
        out[b, half * SH:(half + 1) * SH, :] = res.results[c]["out"].T
    return out


if __name__ == "__main__":
    import reference as _r
    inp = _r.setup_inputs(0)
    want = np.asarray(_r.reference(**inp))
    got = kernel(**{k: np.asarray(v) for k, v in inp.items()})
    err = got - want
    rvr = float((err ** 2).mean() / (want ** 2).mean())
    print("rvr", rvr, "max", float(np.abs(err).max()))
